# version 08
# speedup vs baseline: 1.4015x; 1.0078x over previous
import math
import contextlib
import numpy as np
import ml_dtypes
import concourse.bass as bass
import concourse.mybir as mybir
from concourse.bass_utils import run_bass_kernel_spmd

F32 = mybir.dt.float32
BF16 = mybir.dt.bfloat16
I32 = mybir.dt.int32
AF = mybir.ActivationFunctionType
ALU = mybir.AluOpType
AXX = mybir.AxisListType.X

T = 2048
D = 1024
NCH = 8
NTILE = 16
WIN = 512
HID = 2730
NHP = 22
LAMBDA_INIT = 0.2
C_LOC = 8.0 * math.exp((127.0 / 256.0) * (-math.log(25.6 / 89.6)))

ENGINES = ("sync", "gpsimd", "tensor", "vector", "scalar")
SEM_OF = {"vector": "V", "scalar": "S", "gpsimd": "G", "tensor": "T"}


class Prog:
    def __init__(self, nc, es):
        self.nc = nc
        self.stream = {e: [] for e in ENGINES}
        self.sems = {}
        self.count = {}
        self.last_wait = {e: {} for e in ENGINES}
        self._es = es

    def sem(self, name):
        if name not in self.sems:
            self.sems[name] = self._es.enter_context(self.nc.semaphore(name))
            self.count[name] = 0
        return name

    def wait(self, eng, tok):
        if tok is None:
            return
        s, v = tok
        if v <= 0 or self.last_wait[eng].get(s, 0) >= v:
            return
        self.last_wait[eng][s] = v
        self.stream[eng].append(("__wait__", (self.sems[s], v), {}, []))

    ASYNC_OPS = {"memset", "tensor_reduce", "reduce_max", "reduce_sum",
                 "reciprocal", "iota", "affine_select", "memzero"}
    ASYNC_OPS = ASYNC_OPS | {"tensor_tensor", "tensor_scalar", "tensor_copy",
                             "scalar_tensor_tensor", "tensor_add",
                             "tensor_sub", "tensor_mul", "tensor_max",
                             "activation"}

    def op(self, eng, fname, *args, inc=(), **kwargs):
        if eng == "gpsimd" and fname not in ("dma_start", "__wait__"):
            self.wait("gpsimd", self.tok("G"))
            if not any(s == "G" for s, _ in inc):
                inc = list(inc) + [("G", 1)]
        selfwait = (eng in ("vector", "scalar") and fname in self.ASYNC_OPS)
        if selfwait and not any(s == SEM_OF[eng] for s, _ in inc):
            inc = list(inc) + [(SEM_OF[eng], 1)]
        toks = []
        incs = []
        for s, amt in inc:
            self.count[s] += amt
            toks.append((s, self.count[s]))
            incs.append((self.sems[s], amt))
        self.stream[eng].append((fname, args, kwargs, incs))
        if selfwait:
            self.wait(eng, (SEM_OF[eng], self.count[SEM_OF[eng]]))
        return toks[0] if toks else None

    def opt(self, eng, fname, *args, **kwargs):
        return self.op(eng, fname, *args, inc=[(SEM_OF[eng], 1)], **kwargs)

    def tok(self, semname):
        return (semname, self.count[semname])

    def inc_last(self, eng, semname, amt=1):
        fname, args, kwargs, incs = self.stream[eng][-1]
        assert fname != "__wait__", "inc_last on a wait"
        self.count[semname] += amt
        incs.append((self.sems[semname], amt))
        return (semname, self.count[semname])

    def barrier_all(self):
        for s in ("T", "V", "S", "G"):
            self.wait("sync", self.tok(s))

    def emit(self, block):
        for eng_name in ENGINES:
            ops = self.stream[eng_name]

            def body(eng, ops=ops):
                for fname, args, kwargs, incs in ops:
                    if fname == "__wait__":
                        eng.wait_ge(*args)
                        continue
                    inst = getattr(eng, fname)(*args, **kwargs)
                    for sh, amt in incs:
                        inst.then_inc(sh, amt)

            getattr(block, eng_name)(body)


class Arena:

    def __init__(self, tensor, cols):
        self.t = tensor
        self.cols = cols
        self.regions = {}
        self.cursor = 0
        self.high = 0

    def alloc(self, name, cols, dtype=BF16, align=32):
        assert name not in self.regions, name
        start = (self.cursor + align - 1) // align * align
        assert start + cols <= self.cols, \
            f"arena overflow: {name} {start}+{cols}>{self.cols}"
        self.cursor = start + cols
        self.high = max(self.high, self.cursor)
        self.regions[name] = (start, cols, dtype)
        return self.view(name)

    def view(self, name):
        start, nb, dtype = self.regions[name]
        ap = self.t.ap()
        if dtype == F32:
            return ap.bitcast(F32)[:, start // 2: (start + nb) // 2]
        return ap[:, start: start + nb]

    def free(self, *names):
        for name in names:
            self.regions.pop(name, None)
        self.cursor = max((s + n for s, n, _ in self.regions.values()), default=0)


def _chunked(dram_ap, p=128):
    return dram_ap.rearrange("(c q) o -> c q o", q=p).transpose([1, 0, 2])


def build_program(upto="all", dumps_req=()):
    nc = bass.Bass("TRN2", target_bir_lowering=False)
    es = contextlib.ExitStack()

    dx = nc.dram_tensor("x", [NCH, 128, T], F32, kind="ExternalInput").ap()
    wshapes = {
        "W_attn": [D, 3 * D], "W_proj": [D, D], "lam": [1, 130],
        "wqkv0": [D, 1536], "wqkv1": [D, 1536],
        "wo0": [512, D], "wo1": [512, D],
        "w1_0": [D, 2 * HID], "w1_1": [D, 2 * HID],
        "w2_0": [HID, D], "w2_1": [HID, D],
        "qs": [1, 128], "ks": [1, 128],
        "lng": [128, 16], "lnb": [128, 16], "b1": [128, 88], "b2": [128, 16],
    }
    dw = {nm: nc.dram_tensor(nm, shp, F32, kind="ExternalInput").ap()
          for nm, shp in wshapes.items()}
    dout = nc.dram_tensor("out", [T, D], F32, kind="ExternalOutput").ap()
    dwb = {nm: nc.dram_tensor("b_" + nm, wshapes[nm], BF16, kind="Internal").ap()
           for nm in ("wqkv0", "wqkv1", "W_attn", "W_proj", "wo0", "wo1",
                      "w1_0", "w1_1", "w2_0", "w2_1")}
    dspill = nc.dram_tensor("xspill", [128, NCH * T], F32, kind="Internal").ap()
    drow = nc.dram_tensor("rowsc", [1, 64], F32, kind="Internal").ap()

    with es:
        p = Prog(nc, es)
        XY = es.enter_context(nc.sbuf_tensor("XY", [128, NCH * T], F32))
        AR = es.enter_context(nc.sbuf_tensor("AR", [128, 67000], BF16))
        SM = es.enter_context(nc.sbuf_tensor("SM", [128, 2900], F32))
        ar = Arena(AR, 67000)
        psum = [es.enter_context(nc.psum_tensor(f"ps{i}", [128, 512], F32))
                for i in range(8)]
        psf = [x.ap() for x in psum]
        psb = [x.ap().bitcast(BF16) for x in psum]

        for s in ("dS", "dG", "dC", "dX", "dGX", "T", "V", "S", "G", "dO",
                  "dW0", "dW1", "dW2"):
            p.sem(s)
        for i in range(10):
            p.sem(f"cv{i}")

        xy = XY.ap()
        sm = SM.ap()

        def xyc(c, t0=0, n=T):
            return xy[:, c * T + t0: c * T + t0 + n]

        ones_col = sm[:, 0:1]
        lng_a = sm[:, 4:20]
        lnb_a = sm[:, 20:36]
        b2_a = sm[:, 36:52]
        b1_a = sm[:, 52:140]
        cloc_a = sm[:, 140:142]
        lamcol = sm[:, 142:143]
        eps_col = sm[:, 143:144]
        eps_p0 = sm[0:1, 144:145]

        def lng_c(l, c):
            return lng_a[:, l * 8 + c: l * 8 + c + 1]

        def lnb_c(l, c):
            return lnb_a[:, l * 8 + c: l * 8 + c + 1]

        def b2_c(l, c):
            return b2_a[:, l * 8 + c: l * 8 + c + 1]

        def b1a_c(l, j):
            return b1_a[:, l * 44 + j: l * 44 + j + 1]

        def b1g_c(l, j):
            return b1_a[:, l * 44 + 22 + j: l * 44 + 22 + j + 1]

        def cloc_c(l):
            return cloc_a[:, l:l + 1]

        def row(i, n=512):
            assert i <= 2 and 1300 + i * 512 + n <= 2900
            return sm[0:1, 1300 + i * 512: 1300 + i * 512 + n]

        def r1(i, n=1):
            return sm[0:1, 990 + i: 990 + i + n]

        ones_row = sm[0:1, 150:278]
        qsrow = sm[0:1, 600:728]
        ksrow = sm[0:1, 728:856]
        lamrow = sm[0:1, 856:986]

        t_xin = p.op("sync", "dma_start",
                     out=xy.rearrange("p (c t) -> p c t", t=T),
                     in_=dx.transpose([1, 0, 2]), inc=[("dS", 16)])

        p.op("gpsimd", "memset", ones_col, 1.0)
        p.op("gpsimd", "memset", sm[0:1, 150:278], 1.0)
        p.op("gpsimd", "memset", eps_col, 1e-5)
        p.op("gpsimd", "memset", eps_p0, 1e-5)
        idn = ar.alloc("ident", 128, BF16)
        p.op("gpsimd", "memset", idn, 0.0)
        p.op("gpsimd", "affine_select", out=idn, in_=idn, compare_op=ALU.not_equal,
             fill=1.0, base=0, pattern=[[-1, 128]], channel_multiplier=1)
        idnf = ar.alloc("identf", 256, F32)
        p.op("gpsimd", "memset", idnf, 0.0)
        p.op("gpsimd", "affine_select", out=idnf, in_=idnf, compare_op=ALU.not_equal,
             fill=1.0, base=0, pattern=[[-1, 128]], channel_multiplier=1)
        onesln_bf = ar.alloc("onesln", 32, BF16)[:, 0:1]
        p.op("gpsimd", "memset", onesln_bf, 1.0 / 1024.0)
        mlo = ar.alloc("mask_lo", 128, BF16)
        p.op("gpsimd", "memset", mlo, 1.0)
        p.op("gpsimd", "affine_select", out=mlo, in_=mlo, compare_op=ALU.is_ge,
             fill=0.0, base=0, pattern=[[-1, 128]], channel_multiplier=1)
        mup = ar.alloc("mask_up", 128, BF16)
        p.op("gpsimd", "memset", mup, 1.0)
        p.op("gpsimd", "affine_select", out=mup, in_=mup, compare_op=ALU.is_ge,
             fill=0.0, base=0, pattern=[[1, 128]], channel_multiplier=-1)
        statq16 = ar.alloc("statq16", 64, F32)
        cstats = ar.alloc("cstats", 16, F32)
        cbig = ar.alloc("cbig", 64, F32)
        tabsP = ar.alloc("tabsP", 3072, BF16)
        btabP = ar.alloc("btabP", 1024, BF16)
        t_const = p.op("gpsimd", "engine_nop", inc=[("G", 1)])

        t_small = None
        for nm, dst in (("lng", lng_a), ("lnb", lnb_a), ("b2", b2_a),
                        ("b1", b1_a)):
            t_small = p.op("sync", "dma_start", out=dst, in_=dw[nm],
                           inc=[("dS", 16)])
        t_small = p.op("sync", "dma_start", out=qsrow, in_=dw["qs"], inc=[("dS", 16)])
        t_small = p.op("sync", "dma_start", out=ksrow, in_=dw["ks"], inc=[("dS", 16)])
        t_small = p.op("sync", "dma_start", out=lamrow, in_=dw["lam"], inc=[("dS", 16)])

        import os as _os
        conv_order = ["wqkv0", "wo0", "w1_0", "w2_0", "wqkv1", "wo1", "w1_1",
                      "w2_1", "W_attn", "W_proj"]
        t_conv = {}
        for ci, nm in enumerate(conv_order):
            t_conv[nm] = p.op("gpsimd", "dma_start", out=dwb[nm], in_=dw[nm],
                              inc=[(f"cv{ci}", 16)])

        p.wait("vector", t_small)
        for l in range(2):
            p.op("vector", "tensor_reduce", r1(0 + 4 * l).unsqueeze(2),
                 qsrow[:, 64 * l:64 * l + 64].rearrange("p (a b) -> p a b", a=1),
                 AXX, ALU.max, apply_absolute_value=True)
            p.op("vector", "tensor_reduce", r1(1 + 4 * l).unsqueeze(2),
                 ksrow[:, 64 * l:64 * l + 64].rearrange("p (a b) -> p a b", a=1),
                 AXX, ALU.max, apply_absolute_value=True)
            p.op("vector", "tensor_tensor", r1(2 + 4 * l), r1(0 + 4 * l),
                 r1(1 + 4 * l), ALU.mult)
            p.opt("vector", "tensor_scalar", r1(3 + 4 * l), r1(2 + 4 * l),
                  -C_LOC, None, ALU.mult)
        p.op("vector", "tensor_tensor", r1(8, 32), lamrow[:, 0:32],
             lamrow[:, 32:64], ALU.mult)
        p.op("vector", "tensor_tensor", r1(40, 32), lamrow[:, 64:96],
             lamrow[:, 96:128], ALU.mult)
        p.op("vector", "tensor_reduce", r1(72).unsqueeze(2),
             r1(8, 32).rearrange("p (a b) -> p a b", a=1), AXX, ALU.add)
        tok = p.opt("vector", "tensor_reduce", r1(73).unsqueeze(2),
                    r1(40, 32).rearrange("p (a b) -> p a b", a=1), AXX, ALU.add)
        p.wait("scalar", tok)
        p.op("scalar", "activation", r1(74), r1(72), AF.Exp)
        tok = p.opt("scalar", "activation", r1(75), r1(73), AF.Exp)
        p.wait("vector", tok)
        p.op("vector", "tensor_sub", r1(76), r1(74), r1(75))
        t_lamv = p.opt("vector", "tensor_scalar", r1(77), r1(76), LAMBDA_INIT,
                       None, ALU.add)
        p.wait("tensor", t_lamv)
        p.wait("tensor", t_const)
        p.op("tensor", "matmul", psf[7][:, 0:1], ones_row, r1(77), start=True,
             stop=True)
        p.op("tensor", "matmul", psf[7][:, 1:2], ones_row, r1(3), start=True,
             stop=True)
        tk = p.op("tensor", "matmul", psf[7][:, 2:3], ones_row, r1(7),
                  start=True, stop=True, inc=[("T", 1)])
        p.wait("vector", tk)
        p.op("vector", "tensor_copy", lamcol, psf[7][:, 0:1])
        p.op("vector", "tensor_copy", cloc_c(0), psf[7][:, 1:2])
        t_cloc = p.opt("vector", "tensor_copy", cloc_c(1), psf[7][:, 2:3])

        def tab(tb, i):
            return tabsP[:, tb * 512 + 32 * i: tb * 512 + 32 * i + 32]

        def btab_t(tb, i):
            return btabP[:, tb * 512 + 32 * i: tb * 512 + 32 * i + 32]

        tbsc = ar.alloc("tbsc", 10240, F32)
        tv = tbsc
        i32v = tv.bitcast(I32)
        jint = i32v[:, 0:32]
        jint16 = i32v[:, 112:128]
        Ti = i32v[:, 96:112]
        framp = tv[:, 0:32]
        theta2p = tv[:, 32:64]
        lnbase = tv[:, 64:96]
        Tf = tv[:, 160:176]
        framp16 = tv[:, 112:128]
        theta16 = tv[:, 128:144]
        pw = tv[:, 176:192]
        wA = tv[:, 512:1024]
        wB = tv[:, 1024:1536]
        wC = tv[:, 1536:2048]
        wD = tv[:, 2048:2560]
        wS = tv[:, 2560:3072]
        wI = tv[:, 3072:3584]
        wint = i32v[:, 3584:4096]
        wG = tv[:, 4096:4608]

        def bc3(row, nj):
            return row.unsqueeze(1).broadcast_to([128, 16, nj])

        def tile3(w, nj=32):
            return w[:, 0:16 * nj].rearrange("p (i j) -> p i j", j=nj)

        p.wait("gpsimd", t_const)
        p.op("gpsimd", "iota", jint, [[1, 32]], base=0, channel_multiplier=0)
        p.op("gpsimd", "iota", jint16, [[1, 16]], base=0, channel_multiplier=0)
        t_io = p.op("gpsimd", "iota", Ti, [[128, 16]], base=0,
                    channel_multiplier=1)
        p.wait("vector", t_io)
        p.op("vector", "tensor_copy", framp, jint)
        p.op("vector", "tensor_copy", framp16, jint16)
        tok = p.opt("vector", "tensor_copy", Tf, Ti)
        p.wait("scalar", tok)
        p.op("scalar", "activation", theta2p, framp, AF.Exp,
             scale=-math.log(1e4) / 32.0)
        p.op("scalar", "activation", theta2p, theta2p, AF.Identity,
             scale=1.0 / (2 * math.pi))
        t_th16 = p.opt("scalar", "activation", theta16, framp16, AF.Exp,
                       scale=-math.log(1e4) / 16.0)
        v = p.opt("vector", "tensor_scalar", lnbase, framp, 2.0 / 89.6,
                  25.6 / 89.6, ALU.mult, ALU.add)
        p.wait("vector", t_th16)
        p.op("vector", "tensor_scalar", theta16, theta16,
             1.0 / (2 * math.pi), None, ALU.mult)
        p.wait("scalar", v)
        t_lb = p.opt("scalar", "activation", lnbase, lnbase, AF.Ln)
        p.wait("vector", t_lb)

        last_sin = [None]

        def sincos(theta_row, nj, sin_out, cos_out):
            n = 16 * nj
            p.op("vector", "tensor_tensor", tile3(wA, nj), bc3(theta_row, nj),
                 Tf.unsqueeze(2).broadcast_to([128, 16, nj]), ALU.mult)
            for (dst, shift) in ((sin_out, None), (cos_out, 0.25)):
                src_ = wA[:, 0:n]
                if shift is not None:
                    p.op("vector", "tensor_scalar", wB[:, 0:n], src_, shift,
                         None, ALU.add)
                    src_ = wB[:, 0:n]
                p.wait("vector", last_sin[0])
                p.op("vector", "tensor_copy", wint[:, 0:n], src_)
                p.op("vector", "tensor_copy", wC[:, 0:n], wint[:, 0:n])
                p.op("vector", "tensor_sub", wC[:, 0:n], src_, wC[:, 0:n])
                p.op("vector", "tensor_scalar", wG[:, 0:n], wC[:, 0:n], 0.5,
                     None, ALU.is_gt)
                vv = p.opt("vector", "tensor_sub", wC[:, 0:n], wC[:, 0:n],
                           wG[:, 0:n])
                p.wait("scalar", vv)
                sc = p.opt("scalar", "activation", dst, wC[:, 0:n], AF.Sin,
                           scale=2 * math.pi)
                last_sin[0] = sc
            return sc

        sc1 = sincos(theta2p, 32, wD[:, 0:512], wS[:, 0:512])
        p.op("vector", "tensor_scalar", pw, Tf, 1.0 / 256.0, -4.0, ALU.mult,
             ALU.add)
        v = p.opt("vector", "tensor_tensor", tile3(wI), bc3(lnbase, 32),
                  pw.unsqueeze(2).broadcast_to([128, 16, 32]), ALU.mult)
        p.wait("scalar", v)
        p.wait("scalar", sc1)
        s_S = p.opt("scalar", "activation", wA[:, 0:512], wI[:, 0:512], AF.Exp)
        s_I = p.opt("scalar", "activation", wB[:, 0:512], wI[:, 0:512], AF.Exp,
                    scale=-1.0)
        p.wait("vector", s_S)
        p.wait("vector", s_I)
        p.op("vector", "tensor_mul", tabsP[:, 0:512], wS[:, 0:512],
             wA[:, 0:512])
        p.op("vector", "tensor_mul", tabsP[:, 1024:1536], wD[:, 0:512],
             wA[:, 0:512])
        p.op("vector", "scalar_tensor_tensor", tabsP[:, 512:1024],
             wD[:, 0:512], -1.0, wA[:, 0:512], ALU.mult, ALU.mult)
        p.op("vector", "tensor_mul", tabsP[:, 1536:2048], wS[:, 0:512],
             wB[:, 0:512])
        p.op("vector", "tensor_mul", tabsP[:, 2560:3072], wD[:, 0:512],
             wB[:, 0:512])
        t_tabs_done = p.opt("vector", "scalar_tensor_tensor",
                            tabsP[:, 2048:2560], wD[:, 0:512], -1.0,
                            wB[:, 0:512], ALU.mult, ALU.mult)
        sc2 = sincos(theta16, 16, wD[:, 0:256], wS[:, 0:256])
        p.wait("vector", sc2)
        cse = tabsP.bitcast(BF16) if False else None
        bt0 = btabP[:, 0:512].rearrange("p (i j v) -> p i j v", j=16, v=2)
        bt1 = btabP[:, 512:1024].rearrange("p (i j v) -> p i j v", j=16, v=2)
        w3c = wS[:, 0:256].rearrange("p (i j) -> p i j", j=16).unsqueeze(3)
        w3s = wD[:, 0:256].rearrange("p (i j) -> p i j", j=16).unsqueeze(3)
        p.op("vector", "tensor_copy", bt0[:, :, :, 0:1], w3c)
        p.op("vector", "tensor_copy", bt0[:, :, :, 1:2], w3c)
        p.op("vector", "scalar_tensor_tensor", bt1[:, :, :, 0:1], w3s, -1.0,
             w3s, ALU.mult, ALU.bypass)
        t_tabs_done = p.opt("vector", "tensor_copy", bt1[:, :, :, 1:2], w3s)
        t_tabs = t_tabs_done
        t_btab = t_tabs_done
        ar.free("tbsc")

        bank_free = [None] * 8
        bank_free[7] = t_cloc
        bank_ready = [None] * 8

        def phase_barrier():
            for e in ("vector", "scalar", "gpsimd", "sync"):
                for sname in ("T", "V", "S", "G"):
                    if SEM_OF.get(e) != sname:
                        p.wait(e, p.tok(sname))

        def tgroup_tok(bank):
            tok = p.inc_last("tensor", "T")
            bank_ready[bank] = tok
            return tok

        def tri_col(kc, q0):
            return (kc if kc < 4 else kc - 4) * 128

        def tmm(out_ap, lhsT, rhs, bank, start, stop, **kw):
            if start:
                p.wait("tensor", bank_free[bank])
            if stop:
                tok = p.op("tensor", "matmul", out_ap, lhsT, rhs, start=start,
                           stop=stop, inc=[("T", 1)], **kw)
                bank_ready[bank] = tok
                return tok
            p.op("tensor", "matmul", out_ap, lhsT, rhs, start=start, stop=stop,
                 **kw)
            return None

        LNB_M, LNB_Q, LNB_R = 4, 5, 4

        def ln_pass(srcfn, dstfn, quarters, lg=None, pre=None, post=None):
            castbuf = ar.alloc("ln_cast", 2048, BF16)
            sqbuf = ar.alloc("ln_sq", 2048, BF16)
            tmpq = ar.alloc("ln_tmp", 2048, F32)
            lnbc = ar.alloc("ln_bc", 2048, F32)
            prev_ssq = None
            for qi in range(quarters):
                if pre is not None:
                    pre(qi)
                t_mean = t_ssq = None
                prev_guard = prev_ssq
                for bat in range(2):
                    cs = range(4 * bat, 4 * bat + 4)
                    p.wait("gpsimd", prev_guard)
                    p.wait("vector", prev_guard)
                    vc = None
                    for c in cs:
                        pieces = srcfn(c, qi)
                        cc = c - 4 * bat
                        if not any(off == 0 for (_, off, _) in pieces):
                            p.op("gpsimd", "memset",
                                 castbuf[:, cc * 512: cc * 512 + 1], 0.0)
                        for (a, off, n) in pieces:
                            eng_c = "gpsimd" if c >= 5 else "vector"
                            vc2 = p.op(eng_c, "tensor_copy",
                                       castbuf[:, cc * 512 + off:
                                               cc * 512 + off + n],
                                       a, inc=[(SEM_OF[eng_c], 1)])
                            if c % 2:
                                vc = vc2
                    p.wait("tensor", vc)
                    p.wait("tensor", p.tok("V"))
                    if bat == 0:
                        p.wait("tensor", bank_free[LNB_M])
                    pieces_all = [(c, a, off, n) for c in cs
                                  for (a, off, n) in srcfn(c, qi)]
                    for pi, (c, a, off, n) in enumerate(pieces_all):
                        cc = c - 4 * bat
                        p.op("tensor", "matmul", psf[LNB_M][0:1, off:off + n],
                             onesln_bf,
                             castbuf[:, cc * 512 + off: cc * 512 + off + n],
                             start=(bat == 0 and pi == 0),
                             stop=(bat == 1 and pi == len(pieces_all) - 1))
                    t_mean = tgroup_tok(LNB_M)
                    p.wait("scalar", prev_guard)
                    p.wait("scalar", vc)
                    p.wait("scalar", p.tok("V"))
                    sq_tok = p.opt("scalar", "activation", sqbuf, castbuf,
                                   AF.Square)
                    p.wait("tensor", sq_tok)
                    if bat == 0:
                        p.wait("tensor", bank_free[LNB_Q])
                    for pi, (c, a, off, n) in enumerate(pieces_all):
                        cc = c - 4 * bat
                        p.op("tensor", "matmul", psf[LNB_Q][0:1, off:off + n],
                             onesln_bf,
                             sqbuf[:, cc * 512 + off: cc * 512 + off + n],
                             start=(bat == 0 and pi == 0),
                             stop=(bat == 1 and pi == len(pieces_all) - 1))
                    t_ssq = tgroup_tok(LNB_Q)
                    prev_guard = t_ssq
                prev_ssq = t_ssq
                r_mean, r_rs, r_mrs = row(0), row(1), row(2)
                p.wait("vector", t_mean)
                p.op("vector", "tensor_copy", r_mean, psf[LNB_M][0:1, 0:512])
                v1 = p.opt("vector", "tensor_tensor", r_rs, r_mean, r_mean,
                           ALU.mult)
                bank_free[LNB_M] = v1
                p.wait("vector", t_ssq)
                v2 = p.opt("vector", "tensor_tensor", r_rs,
                           psf[LNB_Q][0:1, 0:512], r_rs, ALU.subtract)
                bank_free[LNB_Q] = v2
                p.wait("scalar", v2)
                p.op("scalar", "activation", r_rs, r_rs, AF.Ln, bias=eps_p0)
                s1 = p.opt("scalar", "activation", r_rs, r_rs, AF.Exp, scale=-0.5)
                p.wait("vector", s1)
                v3 = p.opt("vector", "scalar_tensor_tensor", r_mrs, r_mean,
                           -1.0, r_rs, ALU.mult, ALU.mult)
                p.wait("tensor", v3)
                p.wait("tensor", bank_free[LNB_R])
                tr = p.op("tensor", "matmul", psf[LNB_R][:, 0:512], ones_row,
                          r_rs, start=True, stop=True, inc=[("T", 1)])
                bank_ready[LNB_R] = tr
                p.wait("tensor", bank_free[LNB_Q])
                tm_ = p.op("tensor", "matmul", psf[LNB_Q][:, 0:512], ones_row,
                           r_mrs, start=True, stop=True, inc=[("T", 1)])
                bank_ready[LNB_Q] = tm_
                p.wait("vector", tr)
                p.wait("vector", tm_)
                rs_sb = lnbc[:, 0:512]
                mrs_sb = lnbc[:, 512:1024]
                p.op("vector", "tensor_copy", rs_sb, psf[LNB_R][:, 0:512])
                vev = p.opt("vector", "tensor_copy", mrs_sb,
                            psf[LNB_Q][:, 0:512])
                bank_free[LNB_R] = vev
                bank_free[LNB_Q] = vev
                p.wait("gpsimd", vev)
                for c in range(NCH):
                    pieces = srcfn(c, qi)
                    dst = dstfn(c, qi)
                    covered0 = any(off == 0 for (_, off, _) in pieces)
                    eng_c = "gpsimd" if c >= 5 else "vector"
                    sl_ = 0 if eng_c == "vector" else 1
                    tq2 = tmpq[:, 512 * sl_: 512 * sl_ + 512]
                    for (a, off, n) in pieces:
                        p.op(eng_c, "tensor_tensor", tq2[:, off:off + n],
                             a, rs_sb[:, off:off + n], ALU.mult)
                        p.op(eng_c, "tensor_add",
                             dst[:, off:off + n], tq2[:, off:off + n],
                             mrs_sb[:, off:off + n], inc=[(SEM_OF[eng_c], 1)])
                        if lg is not None:
                            p.op(eng_c, "tensor_scalar",
                                 dst[:, off:off + n], dst[:, off:off + n],
                                 lng_c(lg, c), lnb_c(lg, c),
                                 ALU.mult, ALU.add, inc=[(SEM_OF[eng_c], 1)])
                    if not covered0:
                        if lg is None:
                            p.op(eng_c, "tensor_copy", dst[:, 0:1],
                                 mrs_sb[:, 0:1], inc=[(SEM_OF[eng_c], 1)])
                        else:
                            p.op(eng_c, "tensor_scalar", dst[:, 0:1],
                                 mrs_sb[:, 0:1], lng_c(lg, c),
                                 lnb_c(lg, c), ALU.mult, ALU.add,
                                 inc=[(SEM_OF[eng_c], 1)])
                vlast = p.tok("G")
                if post is not None:
                    post(qi, vlast)
            ar.free("ln_cast", "ln_sq", "ln_tmp", "ln_bc")

        def ffn_entry_ln():
            phase_barrier()
            spill_toks = [None] * 4

            def pre(qi):
                q0 = qi * 512
                p.wait("scalar", p.tok("V"))
                p.wait("scalar", p.tok("dX"))
                p.wait("scalar", p.tok("dGX"))
                p.wait("scalar", p.tok("dS"))
                src3 = xy.rearrange("p (c t) -> p c t", t=T)[:, :, q0:q0 + 512]
                dst3 = dspill.rearrange("p (c t) -> p c t", t=T)[:, :, q0:q0 + 512]
                spill_toks[qi] = p.op("scalar", "dma_start", out=dst3, in_=src3,
                                      inc=[("dX", 16)])

            def src(c, qi):
                return [(xyc(c, qi * 512, 512), 0, 512)]

            def dst(c, qi):
                p.wait("gpsimd", spill_toks[qi])
                p.wait("vector", spill_toks[qi])
                return xyc(c, qi * 512, 512)

            ln_pass(src, dst, 4, pre=pre)

        def rope8(eng, u, out_bf, cs, ssn, ssp, scr):
            u4 = u.rearrange("p (h v d) -> p h v d", v=2, d=32)
            sa = scr[:, 0:512].rearrange("p (h d) -> p h d", d=64)
            sa4 = scr[:, 0:512].rearrange("p (h v d) -> p h v d", v=2, d=32)
            sb = scr[:, 512:1024].rearrange("p (h d) -> p h d", d=64)
            sb3 = scr[:, 512:1024].rearrange("p (h d) -> p h d", d=64)
            o3 = out_bf.rearrange("p (h d) -> p h d", d=64)
            cs2 = cs.unsqueeze(1).unsqueeze(1).broadcast_to([128, 8, 2, 32])
            ssnb = ssn.unsqueeze(1).broadcast_to([128, 8, 32])
            sspb = ssp.unsqueeze(1).broadcast_to([128, 8, 32])
            u3 = u.rearrange("p (h d) -> p h d", d=64)
            p.op(eng, "tensor_tensor", sa4, u4, cs2, ALU.mult)
            p.op(eng, "tensor_tensor", sb3[:, :, 0:32], u3[:, :, 32:64], ssnb,
                 ALU.mult)
            p.op(eng, "tensor_tensor", sb3[:, :, 32:64], u3[:, :, 0:32], sspb,
                 ALU.mult)
            return p.opt(eng, "tensor_tensor", o3, sa, sb, ALU.add)

        def ffn_layer(l, final_ffn=False, parts=("p1", "p1b", "p2"),
                      keep_bufs=False):
            if "p1" in parts:
                phase_barrier()
                qf = ar.alloc("qf", 8192, BF16)
                kf = ar.alloc("kf", 8192, BF16)
                vt = ar.alloc("vt", 8320, BF16)
                sqp = ar.alloc("sqp", 2048, F32)
                stat = ar.alloc("stat", 64, F32)
                ybs = ar.alloc("ybs", 8192, BF16)
                wqs = ar.alloc("wqs", 12288, BF16)
                qtmp = ar.alloc("qtmp", 4096, F32)
                ropesc = ar.alloc("ropesc", 2048, F32)
                ropes2 = ar.alloc("ropes2", 2048, F32)
                qb16 = ar.alloc("qb16", 2048, BF16)
                qsb = ar.alloc("qsb", 256, F32)
                ybsv, wqsv, qfv, kfv = (ar.view(n) for n in
                                        ("ybs", "wqs", "qf", "kf"))
                vtv, qtmpv, ropev = ar.view("vt"), ar.view("qtmp"), ar.view("ropesc")
                rope2v = ar.view("ropes2")
                qbv, qsbv = ar.view("qb16"), ar.view("qsb")
                sqpv, statv = ar.view("sqp"), ar.view("stat")

                def vt_t(i):
                    return vtv[:, 520 * i: 520 * i + 520]

                v65 = vtv.rearrange("p (i h d) -> p (i h) d", h=8, d=65)
                p.op("gpsimd", "memset", v65[:, :, 64:65], 1.0)

                p.wait("sync", t_conv[f"wqkv{l}"])
                t_wqs = p.op("sync", "dma_start",
                             out=wqsv.rearrange("p (c o) -> p c o", o=1536),
                             in_=_chunked(dwb[f"wqkv{l}"]), inc=[("dW0", 16)])


                p.wait("tensor", t_small)
                p.wait("tensor", bank_free[7])
                p.op("tensor", "matmul", psf[7][:, 0:64], ones_row,
                     qsrow[:, 64 * l:64 * l + 64], start=True, stop=True)
                tq = p.op("tensor", "matmul", psf[7][:, 64:128], ones_row,
                          ksrow[:, 64 * l:64 * l + 64], start=True, stop=True,
                          inc=[("T", 1)])
                bank_ready[7] = tq
                p.wait("vector", tq)
                vq = p.opt("vector", "tensor_copy", qsbv, psf[7][:, 0:128])
                bank_free[7] = vq

                def ybs_c(c, t0=0, n=1024):
                    return ybsv[:, c * 1024 + t0: c * 1024 + t0 + n]

                g_q = tq_ = tk_ = v_k = None
                pend_tr = None

                def emit_tr(i, sl, g_q2, v_k2):
                    nonlocal tq_, tk_
                    qbq = qbv[:, 1024 * sl: 1024 * sl + 512]
                    qbk = qbv[:, 1024 * sl + 512: 1024 * sl + 1024]
                    p.wait("tensor", g_q2)
                    p.wait("tensor", v_k2)
                    p.wait("tensor", bank_free[3])
                    for j in range(4):
                        p.op("tensor", "transpose",
                             psb[3][:, 128 * j:128 * j + 128],
                             qbq[:, 128 * j:128 * j + 128], idn)
                    tq_ = tgroup_tok(3)
                    p.wait("tensor", bank_free[6])
                    for j in range(4):
                        p.op("tensor", "transpose",
                             psb[6][:, 128 * j:128 * j + 128],
                             qbk[:, 128 * j:128 * j + 128], idn)
                    tk_ = tgroup_tok(6)
                    p.wait("vector", tq_)
                    v_e = p.opt("vector", "tensor_copy",
                                qfv.rearrange("p (c t) -> p c t", t=T)
                                [:, :, 128 * i:128 * i + 128],
                                psb[3][:, 0:512].rearrange(
                                    "p (j t) -> p j t", t=128))
                    bank_free[3] = v_e
                    p.wait("vector", tk_)
                    v_e2 = p.opt("vector", "tensor_copy",
                                 kfv.rearrange("p (c t) -> p c t", t=T)
                                 [:, :, 128 * i:128 * i + 128],
                                 psb[6][:, 0:512].rearrange(
                                     "p (j t) -> p j t", t=128))
                    bank_free[6] = v_e2
                for half in range(2):
                    h0 = half * 1024
                    p.wait("gpsimd", p.tok("T") if half else None)
                    p.wait("vector", p.tok("T") if half else None)
                    for c in range(4):
                        eng_c = "gpsimd" if c >= 5 else "vector"
                        p.op(eng_c, "tensor_copy", ybs_c(c), xyc(c, h0, 1024),
                             inc=[(SEM_OF[eng_c], 1)])
                    for c in range(4, 8):
                        eng_c = "gpsimd" if c >= 5 else "vector"
                        if half == 0:
                            p.op(eng_c, "tensor_copy", ybs_c(c, 1, 1023),
                                 xyc(c, 0, 1023), inc=[(SEM_OF[eng_c], 1)])
                            p.op("gpsimd", "memset", ybs_c(c, 0, 1), 0.0)
                        else:
                            p.op(eng_c, "tensor_copy", ybs_c(c, 0, 1024),
                                 xyc(c, h0 - 1, 1024), inc=[(SEM_OF[eng_c], 1)])
                    p.wait("tensor", p.tok("G"))
                    p.wait("tensor", p.tok("V"))
                    t_ybs = p.tok("G")
                    p.wait("tensor", t_wqs)
                    p.wait("tensor", t_tabs)
                    for it in range(8):
                        i = half * 8 + it
                        for og in range(3):
                            p.wait("tensor", bank_free[og])
                        for c in range(NCH):
                            lhs = ybs_c(c, it * 128, 128)
                            for og in range(3):
                                p.op("tensor", "matmul", psf[og], lhs,
                                     wqsv[:, c * 1536 + og * 512:
                                          c * 1536 + og * 512 + 512],
                                     start=(c == 0), stop=(c == NCH - 1))
                        t_qkv = p.inc_last("tensor", "T")
                        for og in range(3):
                            bank_ready[og] = t_qkv
                        p.wait("vector", t_qkv)
                        vt3 = vt_t(i).rearrange("p (h d) -> p h d", d=65)[:, :, 0:64]
                        ve_v = p.opt("vector", "tensor_copy", vt3,
                                     psf[2].rearrange("p (h d) -> p h d", d=64))
                        bank_free[2] = ve_v
                        p.wait("scalar", v_k)
                        p.wait("scalar", t_qkv)
                        s_q = p.opt("scalar", "activation", sqpv[:, 0:512],
                                    psf[0], AF.Square)
                        s_k = p.opt("scalar", "activation", sqpv[:, 512:1024],
                                    psf[1], AF.Square)
                        p.wait("vector", s_q)
                        p.op("vector", "tensor_reduce",
                             statv[:, 0:8].unsqueeze(2),
                             sqpv[:, 0:512].rearrange("p (h d) -> p h d", d=64),
                             AXX, ALU.add)
                        p.wait("vector", s_k)
                        p.op("vector", "tensor_reduce",
                             statv[:, 8:16].unsqueeze(2),
                             sqpv[:, 512:1024].rearrange("p (h d) -> p h d",
                                                         d=64), AXX, ALU.add)
                        v_n = p.opt("vector", "tensor_scalar", statv[:, 16:32],
                                    statv[:, 0:16], 1e-24, None, ALU.max)
                        p.wait("scalar", v_n)
                        p.op("scalar", "activation", statv[:, 16:32],
                             statv[:, 16:32], AF.Ln)
                        s_rs = p.opt("scalar", "activation", statv[:, 16:32],
                                     statv[:, 16:32], AF.Exp, scale=-0.5)
                        sl = i % 2
                        p.wait("vector", s_rs)
                        p.wait("vector", g_q)
                        uq = qtmpv[:, 1024 * sl: 1024 * sl + 512]
                        uk = qtmpv[:, 1024 * sl + 512: 1024 * sl + 1024]
                        for (psi, uu, q0_, st0) in ((0, uq, 0, 16),
                                                    (1, uk, 64, 24)):
                            p.op("vector", "tensor_tensor",
                                 uu.rearrange("p (h d) -> p h d", d=64),
                                 psf[psi].rearrange("p (h d) -> p h d", d=64),
                                 qsbv[:, q0_:q0_ + 64].unsqueeze(1)
                                 .broadcast_to([128, 8, 64]), ALU.mult)
                            v_u = p.opt("vector", "tensor_tensor",
                                        uu.rearrange("p (h d) -> p h d", d=64),
                                        uu.rearrange("p (h d) -> p h d", d=64),
                                        statv[:, st0:st0 + 8].unsqueeze(2)
                                        .broadcast_to([128, 8, 64]), ALU.mult)
                            bank_free[psi] = v_u
                        qbq = qbv[:, 1024 * sl: 1024 * sl + 512]
                        qbk = qbv[:, 1024 * sl + 512: 1024 * sl + 1024]
                        p.wait("gpsimd", p.tok("V"))
                        p.wait("gpsimd", tq_)
                        g_q = rope8("gpsimd", uq, qbq, tab(0, i),
                                    tab(1, i), tab(2, i), ropev)
                        p.wait("vector", tk_)
                        v_k = rope8("vector", uk, qbk, tab(3, i),
                                    tab(4, i), tab(5, i), rope2v)
                        if pend_tr is not None:
                            emit_tr(*pend_tr)
                        pend_tr = (i, sl, g_q, v_k)

                if pend_tr is not None:
                    emit_tr(*pend_tr)
                    pend_tr = None
                ar.free("ybs", "wqs", "qtmp", "ropesc", "ropes2", "qb16",
                        "qsb")

            if "p1b" in parts:
                phase_barrier()
                qfv, kfv, vtv = ar.view("qf"), ar.view("kf"), ar.view("vt")

                def vt_t(i):
                    return vtv[:, 520 * i: 520 * i + 520]

                of_ = ar.alloc("of", 8192, BF16)
                pb = ar.alloc("pb", 2048, BF16)
                pscr = ar.alloc("pscr", 512, BF16)
                wob = ar.alloc("wob", 4096, BF16)
                anrm = ar.alloc("anrm", 2048, F32)
                ofv, pbv, pscrv, wobv = (ar.view(n) for n in
                                         ("of", "pb", "pscr", "wob"))
                anv = ar.view("anrm")
                p.wait("sync", t_conv[f"wo{l}"])
                t_wo = p.op("sync", "dma_start",
                            out=wobv.rearrange("p (c o) -> p c o", o=1024),
                            in_=_chunked(dwb[f"wo{l}"]), inc=[("dW1", 16)])

                def headq(h, t0, n):
                    c, r = divmod(h, 2)
                    return qfv[64 * r:64 * r + 64, c * T + t0: c * T + t0 + n]

                def headk(h, t0, n):
                    c, r = divmod(h, 2)
                    return kfv[64 * r:64 * r + 64, c * T + t0: c * T + t0 + n]

                def heado(h, t0, n):
                    c, r = divmod(h, 2)
                    return ofv[64 * r:64 * r + 64, c * T + t0: c * T + t0 + n]

                prev_anrm = [None, None]

                def normalize_h(h, q0, ta):
                    hh = h % 2
                    abank = 4 + hh
                    r_rho = row(hh)
                    p.wait("scalar", ta)
                    p.op("scalar", "activation", r_rho,
                         psf[abank][64:65, 0:512], AF.Ln)
                    s_r = p.opt("scalar", "activation", r_rho, r_rho, AF.Exp,
                                scale=-1.0)
                    p.wait("vector", ta)
                    asl = anv[:, 512 * hh: 512 * hh + 512][0:64, :]
                    vr = p.opt("vector", "tensor_copy", asl,
                               psf[abank][0:64, 0:512])
                    bank_free[abank] = vr
                    bbank = 6
                    p.wait("tensor", s_r)
                    p.wait("tensor", bank_free[bbank])
                    tb = p.op("tensor", "matmul", psf[bbank][0:64, 0:512],
                              ones_row[:, 0:64], r_rho, start=True,
                              stop=True, inc=[("T", 1)])
                    bank_ready[bbank] = tb
                    p.wait("vector", tb)
                    vo = p.opt("vector", "tensor_mul", heado(h, q0, 512),
                               asl, psf[bbank][0:64, 0:512])
                    bank_free[bbank] = vo
                    prev_anrm[hh] = vo

                for w in range(4):
                    q0 = w * WIN

                    def emit_A(h, kc, span0, toff, ns, pchunk, scr, se, vm,
                               first_a, last, q0=q0):
                        hh = h % 2
                        abank = 4 + hh
                        kg = q0 - 512 + 128 * kc
                        va = vt_t(kg // 128).rearrange("p (g d) -> p g d",
                                                       d=65)[:, h, :]
                        p.wait("tensor", se)
                        if kc < 4:
                            clean0, cleann = 0, toff
                        else:
                            clean0, cleann = toff + 128, ns - toff - 128
                        if cleann > 0:
                            tmm(psf[abank][0:65, span0 + clean0:
                                           span0 + clean0 + cleann],
                                va, pchunk[:, clean0:clean0 + cleann],
                                abank, start=first_a, stop=False)
                            first_a = False
                        p.wait("tensor", vm)
                        tmm(psf[abank][0:65, tri_col(kc, q0):
                                       tri_col(kc, q0) + 128],
                            va, scr, abank, start=first_a, stop=last)
                        return False

                    pend = None
                    for h in range(8):
                        hh = h % 2
                        abank = 4 + hh
                        p.wait("tensor", bank_free[abank])
                        first_a = True
                        kcs = list(range(4, 8)) if w == 0 else list(range(8))
                        prev = None
                        for idx, kc in enumerate(kcs):
                            kg = q0 - 512 + 128 * kc
                            if kc < 4:
                                span0, span1, tri, mask = 0, 128 * (kc + 1), kc, mlo
                            else:
                                kr = kc - 4
                                span0, span1, tri, mask = 128 * kr, 512, kr, mup
                            ns = span1 - span0
                            sbank = 2 * hh + idx % 2
                            ts_ = tmm(psf[sbank][:, 0:ns], headk(h, kg, 128),
                                      headq(h, q0 + span0, ns), sbank,
                                      start=True, stop=True)
                            pchunk = pbv[:, 512 * (2 * hh + idx % 2):
                                         512 * (2 * hh + idx % 2) + ns]
                            p.wait("scalar", ts_)
                            se = p.opt("scalar", "activation", pchunk,
                                       psf[sbank][:, 0:ns], AF.Exp,
                                       bias=cloc_c(l), scale=8.0)
                            bank_free[sbank] = se
                            toff = tri * 128 - span0
                            scr = pscrv[:, 128 * (2 * hh + idx % 2):
                                        128 * (2 * hh + idx % 2) + 128]
                            eng_m = "gpsimd" if hh else "vector"
                            p.wait(eng_m, se)
                            vm = p.op(eng_m, "tensor_mul", scr,
                                      pchunk[:, toff:toff + 128], mask,
                                      inc=[(SEM_OF[eng_m], 1)])
                            if prev is not None:
                                first_a = emit_A(*prev, first_a, False)
                            prev = (h, kc, span0, toff, ns, pchunk, scr, se, vm)
                        first_a = emit_A(*prev, first_a, True)
                        ta = bank_ready[abank]
                        if pend is not None:
                            normalize_h(*pend)
                        pend = (h, q0, ta)
                    normalize_h(*pend)
                    p.wait("tensor", t_wo)
                    p.wait("tensor", p.tok("V"))
                    for ofc in range(8):
                        p.wait("tensor", bank_free[6])
                        for fc in range(4):
                            p.op("tensor", "matmul", psf[6],
                                 wobv[:, fc * 1024 + ofc * 128:
                                      fc * 1024 + ofc * 128 + 128],
                                 ofv[:, fc * T + q0: fc * T + q0 + 512],
                                 start=(fc == 0), stop=(fc == 3))
                        tw = tgroup_tok(6)
                        p.wait("vector", tw)
                        vv = p.opt("vector", "tensor_tensor", xyc(ofc, q0, 512),
                                   xyc(ofc, q0, 512), psf[6], ALU.add)
                        bank_free[6] = vv
                if not keep_bufs:
                    ar.free("of", "pb", "pscr", "wob", "anrm", "qf", "kf", "vt",
                            "sqp", "stat")

            if "p2" in parts:
                phase_barrier()
                w2b = ar.alloc("w2b", 22528, BF16)
                ffb = ar.alloc("ffb", 11264, BF16)
                lnh = ar.alloc("lnh", 8192, BF16)
                w1b = ar.alloc("w1b", 4096, BF16)
                ggs = ar.alloc("ggs", 2048, F32)
                t2s = ar.alloc("t2s", 2048, F32)
                lnhv, ffv, w1v, w2v = (ar.view(n) for n in
                                       ("lnh", "ffb", "w1b", "w2b"))
                ggv, t2v = ar.view("ggs"), ar.view("t2s")
                p.wait("sync", t_conv[f"w2_{l}"])
                p.op("sync", "dma_start",
                     out=w2v[:, 0:21 * 1024].rearrange("p (c o) -> p c o",
                                                       o=1024),
                     in_=_chunked(dwb[f"w2_{l}"][0:2688, :]),
                     inc=[("dW2", 16)])
                t_w2 = p.op("sync", "dma_start",
                            out=w2v[0:42, 21 * 1024:22 * 1024],
                            in_=dwb[f"w2_{l}"][2688:2730, :],
                            inc=[("dW2", 16)])
                p.wait("sync", t_conv[f"w1_{l}"])
                w1_slot_free = [None, None]
                shsv = sm[:, 280:288]

                def emit_save_ln(qq):
                    h0 = qq * 512
                    sv_w = shsv[:, (qq % 2) * 4: (qq % 2) * 4 + 4]
                    sv_r = shsv[:, ((qq + 1) % 2) * 4: ((qq + 1) % 2) * 4 + 4]
                    p.wait("gpsimd", p.tok("V"))
                    for c in range(4, 8):
                        t_sv = p.op("gpsimd", "tensor_copy",
                                    sv_w[:, c - 4:c - 3], xyc(c, h0 + 511, 1))

                    def src(c, qi, h0=h0, sv_r=sv_r):
                        q0 = h0
                        if c < 4:
                            return [(xyc(c, q0, 512), 0, 512)]
                        if q0 == 0:
                            return [(xyc(c, 0, 511), 1, 511)]
                        return [(sv_r[:, c - 4:c - 3], 0, 1),
                                (xyc(c, q0, 511), 1, 511)]

                    def dst(c, qi, qq=qq):
                        return lnhv[:, 4096 * (qq % 2) + c * 512:
                                    4096 * (qq % 2) + c * 512 + 512]

                    p.wait("vector", p.tok("T"))
                    p.wait("gpsimd", p.tok("T"))
                    p.wait("vector", p.tok("dGX"))
                    p.wait("gpsimd", p.tok("dGX"))
                    ln_pass(src, dst, 1, lg=l)
                    p.wait("tensor", p.tok("G"))
                    p.wait("tensor", p.tok("V"))
                    return t_sv

                emit_save_ln(0)
                for qq in range(4):
                    h0 = qq * 512
                    lnslot = 4096 * (qq % 2)
                    t_sv = None
                    for pair in range(NHP):
                        pslot = pair % 2
                        wblk = w1v[:, pslot * 2048: pslot * 2048 + 2048]
                        wblk3 = wblk.rearrange("p (c o) -> p c o", o=256)
                        pa0 = 128 * pair
                        pg0 = HID + 128 * pair
                        na = min(128, HID - pa0)
                        dsem = "dW0" if pslot == 0 else "dW1"
                        p.wait("sync", w1_slot_free[pslot])
                        p.op("sync", "dma_start", out=wblk3[:, :, 0:na],
                             in_=_chunked(dwb[f"w1_{l}"])[:, :, pa0:pa0 + na],
                             inc=[(dsem, 16)])
                        t_wp = p.op("sync", "dma_start",
                                    out=wblk3[:, :, 128:128 + na],
                                    in_=_chunked(dwb[f"w1_{l}"])
                                    [:, :, pg0:pg0 + na], inc=[(dsem, 16)])
                        p.wait("tensor", t_wp)
                        abk, gbk = pair % 2, 2 + pair % 2
                        p.wait("tensor", bank_free[abk])
                        p.wait("tensor", bank_free[gbk])
                        for c in range(NCH):
                            rhs = lnhv[:, lnslot + c * 512:
                                       lnslot + c * 512 + 512]
                            p.op("tensor", "matmul", psf[abk][0:na, 0:512],
                                 wblk3[:, c, 0:na], rhs, start=(c == 0),
                                 stop=(c == NCH - 1))
                            p.op("tensor", "matmul", psf[gbk][0:na, 0:512],
                                 wblk3[:, c, 128:128 + na], rhs, start=(c == 0),
                                 stop=(c == NCH - 1))
                        tg = p.inc_last("tensor", "T")
                        bank_ready[abk] = tg
                        bank_ready[gbk] = tg
                        w1_slot_free[pslot] = tg
                        p.wait("scalar", tg)
                        gsl = ggv[:, 512 * (pair % 2): 512 * (pair % 2) + 512]
                        sg = p.opt("scalar", "activation", gsl[0:na, :],
                                   psf[gbk][0:na, 0:512], AF.Gelu,
                                   bias=b1g_c(l, pair)[0:na, :])
                        bank_free[gbk] = sg
                        p.wait("vector", sg)
                        vf = p.opt("vector", "scalar_tensor_tensor",
                                   ffv[0:na, pair * 512: pair * 512 + 512],
                                   psf[abk][0:na, 0:512],
                                   b1a_c(l, pair)[0:na, :],
                                   gsl[0:na, :], ALU.add, ALU.mult)
                        bank_free[abk] = vf
                    t_ff = p.tok("V")
                    if qq < 3:
                        t_sv = emit_save_ln(qq + 1)
                    p.wait("tensor", t_ff)
                    p.wait("tensor", t_w2)
                    p.wait("vector", t_sv)
                    for ofc in range(NCH):
                        bank = 6 + ofc % 2
                        p.wait("tensor", bank_free[bank])
                        for pair in range(NHP):
                            npr = min(128, HID - 128 * pair)
                            p.op("tensor", "matmul", psf[bank][:, 0:512],
                                 w2v[0:npr, pair * 1024 + ofc * 128:
                                     pair * 1024 + ofc * 128 + 128],
                                 ffv[0:npr, pair * 512: pair * 512 + 512],
                                 start=(pair == 0), stop=(pair == NHP - 1))
                        tw2 = tgroup_tok(bank)
                        p.wait("vector", tw2)
                        yq = xyc(ofc, h0, 512)
                        tsl = t2v[:, 512 * (ofc % 2): 512 * (ofc % 2) + 512]
                        if not final_ffn:
                            p.op("vector", "tensor_scalar", tsl, psf[bank],
                                 b2_c(l, ofc), None, ALU.add)
                            vy = p.opt("vector", "tensor_tensor", yq, yq, tsl,
                                       ALU.add)
                        else:
                            p.op("vector", "tensor_scalar", tsl, psf[bank],
                                 b2_c(l, ofc), 0.5, ALU.add, ALU.mult)
                            vy = p.opt("vector", "scalar_tensor_tensor", yq,
                                       yq, 0.5, tsl, ALU.mult, ALU.add)
                            xr = dspill[:, ofc * T + h0: ofc * T + h0 + 512]
                            p.wait("gpsimd", vy)
                            p.op("gpsimd", "dma_start", out=yq, in_=xr,
                                 accum_op=ALU.add, inc=[("dGX", 16)])
                        bank_free[bank] = vy
                if not keep_bufs:
                    ar.free("lnh", "ffb", "w1b", "w2b", "ggs", "t2s")

        def attn_phase(keep_bufs=False):
            phase_barrier()
            p.wait("vector", p.tok("dX"))
            p.wait("vector", p.tok("dGX"))
            lnx = ar.alloc("lnx", 16384, BF16)
            lnxv = ar.view("lnx")

            def srcA(c, qi):
                return [(xyc(c, qi * 512, 512), 0, 512)]

            def dstA(c, qi):
                return lnxv[:, c * T + qi * 512: c * T + qi * 512 + 512]

            ln_pass(srcA, dstA, 4)
            p.wait("tensor", p.tok("G"))
            p.wait("tensor", p.tok("V"))
            t_lnx = p.tok("G")
            phase_barrier()

            wpb = ar.alloc("wpb", 8192, BF16)
            wpv = ar.view("wpb")
            p.wait("sync", t_conv["W_proj"])
            t_wpj = p.op("sync", "dma_start",
                         out=wpv.rearrange("p (c o) -> p c o", o=1024),
                         in_=_chunked(dwb["W_proj"]), inc=[("dW2", 16)])


            tmax = ar.alloc("tmax", 64, F32)
            tmaxv = ar.view("tmax")[0:16, 0:16]
            statv = ar.view("statq16")
            cst = ar.view("cstats")
            cbg = ar.view("cbig")

            for g in range(4):
                gqk = ar.alloc("gqk", 8192, BF16)
                gvt = ar.alloc("gvt", 4160, BF16)
                wat = ar.alloc("wat", 6144, BF16)
                qkst = ar.alloc("qkst", 1024, BF16)
                sqb = ar.alloc("sqb", 2048, F32)
                gqkv, gvtv, watv = ar.view("gqk"), ar.view("gvt"), ar.view("wat")
                qkstv, sqbv = ar.view("qkst"), ar.view("sqb")
                gv65 = gvtv.rearrange("p (i h d) -> p (i h) d", h=4, d=65)
                p.op("gpsimd", "memset", gv65[:, :, 64:65], 1.0)
                t_gv = p.op("gpsimd", "engine_nop", inc=[("G", 1)])
                p.wait("sync", t_conv["W_attn"])
                wa3 = watv.rearrange("p (c o) -> p c o", o=768)
                dwa = _chunked(dwb["W_attn"])
                for r in range(2):
                    p.op("sync", "dma_start",
                         out=wa3[:, :, 128 * r:128 * r + 128],
                         in_=dwa[:, :, 512 * r + 128 * g: 512 * r + 128 * g + 128],
                         inc=[("dW2", 16)])
                    p.op("sync", "dma_start",
                         out=wa3[:, :, 256 + 128 * r:256 + 128 * r + 128],
                         in_=dwa[:, :, 1024 + 512 * r + 128 * g:
                                 1024 + 512 * r + 128 * g + 128],
                         inc=[("dW2", 16)])
                t_wa = p.op("sync", "dma_start", out=wa3[:, :, 512:768],
                            in_=dwa[:, :, 2048 + 256 * g: 2048 + 256 * g + 256],
                            inc=[("dW2", 16)])
                p.wait("tensor", t_wa)
                p.wait("tensor", t_lnx)
                p.wait("tensor", t_btab)
                p.wait("tensor", t_gv)
                v_r = tq_ = None
                for i in range(NTILE):
                    p.wait("tensor", bank_free[0])
                    p.wait("tensor", bank_free[1])
                    for c in range(NCH):
                        lhs = lnxv[:, c * T + 128 * i: c * T + 128 * i + 128]
                        p.op("tensor", "matmul", psf[0], lhs, wa3[:, c, 0:512],
                             start=(c == 0), stop=(c == NCH - 1))
                        p.op("tensor", "matmul", psf[1][:, 0:256], lhs,
                             wa3[:, c, 512:768], start=(c == 0),
                             stop=(c == NCH - 1))
                    t_qkv = p.inc_last("tensor", "T")
                    bank_ready[0] = t_qkv
                    bank_ready[1] = t_qkv
                    p.wait("vector", t_qkv)
                    vt3 = gvtv[:, 260 * i:260 * i + 260].rearrange(
                        "p (h d) -> p h d", d=65)[:, :, 0:64]
                    ve_v = p.opt("vector", "tensor_copy", vt3,
                                 psf[1][:, 0:256].rearrange("p (h d) -> p h d",
                                                            d=64))
                    bank_free[1] = ve_v
                    p.wait("scalar", v_r)
                    p.wait("scalar", t_qkv)
                    s_q = p.opt("scalar", "activation", sqbv[:, 0:512], psf[0],
                                AF.Square)
                    p.wait("vector", s_q)
                    v_n = p.opt("vector", "tensor_reduce",
                                statv[:, 0:16].unsqueeze(2),
                                sqbv[:, 0:512].rearrange("p (h d) -> p h d",
                                                         d=32), AXX, ALU.add)
                    p.wait("tensor", v_n)
                    p.wait("tensor", bank_free[6])
                    tt_ = p.op("tensor", "transpose", psf[6][0:16, 0:128],
                               statv[:, 0:16], idnf, inc=[("T", 1)])
                    bank_ready[6] = tt_
                    p.wait("vector", tt_)
                    v_tm = p.opt("vector", "tensor_reduce",
                                 tmaxv[:, i:i + 1].unsqueeze(2),
                                 psf[6][0:16, 0:128].rearrange(
                                     "p (a b) -> p a b", a=1), AXX, ALU.max)
                    bank_free[6] = v_tm
                    qe = psf[0].rearrange("p (h j v) -> p h j v", j=16, v=2)
                    sa4 = sqbv[:, 0:512].rearrange("p (h j v) -> p h j v",
                                                   j=16, v=2)
                    sb4 = sqbv[:, 512:1024].rearrange("p (h j v) -> p h j v",
                                                      j=16, v=2)
                    ob = qkstv[:, 0:512].rearrange("p (h j v) -> p h j v",
                                                   j=16, v=2)
                    csb = btab_t(0, i).rearrange("p (j v) -> p j v", v=2) \
                        .unsqueeze(1).broadcast_to([128, 16, 16, 2])
                    ssb = btab_t(1, i).rearrange("p (j v) -> p j v", v=2) \
                        .unsqueeze(1).broadcast_to([128, 16, 16, 2])
                    p.wait("vector", tq_)
                    p.op("vector", "tensor_tensor", sa4, qe, csb, ALU.mult)
                    p.op("vector", "tensor_tensor", sb4[:, :, :, 0:1],
                         qe[:, :, :, 1:2], ssb[:, :, :, 0:1], ALU.mult)
                    p.op("vector", "tensor_tensor", sb4[:, :, :, 1:2],
                         qe[:, :, :, 0:1], ssb[:, :, :, 1:2], ALU.mult)
                    v_r = p.opt("vector", "tensor_add", ob, sa4, sb4)
                    bank_free[0] = v_r
                    p.wait("tensor", v_r)
                    p.wait("tensor", bank_free[3])
                    for j in range(4):
                        p.op("tensor", "transpose",
                             psb[3][:, 128 * j:128 * j + 128],
                             qkstv[:, 128 * j:128 * j + 128], idn)
                    tq_ = tgroup_tok(3)
                    p.wait("vector", tq_)
                    v_e = p.opt("vector", "tensor_copy",
                                gqkv.rearrange("p (c t) -> p c t", t=T)
                                [:, :, 128 * i:128 * i + 128],
                                psb[3][:, 0:512].rearrange(
                                    "p (j t) -> p j t", t=128))
                    bank_free[3] = v_e
                t_all = p.tok("V")
                p.wait("vector", t_all)
                p.op("vector", "tensor_reduce", cst[0:16, 0:4].unsqueeze(2),
                     tmaxv.rearrange("p (a b) -> p a b", b=4), AXX, ALU.max)
                v_km = p.opt("vector", "tensor_reduce",
                             cst[0:16, 4:5].unsqueeze(2),
                             tmaxv.rearrange("p (a b) -> p a b", a=1), AXX,
                             ALU.max)
                p.wait("scalar", v_km)
                b_h = p.op("scalar", "dma_start", out=drow[0:1, 0:8],
                           in_=cst[8:16, 4:5], inc=[("dX", 16)])
                p.wait("gpsimd", b_h)
                t_kb = p.op("gpsimd", "dma_start", out=cst[0:8, 5:6],
                            in_=drow[0:1, 0:8], inc=[("dGX", 16)])
                p.wait("vector", t_kb)
                p.op("vector", "tensor_scalar", cst[0:8, 0:4], cst[0:8, 0:4],
                     cst[0:8, 5:6], None, ALU.mult)
                v_c = p.opt("vector", "tensor_scalar", cst[0:8, 0:4],
                            cst[0:8, 0:4], 1e-30, None, ALU.max)
                p.wait("scalar", v_c)
                p.op("scalar", "activation", cst[0:8, 0:4], cst[0:8, 0:4],
                     AF.Ln)
                s_c = p.opt("scalar", "activation", cst[0:8, 0:4],
                            cst[0:8, 0:4], AF.Exp, scale=0.5)
                p.wait("vector", s_c)
                v_c2 = p.opt("vector", "tensor_scalar", cst[0:8, 0:4],
                             cst[0:8, 0:4], -1.0, None, ALU.mult)
                p.wait("scalar", v_c2)
                b1_ = p.op("scalar", "dma_start", out=drow[0:1, 0:32],
                           in_=cst[0:8, 0:4], inc=[("dX", 16)])
                p.wait("gpsimd", b1_)
                b2_ = p.op("gpsimd", "dma_start", out=row(2, 32),
                           in_=drow[0:1, 0:32], inc=[("dGX", 16)])
                p.wait("tensor", b2_)
                p.wait("tensor", bank_free[6])
                t_cb = p.op("tensor", "matmul", psf[6][:, 0:32], ones_row,
                            row(2, 32), start=True, stop=True, inc=[("T", 1)])
                bank_ready[6] = t_cb
                p.wait("vector", t_cb)
                v_cb = p.opt("vector", "tensor_copy", cbg[:, 0:32],
                             psf[6][:, 0:32])
                bank_free[6] = v_cb

                def ghead(is_k, hh, t0, n):
                    c = (2 if is_k else 0) + hh // 4
                    r = hh % 4
                    return gqkv[32 * r:32 * r + 32, c * T + t0: c * T + t0 + n]

                dfm = ar.alloc("dfm", 4096, BF16)
                pbuf2 = ar.alloc("pbuf2", 2048, BF16)
                pscr2 = ar.alloc("pscr2", 512, BF16)
                ast = ar.alloc("ast", 4096, F32)
                dstg = ar.alloc("dstg", 1088, F32)
                dnb = ar.alloc("dnb", 512, BF16)
                dfmv = ar.view("dfm")
                pb2v, ps2v = ar.view("pbuf2"), ar.view("pscr2")
                astv, dstgv, dnbv = ar.view("ast"), ar.view("dstg"), ar.view("dnb")
                def pair_tail(pr, qt, q0, slot, ve0, ve1):
                    p.wait("tensor", ve0)
                    p.wait("tensor", ve1)
                    for side in range(2):
                        bank = 4 + side
                        p.wait("tensor", bank_free[bank])
                        for j in range(4):
                            p.op("tensor", "transpose",
                                 psf[bank][:, 128 * j:128 * j + 65],
                                 astv[0:65, 1024 * slot + 512 * side + 128 * j:
                                      1024 * slot + 512 * side + 128 * j + 128],
                                 idnf[0:65, 0:65])
                        tgroup_tok(bank)
                    t_t2 = bank_ready[5]
                    p.wait("vector", bank_ready[4])
                    p.wait("vector", t_t2)
                    a1 = psf[4].rearrange("p (j u) -> p j u", u=128)
                    a2 = psf[5].rearrange("p (j u) -> p j u", u=128)
                    p.op("vector", "reciprocal",
                         dstgv[:, 512:516].unsqueeze(2), a1[:, :, 64:65])
                    p.op("vector", "reciprocal",
                         dstgv[:, 516:520].unsqueeze(2), a2[:, :, 64:65])
                    p.op("vector", "tensor_tensor",
                         dstgv[:, 0:256].rearrange("p (j d) -> p j d", d=64),
                         a1[:, :, 0:64],
                         dstgv[:, 512:516].unsqueeze(2)
                         .broadcast_to([128, 4, 64]), ALU.mult)
                    v_n2 = p.opt("vector", "tensor_tensor",
                                 dstgv[:, 256:512].rearrange(
                                     "p (j d) -> p j d", d=64),
                                 a2[:, :, 0:64],
                                 dstgv[:, 516:520].unsqueeze(2)
                                 .broadcast_to([128, 4, 64]), ALU.mult)
                    bank_free[4] = v_n2
                    bank_free[5] = v_n2
                    v_d = p.opt("vector", "scalar_tensor_tensor",
                                dstgv[:, 256:512], dstgv[:, 256:512],
                                lamcol, dstgv[:, 0:256], ALU.mult,
                                ALU.subtract)
                    p.wait("scalar", v_d)
                    s_sq = p.opt("scalar", "activation", dstgv[:, 0:256],
                                 dstgv[:, 256:512], AF.Square)
                    p.wait("vector", s_sq)
                    v_ss = p.opt("vector", "tensor_reduce",
                                 dstgv[:, 512:516].unsqueeze(2),
                                 dstgv[:, 0:256].rearrange(
                                     "p (j d) -> p j d", d=64), AXX, ALU.add)
                    p.wait("scalar", v_ss)
                    p.op("scalar", "activation", dstgv[:, 512:516],
                         dstgv[:, 512:516], AF.Ln, bias=eps_col,
                         scale=1.0 / 64.0)
                    s_rm = p.opt("scalar", "activation", dstgv[:, 512:516],
                                 dstgv[:, 512:516], AF.Exp, scale=-0.5)
                    p.wait("vector", s_rm)
                    p.op("vector", "tensor_scalar", dstgv[:, 512:516],
                         dstgv[:, 512:516], -(1.0 - LAMBDA_INIT), None,
                         ALU.mult)
                    v_dn = p.opt("vector", "tensor_tensor",
                                 dnbv.rearrange("p (j d) -> p j d", d=64)
                                 [:, :, :] if False else
                                 dnbv[:, 0:256].rearrange(
                                     "p (j d) -> p j d", d=64),
                                 dstgv[:, 256:512].rearrange(
                                     "p (j d) -> p j d", d=64),
                                 dstgv[:, 512:516].unsqueeze(2)
                                 .broadcast_to([128, 4, 64]), ALU.mult)
                    p.wait("tensor", v_dn)
                    p.wait("tensor", bank_free[6])
                    for j in range(4):
                        p.op("tensor", "transpose",
                             psb[6][0:64, 128 * j:128 * j + 128],
                             dnbv[:, 64 * j:64 * j + 64], idn)
                    t_tb = tgroup_tok(6)
                    p.wait("vector", t_tb)
                    cc, rr = divmod(pr, 2)
                    v_df = p.opt("vector", "tensor_copy",
                                 dfmv[64 * rr:64 * rr + 64,
                                      cc * T + q0: cc * T + q0 + 512]
                                 .rearrange("p (j t) -> p j t", t=128),
                                 psb[6][:, 0:512].rearrange(
                                     "p (j t) -> p j t", t=128)[0:64, :, :])
                    bank_free[6] = v_df

                p.wait("vector", v_cb)
                p.wait("scalar", v_cb)
                tails = []
                ve_pair = [None, None]
                for qt in range(4):
                    q0 = qt * 512
                    for pr in range(4):
                        for side in range(2):
                            hh = pr + 4 * side
                            abank = 2 + side
                            p.wait("tensor", bank_free[abank])
                            first_a = True
                            nkc = 4 * qt + 4
                            prev = None

                            def emit_bA(kc, pchunk, scr, se, vm, first_a, last,
                                        abank=abank, q0=q0, qt=qt, pr=pr):
                                diag = kc - 4 * qt
                                span0 = 128 * diag if diag >= 0 else 0
                                ns = 512 - span0
                                va = gvtv[:, 260 * kc + 65 * pr:
                                          260 * kc + 65 * pr + 65]
                                p.wait("tensor", se)
                                if diag < 0:
                                    tmm(psf[abank][0:65, 0:512], va, pchunk,
                                        abank, start=first_a, stop=last)
                                    return False
                                if ns > 128:
                                    tmm(psf[abank][0:65, span0 + 128:512],
                                        va, pchunk[:, 128:ns], abank,
                                        start=first_a, stop=False)
                                    first_a = False
                                p.wait("tensor", vm)
                                tmm(psf[abank][0:65, span0:span0 + 128],
                                    va, scr, abank, start=first_a, stop=last)
                                return False

                            for kc in range(nkc):
                                kg = 128 * kc
                                diag = kc - 4 * qt
                                span0 = 128 * diag if diag >= 0 else 0
                                ns = 512 - span0
                                sbank = kc % 2
                                ts_ = tmm(psf[sbank][:, 0:ns],
                                          ghead(True, hh, kg, 128),
                                          ghead(False, hh, q0 + span0, ns),
                                          sbank, start=True, stop=True,
                                          tile_position=(32 * (hh % 4), 0))
                                pchunk = pb2v[:, 512 * (kc % 2):
                                              512 * (kc % 2) + ns]
                                p.wait("scalar", ts_)
                                se = p.opt("scalar", "activation", pchunk,
                                           psf[sbank][:, 0:ns], AF.Exp,
                                           bias=cbg[:, 4 * hh + qt:
                                                    4 * hh + qt + 1],
                                           scale=1.0 / math.sqrt(32.0))
                                bank_free[sbank] = se
                                vm = None
                                scr = None
                                if diag >= 0:
                                    scr = ps2v[:, 128 * (kc % 2):
                                               128 * (kc % 2) + 128]
                                    eng_m = "gpsimd" if side else "vector"
                                    p.wait(eng_m, se)
                                    vm = p.op(eng_m, "tensor_mul", scr,
                                              pchunk[:, 0:128], mup,
                                              inc=[(SEM_OF[eng_m], 1)])
                                if prev is not None:
                                    first_a = emit_bA(*prev, first_a, False)
                                prev = (kc, pchunk, scr, se, vm)
                            first_a = emit_bA(*prev, first_a, True)
                            ta = bank_ready[abank]
                            p.wait("vector", ta)
                            ve = p.opt("vector", "tensor_copy",
                                       astv[0:65, 1024 * (pr % 2) + 512 * side:
                                            1024 * (pr % 2) + 512 * side + 512],
                                       psf[abank][0:65, 0:512])
                            bank_free[abank] = ve
                            ve_pair[side] = ve
                        tails.append((pr, qt, q0, pr % 2,
                                      ve_pair[0], ve_pair[1]))
                        if len(tails) > 1:
                            pair_tail(*tails.pop(0))
                for t_ in tails:
                    pair_tail(*t_)
                tails = []
                t_dfm = p.tok("V")
                p.wait("tensor", t_dfm)
                p.wait("tensor", t_wpj)
                for ofc in range(8):
                    for tq in range(4):
                        bank = tq % 2
                        p.wait("tensor", bank_free[bank])
                        for cc in range(2):
                            p.op("tensor", "matmul", psf[bank],
                                 wpv[:, (2 * g + cc) * 1024 + 128 * ofc:
                                     (2 * g + cc) * 1024 + 128 * ofc + 128],
                                 dfmv[:, cc * T + 512 * tq:
                                      cc * T + 512 * tq + 512],
                                 start=(cc == 0), stop=(cc == 1))
                        twp = tgroup_tok(bank)
                        p.wait("vector", twp)
                        v_x = p.opt("vector", "tensor_tensor",
                                    xyc(ofc, 512 * tq, 512),
                                    xyc(ofc, 512 * tq, 512), psf[bank], ALU.add)
                        bank_free[bank] = v_x
                ar.free("gqk", "gvt", "wat", "qkst", "sqb", "dfm", "pbuf2",
                        "pscr2", "ast", "dstg", "dnb")
                phase_barrier()
            if not keep_bufs:
                ar.free("lnx", "wpb", "tmax")

        def final_store():
            phase_barrier()
            p.wait("tensor", p.tok("dX"))
            p.wait("tensor", p.tok("dGX"))
            stg = ar.alloc("outst", 2048, F32)
            stv = ar.view("outst")
            t_store = None
            for i in range(NTILE):
                for half in range(2):
                    b = half
                    p.wait("tensor", bank_free[b])
                    for j in range(4):
                        c = half * 4 + j
                        p.op("tensor", "transpose",
                             psf[b][:, 128 * j:128 * j + 128],
                             xyc(c, 128 * i, 128), idnf)
                    tf = tgroup_tok(b)
                    p.wait("vector", tf)
                    p.wait("vector", t_store)
                    vv = p.opt("vector", "tensor_copy",
                               stv[:, half * 512: half * 512 + 512], psf[b])
                    bank_free[b] = vv
                p.wait("sync", p.tok("V"))
                t_store = p.op("sync", "dma_start",
                               out=dout[128 * i:128 * i + 128, :], in_=stv,
                               inc=[("dO", 16)])
            p.wait("sync", t_store)

        p.wait("vector", t_xin)
        if upto == "tiny":
            pass
        elif upto == "ln0":
            ffn_entry_ln()
        elif upto == "p1":
            ffn_entry_ln()
            ffn_layer(0, parts=("p1",), keep_bufs=True)
        elif upto == "p1b":
            ffn_entry_ln()
            ffn_layer(0, parts=("p1", "p1b"), keep_bufs=True)
        elif upto == "l0":
            ffn_entry_ln()
            ffn_layer(0)
        elif upto == "ffn1":
            ffn_entry_ln()
            ffn_layer(0)
            ffn_layer(1, final_ffn=True)
        elif upto == "attn":
            ffn_entry_ln()
            ffn_layer(0)
            ffn_layer(1, final_ffn=True)
            attn_phase()
        else:
            ffn_entry_ln()
            ffn_layer(0)
            ffn_layer(1, final_ffn=True)
            attn_phase()
            ffn_entry_ln()
            ffn_layer(0)
            ffn_layer(1, final_ffn=True)
        final_store()

        if dumps_req:
            p.barrier_all()
            p.wait("sync", p.tok("dX"))
            p.wait("sync", p.tok("dGX"))
            tok = None
            for nm in dumps_req:
                if nm == "XY":
                    ap_, shp, dt = xy[:, :], [128, NCH * T], F32
                elif nm == "SM":
                    ap_, shp, dt = sm[:, :], [128, 1800], F32
                else:
                    ap_ = ar.view(nm)
                    dt = ar.regions[nm][2]
                    shp = list(ap_.shape)
                dd = nc.dram_tensor("d_" + nm, shp, dt, kind="ExternalOutput").ap()
                tok = p.op("sync", "dma_start", out=dd, in_=ap_, inc=[("dO", 16)])
            p.wait("sync", tok)

        block = es.enter_context(nc.Block())
        p.emit(block)
    return nc


def _prep_weights(inputs):
    f = np.float32
    wq = np.asarray(inputs["L0_ffn_wqkv"], f)
    wo = np.asarray(inputs["L0_ffn_wo"], f)
    w1 = np.asarray(inputs["L0_ffn_w1"], f)
    w2 = np.asarray(inputs["L0_ffn_w2"], f)
    lam = np.zeros((1, 130), f)
    lam[0, 0:32] = np.asarray(inputs["L0_attn_lambda_q1"], f)
    lam[0, 32:64] = np.asarray(inputs["L0_attn_lambda_k1"], f)
    lam[0, 64:96] = np.asarray(inputs["L0_attn_lambda_q2"], f)
    lam[0, 96:128] = np.asarray(inputs["L0_attn_lambda_k2"], f)
    qs = np.zeros((1, 128), f)
    qs[0, 0:64] = np.asarray(inputs["L0_ffn_q_scale"], f)[0]
    qs[0, 64:128] = np.asarray(inputs["L0_ffn_q_scale"], f)[1]
    ks = np.zeros((1, 128), f)
    ks[0, 0:64] = np.asarray(inputs["L0_ffn_k_scale"], f)[0]
    ks[0, 64:128] = np.asarray(inputs["L0_ffn_k_scale"], f)[1]

    def fmcols(vec, n):
        return np.asarray(vec, f).reshape(n, 128).T.copy()

    lng = np.concatenate([fmcols(inputs["L0_ffn_ln_g"][l], 8) for l in range(2)],
                         axis=1)
    lnb = np.concatenate([fmcols(inputs["L0_ffn_ln_b"][l], 8) for l in range(2)],
                         axis=1)
    b2 = np.concatenate([fmcols(inputs["L0_ffn_b2"][l], 8) for l in range(2)],
                        axis=1)
    b1p = np.zeros((128, 88), f)
    for l in range(2):
        b1l = np.asarray(inputs["L0_ffn_b1"][l], f)
        va = np.zeros(22 * 128, f)
        va[:HID] = b1l[:HID]
        vg = np.zeros(22 * 128, f)
        vg[:HID] = b1l[HID:]
        b1p[:, 44 * l: 44 * l + 22] = va.reshape(22, 128).T
        b1p[:, 44 * l + 22: 44 * l + 44] = vg.reshape(22, 128).T
    return {
        "W_attn": np.ascontiguousarray(inputs["L0_attn_W_attn"], f),
        "W_proj": np.ascontiguousarray(inputs["L0_attn_W_proj"], f),
        "lam": lam, "qs": qs, "ks": ks,
        "wqkv0": np.ascontiguousarray(wq[0]), "wqkv1": np.ascontiguousarray(wq[1]),
        "wo0": np.ascontiguousarray(wo[0]), "wo1": np.ascontiguousarray(wo[1]),
        "w1_0": np.ascontiguousarray(w1[0]), "w1_1": np.ascontiguousarray(w1[1]),
        "w2_0": np.ascontiguousarray(w2[0]), "w2_1": np.ascontiguousarray(w2[1]),
        "lng": lng, "lnb": lnb, "b1": b1p, "b2": b2,
    }


def _prep_x(x_b):
    return np.ascontiguousarray(np.asarray(x_b, np.float32).T.reshape(NCH, 128, T))


def kernel(**inputs):
    x = np.asarray(inputs["L0_attn_x"], np.float32)
    w = _prep_weights(inputs)
    nc = build_program("all")
    in_maps = []
    for b in range(8):
        m = dict(w)
        m["x"] = _prep_x(x[b])
        in_maps.append(m)
    res = run_bass_kernel_spmd(nc, in_maps, core_ids=list(range(8)))
    out = np.stack([res.results[b]["out"] for b in range(8)], axis=0)
    return out.astype(np.float32)


# version 09
# speedup vs baseline: 1.4641x; 1.0447x over previous
import math
import contextlib
import numpy as np
import ml_dtypes
import concourse.bass as bass
import concourse.mybir as mybir
from concourse.bass_utils import run_bass_kernel_spmd

F32 = mybir.dt.float32
BF16 = mybir.dt.bfloat16
I32 = mybir.dt.int32
AF = mybir.ActivationFunctionType
ALU = mybir.AluOpType
AXX = mybir.AxisListType.X

T = 2048
D = 1024
NCH = 8
NTILE = 16
WIN = 512
HID = 2730
NHP = 22
LAMBDA_INIT = 0.2
C_LOC = 8.0 * math.exp((127.0 / 256.0) * (-math.log(25.6 / 89.6)))

ENGINES = ("sync", "gpsimd", "tensor", "vector", "scalar")
SEM_OF = {"vector": "V", "scalar": "S", "gpsimd": "G", "tensor": "T"}


class Prog:
    def __init__(self, nc, es):
        self.nc = nc
        self.stream = {e: [] for e in ENGINES}
        self.sems = {}
        self.count = {}
        self.last_wait = {e: {} for e in ENGINES}
        self._es = es

    def sem(self, name):
        if name not in self.sems:
            self.sems[name] = self._es.enter_context(self.nc.semaphore(name))
            self.count[name] = 0
        return name

    def wait(self, eng, tok):
        if tok is None:
            return
        s, v = tok
        if v <= 0 or self.last_wait[eng].get(s, 0) >= v:
            return
        self.last_wait[eng][s] = v
        self.stream[eng].append(("__wait__", (self.sems[s], v), {}, []))

    ASYNC_OPS = {"memset", "tensor_reduce", "reduce_max", "reduce_sum",
                 "reciprocal", "iota", "affine_select", "memzero"}
    ASYNC_OPS = ASYNC_OPS | {"tensor_tensor", "tensor_scalar", "tensor_copy",
                             "scalar_tensor_tensor", "tensor_add",
                             "tensor_sub", "tensor_mul", "tensor_max",
                             "activation"}

    def op(self, eng, fname, *args, inc=(), **kwargs):
        if eng == "gpsimd" and fname not in ("dma_start", "__wait__"):
            self.wait("gpsimd", self.tok("G"))
            if not any(s == "G" for s, _ in inc):
                inc = list(inc) + [("G", 1)]
        selfwait = (eng in ("vector", "scalar") and fname in self.ASYNC_OPS)
        if selfwait and not any(s == SEM_OF[eng] for s, _ in inc):
            inc = list(inc) + [(SEM_OF[eng], 1)]
        toks = []
        incs = []
        for s, amt in inc:
            self.count[s] += amt
            toks.append((s, self.count[s]))
            incs.append((self.sems[s], amt))
        self.stream[eng].append((fname, args, kwargs, incs))
        if selfwait:
            self.wait(eng, (SEM_OF[eng], self.count[SEM_OF[eng]]))
        return toks[0] if toks else None

    def opt(self, eng, fname, *args, **kwargs):
        return self.op(eng, fname, *args, inc=[(SEM_OF[eng], 1)], **kwargs)

    def tok(self, semname):
        return (semname, self.count[semname])

    def inc_last(self, eng, semname, amt=1):
        fname, args, kwargs, incs = self.stream[eng][-1]
        assert fname != "__wait__", "inc_last on a wait"
        self.count[semname] += amt
        incs.append((self.sems[semname], amt))
        return (semname, self.count[semname])

    def barrier_all(self):
        for s in ("T", "V", "S", "G"):
            self.wait("sync", self.tok(s))

    def emit(self, block):
        for eng_name in ENGINES:
            ops = self.stream[eng_name]

            def body(eng, ops=ops):
                for fname, args, kwargs, incs in ops:
                    if fname == "__wait__":
                        eng.wait_ge(*args)
                        continue
                    inst = getattr(eng, fname)(*args, **kwargs)
                    for sh, amt in incs:
                        inst.then_inc(sh, amt)

            getattr(block, eng_name)(body)


class Arena:

    def __init__(self, tensor, cols):
        self.t = tensor
        self.cols = cols
        self.regions = {}
        self.cursor = 0
        self.high = 0

    def alloc(self, name, cols, dtype=BF16, align=32):
        assert name not in self.regions, name
        start = (self.cursor + align - 1) // align * align
        assert start + cols <= self.cols, \
            f"arena overflow: {name} {start}+{cols}>{self.cols}"
        self.cursor = start + cols
        self.high = max(self.high, self.cursor)
        self.regions[name] = (start, cols, dtype)
        return self.view(name)

    def view(self, name):
        start, nb, dtype = self.regions[name]
        ap = self.t.ap()
        if dtype == F32:
            return ap.bitcast(F32)[:, start // 2: (start + nb) // 2]
        return ap[:, start: start + nb]

    def free(self, *names):
        for name in names:
            self.regions.pop(name, None)
        self.cursor = max((s + n for s, n, _ in self.regions.values()), default=0)


def _chunked(dram_ap, p=128):
    return dram_ap.rearrange("(c q) o -> c q o", q=p).transpose([1, 0, 2])


def build_program(upto="all", dumps_req=()):
    nc = bass.Bass("TRN2", target_bir_lowering=False)
    es = contextlib.ExitStack()

    dx = nc.dram_tensor("x", [NCH, 128, T], F32, kind="ExternalInput").ap()
    wshapes = {
        "W_attn": [D, 3 * D], "W_proj": [D, D], "lam": [1, 130],
        "wqkv0": [D, 1536], "wqkv1": [D, 1536],
        "wo0": [512, D], "wo1": [512, D],
        "w1_0": [D, 2 * HID], "w1_1": [D, 2 * HID],
        "w2_0": [HID, D], "w2_1": [HID, D],
        "qs": [1, 128], "ks": [1, 128],
        "lng": [128, 16], "lnb": [128, 16], "b1": [128, 88], "b2": [128, 16],
    }
    dw = {nm: nc.dram_tensor(nm, shp, F32, kind="ExternalInput").ap()
          for nm, shp in wshapes.items()}
    dout = nc.dram_tensor("out", [T, D], F32, kind="ExternalOutput").ap()
    dwb = {nm: nc.dram_tensor("b_" + nm, wshapes[nm], BF16, kind="Internal").ap()
           for nm in ("wqkv0", "wqkv1", "W_attn", "W_proj", "wo0", "wo1",
                      "w1_0", "w1_1", "w2_0", "w2_1")}
    dspill = nc.dram_tensor("xspill", [128, NCH * T], F32, kind="Internal").ap()
    drow = nc.dram_tensor("rowsc", [1, 64], F32, kind="Internal").ap()

    with es:
        p = Prog(nc, es)
        XY = es.enter_context(nc.sbuf_tensor("XY", [128, NCH * T], F32))
        AR = es.enter_context(nc.sbuf_tensor("AR", [128, 67000], BF16))
        SM = es.enter_context(nc.sbuf_tensor("SM", [128, 2900], F32))
        ar = Arena(AR, 67000)
        psum = [es.enter_context(nc.psum_tensor(f"ps{i}", [128, 512], F32))
                for i in range(8)]
        psf = [x.ap() for x in psum]
        psb = [x.ap().bitcast(BF16) for x in psum]

        for s in ("dS", "dG", "dC", "dX", "dGX", "T", "V", "S", "G", "dO",
                  "dW0", "dW1", "dW2"):
            p.sem(s)
        for i in range(10):
            p.sem(f"cv{i}")

        xy = XY.ap()
        sm = SM.ap()

        def xyc(c, t0=0, n=T):
            return xy[:, c * T + t0: c * T + t0 + n]

        ones_col = sm[:, 0:1]
        lng_a = sm[:, 4:20]
        lnb_a = sm[:, 20:36]
        b2_a = sm[:, 36:52]
        b1_a = sm[:, 52:140]
        cloc_a = sm[:, 140:142]
        lamcol = sm[:, 142:143]
        eps_col = sm[:, 143:144]
        eps_p0 = sm[0:1, 144:145]

        def lng_c(l, c):
            return lng_a[:, l * 8 + c: l * 8 + c + 1]

        def lnb_c(l, c):
            return lnb_a[:, l * 8 + c: l * 8 + c + 1]

        def b2_c(l, c):
            return b2_a[:, l * 8 + c: l * 8 + c + 1]

        def b1a_c(l, j):
            return b1_a[:, l * 44 + j: l * 44 + j + 1]

        def b1g_c(l, j):
            return b1_a[:, l * 44 + 22 + j: l * 44 + 22 + j + 1]

        def cloc_c(l):
            return cloc_a[:, l:l + 1]

        def row(i, n=512):
            assert i <= 2 and 1300 + i * 512 + n <= 2900
            return sm[0:1, 1300 + i * 512: 1300 + i * 512 + n]

        def r1(i, n=1):
            return sm[0:1, 990 + i: 990 + i + n]

        ones_row = sm[0:1, 150:278]
        qsrow = sm[0:1, 600:728]
        ksrow = sm[0:1, 728:856]
        lamrow = sm[0:1, 856:986]

        t_xin = p.op("sync", "dma_start",
                     out=xy.rearrange("p (c t) -> p c t", t=T),
                     in_=dx.transpose([1, 0, 2]), inc=[("dS", 16)])

        p.op("gpsimd", "memset", ones_col, 1.0)
        p.op("gpsimd", "memset", sm[0:1, 150:278], 1.0)
        p.op("gpsimd", "memset", eps_col, 1e-5)
        p.op("gpsimd", "memset", eps_p0, 1e-5)
        idn = ar.alloc("ident", 128, BF16)
        p.op("gpsimd", "memset", idn, 0.0)
        p.op("gpsimd", "affine_select", out=idn, in_=idn, compare_op=ALU.not_equal,
             fill=1.0, base=0, pattern=[[-1, 128]], channel_multiplier=1)
        idnf = ar.alloc("identf", 256, F32)
        p.op("gpsimd", "memset", idnf, 0.0)
        p.op("gpsimd", "affine_select", out=idnf, in_=idnf, compare_op=ALU.not_equal,
             fill=1.0, base=0, pattern=[[-1, 128]], channel_multiplier=1)
        onesln_bf = ar.alloc("onesln", 32, BF16)[:, 0:1]
        p.op("gpsimd", "memset", onesln_bf, 1.0 / 1024.0)
        mlo = ar.alloc("mask_lo", 128, BF16)
        p.op("gpsimd", "memset", mlo, 1.0)
        p.op("gpsimd", "affine_select", out=mlo, in_=mlo, compare_op=ALU.is_ge,
             fill=0.0, base=0, pattern=[[-1, 128]], channel_multiplier=1)
        mup = ar.alloc("mask_up", 128, BF16)
        p.op("gpsimd", "memset", mup, 1.0)
        p.op("gpsimd", "affine_select", out=mup, in_=mup, compare_op=ALU.is_ge,
             fill=0.0, base=0, pattern=[[1, 128]], channel_multiplier=-1)
        statq16 = ar.alloc("statq16", 64, F32)
        cstats = ar.alloc("cstats", 16, F32)
        cbig = ar.alloc("cbig", 64, F32)
        tabsP = ar.alloc("tabsP", 3072, BF16)
        btabP = ar.alloc("btabP", 1024, BF16)
        t_const = p.op("gpsimd", "engine_nop", inc=[("G", 1)])

        t_small = None
        for nm, dst in (("lng", lng_a), ("lnb", lnb_a), ("b2", b2_a),
                        ("b1", b1_a)):
            t_small = p.op("sync", "dma_start", out=dst, in_=dw[nm],
                           inc=[("dS", 16)])
        t_small = p.op("sync", "dma_start", out=qsrow, in_=dw["qs"], inc=[("dS", 16)])
        t_small = p.op("sync", "dma_start", out=ksrow, in_=dw["ks"], inc=[("dS", 16)])
        t_small = p.op("sync", "dma_start", out=lamrow, in_=dw["lam"], inc=[("dS", 16)])

        import os as _os
        conv_order = ["wqkv0", "wo0", "w1_0", "w2_0", "wqkv1", "wo1", "w1_1",
                      "w2_1", "W_attn", "W_proj"]
        t_conv = {}
        for ci, nm in enumerate(conv_order):
            t_conv[nm] = p.op("gpsimd", "dma_start", out=dwb[nm], in_=dw[nm],
                              inc=[(f"cv{ci}", 16)])

        p.wait("vector", t_small)
        for l in range(2):
            p.op("vector", "tensor_reduce", r1(0 + 4 * l).unsqueeze(2),
                 qsrow[:, 64 * l:64 * l + 64].rearrange("p (a b) -> p a b", a=1),
                 AXX, ALU.max, apply_absolute_value=True)
            p.op("vector", "tensor_reduce", r1(1 + 4 * l).unsqueeze(2),
                 ksrow[:, 64 * l:64 * l + 64].rearrange("p (a b) -> p a b", a=1),
                 AXX, ALU.max, apply_absolute_value=True)
            p.op("vector", "tensor_tensor", r1(2 + 4 * l), r1(0 + 4 * l),
                 r1(1 + 4 * l), ALU.mult)
            p.opt("vector", "tensor_scalar", r1(3 + 4 * l), r1(2 + 4 * l),
                  -C_LOC, None, ALU.mult)
        p.op("vector", "tensor_tensor", r1(8, 32), lamrow[:, 0:32],
             lamrow[:, 32:64], ALU.mult)
        p.op("vector", "tensor_tensor", r1(40, 32), lamrow[:, 64:96],
             lamrow[:, 96:128], ALU.mult)
        p.op("vector", "tensor_reduce", r1(72).unsqueeze(2),
             r1(8, 32).rearrange("p (a b) -> p a b", a=1), AXX, ALU.add)
        tok = p.opt("vector", "tensor_reduce", r1(73).unsqueeze(2),
                    r1(40, 32).rearrange("p (a b) -> p a b", a=1), AXX, ALU.add)
        p.wait("scalar", tok)
        p.op("scalar", "activation", r1(74), r1(72), AF.Exp)
        tok = p.opt("scalar", "activation", r1(75), r1(73), AF.Exp)
        p.wait("vector", tok)
        p.op("vector", "tensor_sub", r1(76), r1(74), r1(75))
        t_lamv = p.opt("vector", "tensor_scalar", r1(77), r1(76), LAMBDA_INIT,
                       None, ALU.add)
        p.wait("tensor", t_lamv)
        p.wait("tensor", t_const)
        p.op("tensor", "matmul", psf[7][:, 0:1], ones_row, r1(77), start=True,
             stop=True)
        p.op("tensor", "matmul", psf[7][:, 1:2], ones_row, r1(3), start=True,
             stop=True)
        tk = p.op("tensor", "matmul", psf[7][:, 2:3], ones_row, r1(7),
                  start=True, stop=True, inc=[("T", 1)])
        p.wait("vector", tk)
        p.op("vector", "tensor_copy", lamcol, psf[7][:, 0:1])
        p.op("vector", "tensor_copy", cloc_c(0), psf[7][:, 1:2])
        t_cloc = p.opt("vector", "tensor_copy", cloc_c(1), psf[7][:, 2:3])

        def tab(tb, i):
            return tabsP[:, tb * 512 + 32 * i: tb * 512 + 32 * i + 32]

        def btab_t(tb, i):
            return btabP[:, tb * 512 + 32 * i: tb * 512 + 32 * i + 32]

        tbsc = ar.alloc("tbsc", 10240, F32)
        tv = tbsc
        i32v = tv.bitcast(I32)
        jint = i32v[:, 0:32]
        jint16 = i32v[:, 112:128]
        Ti = i32v[:, 96:112]
        framp = tv[:, 0:32]
        theta2p = tv[:, 32:64]
        lnbase = tv[:, 64:96]
        Tf = tv[:, 160:176]
        framp16 = tv[:, 112:128]
        theta16 = tv[:, 128:144]
        pw = tv[:, 176:192]
        wA = tv[:, 512:1024]
        wB = tv[:, 1024:1536]
        wC = tv[:, 1536:2048]
        wD = tv[:, 2048:2560]
        wS = tv[:, 2560:3072]
        wI = tv[:, 3072:3584]
        wint = i32v[:, 3584:4096]
        wG = tv[:, 4096:4608]

        def bc3(row, nj):
            return row.unsqueeze(1).broadcast_to([128, 16, nj])

        def tile3(w, nj=32):
            return w[:, 0:16 * nj].rearrange("p (i j) -> p i j", j=nj)

        p.wait("gpsimd", t_const)
        p.op("gpsimd", "iota", jint, [[1, 32]], base=0, channel_multiplier=0)
        p.op("gpsimd", "iota", jint16, [[1, 16]], base=0, channel_multiplier=0)
        t_io = p.op("gpsimd", "iota", Ti, [[128, 16]], base=0,
                    channel_multiplier=1)
        p.wait("vector", t_io)
        p.op("vector", "tensor_copy", framp, jint)
        p.op("vector", "tensor_copy", framp16, jint16)
        tok = p.opt("vector", "tensor_copy", Tf, Ti)
        p.wait("scalar", tok)
        p.op("scalar", "activation", theta2p, framp, AF.Exp,
             scale=-math.log(1e4) / 32.0)
        p.op("scalar", "activation", theta2p, theta2p, AF.Identity,
             scale=1.0 / (2 * math.pi))
        t_th16 = p.opt("scalar", "activation", theta16, framp16, AF.Exp,
                       scale=-math.log(1e4) / 16.0)
        v = p.opt("vector", "tensor_scalar", lnbase, framp, 2.0 / 89.6,
                  25.6 / 89.6, ALU.mult, ALU.add)
        p.wait("vector", t_th16)
        p.op("vector", "tensor_scalar", theta16, theta16,
             1.0 / (2 * math.pi), None, ALU.mult)
        p.wait("scalar", v)
        t_lb = p.opt("scalar", "activation", lnbase, lnbase, AF.Ln)
        p.wait("vector", t_lb)

        last_sin = [None]

        def sincos(theta_row, nj, sin_out, cos_out):
            n = 16 * nj
            p.op("vector", "tensor_tensor", tile3(wA, nj), bc3(theta_row, nj),
                 Tf.unsqueeze(2).broadcast_to([128, 16, nj]), ALU.mult)
            for (dst, shift) in ((sin_out, None), (cos_out, 0.25)):
                src_ = wA[:, 0:n]
                if shift is not None:
                    p.op("vector", "tensor_scalar", wB[:, 0:n], src_, shift,
                         None, ALU.add)
                    src_ = wB[:, 0:n]
                p.wait("vector", last_sin[0])
                p.op("vector", "tensor_copy", wint[:, 0:n], src_)
                p.op("vector", "tensor_copy", wC[:, 0:n], wint[:, 0:n])
                p.op("vector", "tensor_sub", wC[:, 0:n], src_, wC[:, 0:n])
                p.op("vector", "tensor_scalar", wG[:, 0:n], wC[:, 0:n], 0.5,
                     None, ALU.is_gt)
                vv = p.opt("vector", "tensor_sub", wC[:, 0:n], wC[:, 0:n],
                           wG[:, 0:n])
                p.wait("scalar", vv)
                sc = p.opt("scalar", "activation", dst, wC[:, 0:n], AF.Sin,
                           scale=2 * math.pi)
                last_sin[0] = sc
            return sc

        sc1 = sincos(theta2p, 32, wD[:, 0:512], wS[:, 0:512])
        p.op("vector", "tensor_scalar", pw, Tf, 1.0 / 256.0, -4.0, ALU.mult,
             ALU.add)
        v = p.opt("vector", "tensor_tensor", tile3(wI), bc3(lnbase, 32),
                  pw.unsqueeze(2).broadcast_to([128, 16, 32]), ALU.mult)
        p.wait("scalar", v)
        p.wait("scalar", sc1)
        s_S = p.opt("scalar", "activation", wA[:, 0:512], wI[:, 0:512], AF.Exp)
        s_I = p.opt("scalar", "activation", wB[:, 0:512], wI[:, 0:512], AF.Exp,
                    scale=-1.0)
        p.wait("vector", s_S)
        p.wait("vector", s_I)
        p.op("vector", "tensor_mul", tabsP[:, 0:512], wS[:, 0:512],
             wA[:, 0:512])
        p.op("vector", "tensor_mul", tabsP[:, 1024:1536], wD[:, 0:512],
             wA[:, 0:512])
        p.op("vector", "scalar_tensor_tensor", tabsP[:, 512:1024],
             wD[:, 0:512], -1.0, wA[:, 0:512], ALU.mult, ALU.mult)
        p.op("vector", "tensor_mul", tabsP[:, 1536:2048], wS[:, 0:512],
             wB[:, 0:512])
        p.op("vector", "tensor_mul", tabsP[:, 2560:3072], wD[:, 0:512],
             wB[:, 0:512])
        t_tabs_done = p.opt("vector", "scalar_tensor_tensor",
                            tabsP[:, 2048:2560], wD[:, 0:512], -1.0,
                            wB[:, 0:512], ALU.mult, ALU.mult)
        sc2 = sincos(theta16, 16, wD[:, 0:256], wS[:, 0:256])
        p.wait("vector", sc2)
        cse = tabsP.bitcast(BF16) if False else None
        bt0 = btabP[:, 0:512].rearrange("p (i j v) -> p i j v", j=16, v=2)
        bt1 = btabP[:, 512:1024].rearrange("p (i j v) -> p i j v", j=16, v=2)
        w3c = wS[:, 0:256].rearrange("p (i j) -> p i j", j=16).unsqueeze(3)
        w3s = wD[:, 0:256].rearrange("p (i j) -> p i j", j=16).unsqueeze(3)
        p.op("vector", "tensor_copy", bt0[:, :, :, 0:1], w3c)
        p.op("vector", "tensor_copy", bt0[:, :, :, 1:2], w3c)
        p.op("vector", "scalar_tensor_tensor", bt1[:, :, :, 0:1], w3s, -1.0,
             w3s, ALU.mult, ALU.bypass)
        t_tabs_done = p.opt("vector", "tensor_copy", bt1[:, :, :, 1:2], w3s)
        t_tabs = t_tabs_done
        t_btab = t_tabs_done
        ar.free("tbsc")

        bank_free = [None] * 8
        bank_free[7] = t_cloc
        bank_ready = [None] * 8

        def phase_barrier():
            for e in ("vector", "scalar", "gpsimd", "sync"):
                for sname in ("T", "V", "S", "G"):
                    if SEM_OF.get(e) != sname:
                        p.wait(e, p.tok(sname))

        def tgroup_tok(bank):
            tok = p.inc_last("tensor", "T")
            bank_ready[bank] = tok
            return tok

        def tri_col(kc, q0):
            return (kc if kc < 4 else kc - 4) * 128

        def tmm(out_ap, lhsT, rhs, bank, start, stop, **kw):
            if start:
                p.wait("tensor", bank_free[bank])
            if stop:
                tok = p.op("tensor", "matmul", out_ap, lhsT, rhs, start=start,
                           stop=stop, inc=[("T", 1)], **kw)
                bank_ready[bank] = tok
                return tok
            p.op("tensor", "matmul", out_ap, lhsT, rhs, start=start, stop=stop,
                 **kw)
            return None

        LNB_M, LNB_Q, LNB_R = 4, 5, 4

        def ln_pass(srcfn, dstfn, quarters, lg=None, pre=None, post=None):
            castbuf = ar.alloc("ln_cast", 2048, BF16)
            sqbuf = ar.alloc("ln_sq", 2048, BF16)
            tmpq = ar.alloc("ln_tmp", 2048, F32)
            lnbc = ar.alloc("ln_bc", 2048, F32)
            prev_ssq = None
            for qi in range(quarters):
                if pre is not None:
                    pre(qi)
                t_mean = t_ssq = None
                prev_guard = prev_ssq
                for bat in range(2):
                    cs = range(4 * bat, 4 * bat + 4)
                    p.wait("gpsimd", prev_guard)
                    p.wait("vector", prev_guard)
                    vc = None
                    for c in cs:
                        pieces = srcfn(c, qi)
                        cc = c - 4 * bat
                        if not any(off == 0 for (_, off, _) in pieces):
                            p.op("gpsimd", "memset",
                                 castbuf[:, cc * 512: cc * 512 + 1], 0.0)
                        for (a, off, n) in pieces:
                            eng_c = "gpsimd" if c >= 5 else "vector"
                            vc2 = p.op(eng_c, "tensor_copy",
                                       castbuf[:, cc * 512 + off:
                                               cc * 512 + off + n],
                                       a, inc=[(SEM_OF[eng_c], 1)])
                            if c % 2:
                                vc = vc2
                    p.wait("tensor", vc)
                    p.wait("tensor", p.tok("V"))
                    if bat == 0:
                        p.wait("tensor", bank_free[LNB_M])
                    pieces_all = [(c, a, off, n) for c in cs
                                  for (a, off, n) in srcfn(c, qi)]
                    for pi, (c, a, off, n) in enumerate(pieces_all):
                        cc = c - 4 * bat
                        p.op("tensor", "matmul", psf[LNB_M][0:1, off:off + n],
                             onesln_bf,
                             castbuf[:, cc * 512 + off: cc * 512 + off + n],
                             start=(bat == 0 and pi == 0),
                             stop=(bat == 1 and pi == len(pieces_all) - 1))
                    t_mean = tgroup_tok(LNB_M)
                    p.wait("scalar", prev_guard)
                    p.wait("scalar", vc)
                    p.wait("scalar", p.tok("V"))
                    sq_tok = p.opt("scalar", "activation", sqbuf, castbuf,
                                   AF.Square)
                    p.wait("tensor", sq_tok)
                    if bat == 0:
                        p.wait("tensor", bank_free[LNB_Q])
                    for pi, (c, a, off, n) in enumerate(pieces_all):
                        cc = c - 4 * bat
                        p.op("tensor", "matmul", psf[LNB_Q][0:1, off:off + n],
                             onesln_bf,
                             sqbuf[:, cc * 512 + off: cc * 512 + off + n],
                             start=(bat == 0 and pi == 0),
                             stop=(bat == 1 and pi == len(pieces_all) - 1))
                    t_ssq = tgroup_tok(LNB_Q)
                    prev_guard = t_ssq
                prev_ssq = t_ssq
                r_mean, r_rs, r_mrs = row(0), row(1), row(2)
                p.wait("vector", t_mean)
                p.op("vector", "tensor_copy", r_mean, psf[LNB_M][0:1, 0:512])
                v1 = p.opt("vector", "tensor_tensor", r_rs, r_mean, r_mean,
                           ALU.mult)
                bank_free[LNB_M] = v1
                p.wait("vector", t_ssq)
                v2 = p.opt("vector", "tensor_tensor", r_rs,
                           psf[LNB_Q][0:1, 0:512], r_rs, ALU.subtract)
                bank_free[LNB_Q] = v2
                p.wait("scalar", v2)
                p.op("scalar", "activation", r_rs, r_rs, AF.Ln, bias=eps_p0)
                s1 = p.opt("scalar", "activation", r_rs, r_rs, AF.Exp, scale=-0.5)
                p.wait("vector", s1)
                v3 = p.opt("vector", "scalar_tensor_tensor", r_mrs, r_mean,
                           -1.0, r_rs, ALU.mult, ALU.mult)
                p.wait("tensor", v3)
                p.wait("tensor", bank_free[LNB_R])
                tr = p.op("tensor", "matmul", psf[LNB_R][:, 0:512], ones_row,
                          r_rs, start=True, stop=True, inc=[("T", 1)])
                bank_ready[LNB_R] = tr
                p.wait("tensor", bank_free[LNB_Q])
                tm_ = p.op("tensor", "matmul", psf[LNB_Q][:, 0:512], ones_row,
                           r_mrs, start=True, stop=True, inc=[("T", 1)])
                bank_ready[LNB_Q] = tm_
                p.wait("vector", tr)
                p.wait("vector", tm_)
                rs_sb = lnbc[:, 0:512]
                mrs_sb = lnbc[:, 512:1024]
                p.op("vector", "tensor_copy", rs_sb, psf[LNB_R][:, 0:512])
                vev = p.opt("vector", "tensor_copy", mrs_sb,
                            psf[LNB_Q][:, 0:512])
                bank_free[LNB_R] = vev
                bank_free[LNB_Q] = vev
                p.wait("gpsimd", vev)
                for c in range(NCH):
                    pieces = srcfn(c, qi)
                    dst = dstfn(c, qi)
                    covered0 = any(off == 0 for (_, off, _) in pieces)
                    eng_c = "gpsimd" if c >= 5 else "vector"
                    sl_ = 0 if eng_c == "vector" else 1
                    tq2 = tmpq[:, 512 * sl_: 512 * sl_ + 512]
                    for (a, off, n) in pieces:
                        p.op(eng_c, "tensor_tensor", tq2[:, off:off + n],
                             a, rs_sb[:, off:off + n], ALU.mult)
                        p.op(eng_c, "tensor_add",
                             dst[:, off:off + n], tq2[:, off:off + n],
                             mrs_sb[:, off:off + n], inc=[(SEM_OF[eng_c], 1)])
                        if lg is not None:
                            p.op(eng_c, "tensor_scalar",
                                 dst[:, off:off + n], dst[:, off:off + n],
                                 lng_c(lg, c), lnb_c(lg, c),
                                 ALU.mult, ALU.add, inc=[(SEM_OF[eng_c], 1)])
                    if not covered0:
                        if lg is None:
                            p.op(eng_c, "tensor_copy", dst[:, 0:1],
                                 mrs_sb[:, 0:1], inc=[(SEM_OF[eng_c], 1)])
                        else:
                            p.op(eng_c, "tensor_scalar", dst[:, 0:1],
                                 mrs_sb[:, 0:1], lng_c(lg, c),
                                 lnb_c(lg, c), ALU.mult, ALU.add,
                                 inc=[(SEM_OF[eng_c], 1)])
                vlast = p.tok("G")
                if post is not None:
                    post(qi, vlast)
            ar.free("ln_cast", "ln_sq", "ln_tmp", "ln_bc")

        def ffn_entry_ln():
            phase_barrier()
            spill_toks = [None] * 4

            def pre(qi):
                q0 = qi * 512
                p.wait("scalar", p.tok("V"))
                p.wait("scalar", p.tok("dX"))
                p.wait("scalar", p.tok("dGX"))
                p.wait("scalar", p.tok("dS"))
                src3 = xy.rearrange("p (c t) -> p c t", t=T)[:, :, q0:q0 + 512]
                dst3 = dspill.rearrange("p (c t) -> p c t", t=T)[:, :, q0:q0 + 512]
                spill_toks[qi] = p.op("scalar", "dma_start", out=dst3, in_=src3,
                                      inc=[("dX", 16)])

            def src(c, qi):
                return [(xyc(c, qi * 512, 512), 0, 512)]

            def dst(c, qi):
                p.wait("gpsimd", spill_toks[qi])
                p.wait("vector", spill_toks[qi])
                return xyc(c, qi * 512, 512)

            ln_pass(src, dst, 4, pre=pre)

        def rope8(eng, u, out_bf, cs, ssn, ssp, scr):
            u4 = u.rearrange("p (h v d) -> p h v d", v=2, d=32)
            sa = scr[:, 0:512].rearrange("p (h d) -> p h d", d=64)
            sa4 = scr[:, 0:512].rearrange("p (h v d) -> p h v d", v=2, d=32)
            sb = scr[:, 512:1024].rearrange("p (h d) -> p h d", d=64)
            sb3 = scr[:, 512:1024].rearrange("p (h d) -> p h d", d=64)
            o3 = out_bf.rearrange("p (h d) -> p h d", d=64)
            cs2 = cs.unsqueeze(1).unsqueeze(1).broadcast_to([128, 8, 2, 32])
            ssnb = ssn.unsqueeze(1).broadcast_to([128, 8, 32])
            sspb = ssp.unsqueeze(1).broadcast_to([128, 8, 32])
            u3 = u.rearrange("p (h d) -> p h d", d=64)
            p.op(eng, "tensor_tensor", sa4, u4, cs2, ALU.mult)
            p.op(eng, "tensor_tensor", sb3[:, :, 0:32], u3[:, :, 32:64], ssnb,
                 ALU.mult)
            p.op(eng, "tensor_tensor", sb3[:, :, 32:64], u3[:, :, 0:32], sspb,
                 ALU.mult)
            return p.opt(eng, "tensor_tensor", o3, sa, sb, ALU.add)

        def ffn_layer(l, final_ffn=False, parts=("p1", "p1b", "p2"),
                      keep_bufs=False):
            if "p1" in parts:
                phase_barrier()
                qf = ar.alloc("qf", 8192, BF16)
                kf = ar.alloc("kf", 8192, BF16)
                vt = ar.alloc("vt", 8320, BF16)
                sqp = ar.alloc("sqp", 2048, F32)
                stat = ar.alloc("stat", 64, F32)
                ybs = ar.alloc("ybs", 8192, BF16)
                wqs = ar.alloc("wqs", 12288, BF16)
                qtmp = ar.alloc("qtmp", 4096, F32)
                ropesc = ar.alloc("ropesc", 2048, F32)
                ropes2 = ar.alloc("ropes2", 2048, F32)
                qb16 = ar.alloc("qb16", 2048, BF16)
                qsb = ar.alloc("qsb", 256, F32)
                ybsv, wqsv, qfv, kfv = (ar.view(n) for n in
                                        ("ybs", "wqs", "qf", "kf"))
                vtv, qtmpv, ropev = ar.view("vt"), ar.view("qtmp"), ar.view("ropesc")
                rope2v = ar.view("ropes2")
                qbv, qsbv = ar.view("qb16"), ar.view("qsb")
                sqpv, statv = ar.view("sqp"), ar.view("stat")

                def vt_t(i):
                    return vtv[:, 520 * i: 520 * i + 520]

                v65 = vtv.rearrange("p (i h d) -> p (i h) d", h=8, d=65)
                p.op("gpsimd", "memset", v65[:, :, 64:65], 1.0)

                p.wait("sync", t_conv[f"wqkv{l}"])
                t_wqs = p.op("sync", "dma_start",
                             out=wqsv.rearrange("p (c o) -> p c o", o=1536),
                             in_=_chunked(dwb[f"wqkv{l}"]), inc=[("dW0", 16)])


                p.wait("tensor", t_small)
                p.wait("tensor", bank_free[7])
                p.op("tensor", "matmul", psf[7][:, 0:64], ones_row,
                     qsrow[:, 64 * l:64 * l + 64], start=True, stop=True)
                tq = p.op("tensor", "matmul", psf[7][:, 64:128], ones_row,
                          ksrow[:, 64 * l:64 * l + 64], start=True, stop=True,
                          inc=[("T", 1)])
                bank_ready[7] = tq
                p.wait("vector", tq)
                vq = p.opt("vector", "tensor_copy", qsbv, psf[7][:, 0:128])
                bank_free[7] = vq

                def ybs_c(c, t0=0, n=1024):
                    return ybsv[:, c * 1024 + t0: c * 1024 + t0 + n]

                g_q = tq_ = tk_ = v_k = None
                pend_tr = None

                def emit_tr(i, sl, g_q2, v_k2):
                    nonlocal tq_, tk_
                    qbq = qbv[:, 1024 * sl: 1024 * sl + 512]
                    qbk = qbv[:, 1024 * sl + 512: 1024 * sl + 1024]
                    p.wait("tensor", g_q2)
                    p.wait("tensor", v_k2)
                    p.wait("tensor", bank_free[3])
                    for j in range(4):
                        p.op("tensor", "transpose",
                             psb[3][:, 128 * j:128 * j + 128],
                             qbq[:, 128 * j:128 * j + 128], idn)
                    tq_ = tgroup_tok(3)
                    p.wait("tensor", bank_free[6])
                    for j in range(4):
                        p.op("tensor", "transpose",
                             psb[6][:, 128 * j:128 * j + 128],
                             qbk[:, 128 * j:128 * j + 128], idn)
                    tk_ = tgroup_tok(6)
                    p.wait("vector", tq_)
                    v_e = p.opt("vector", "tensor_copy",
                                qfv.rearrange("p (c t) -> p c t", t=T)
                                [:, :, 128 * i:128 * i + 128],
                                psb[3][:, 0:512].rearrange(
                                    "p (j t) -> p j t", t=128))
                    bank_free[3] = v_e
                    p.wait("vector", tk_)
                    v_e2 = p.opt("vector", "tensor_copy",
                                 kfv.rearrange("p (c t) -> p c t", t=T)
                                 [:, :, 128 * i:128 * i + 128],
                                 psb[6][:, 0:512].rearrange(
                                     "p (j t) -> p j t", t=128))
                    bank_free[6] = v_e2
                for half in range(2):
                    h0 = half * 1024
                    p.wait("gpsimd", p.tok("T") if half else None)
                    p.wait("vector", p.tok("T") if half else None)
                    for c in range(4):
                        eng_c = "gpsimd" if c >= 5 else "vector"
                        p.op(eng_c, "tensor_copy", ybs_c(c), xyc(c, h0, 1024),
                             inc=[(SEM_OF[eng_c], 1)])
                    for c in range(4, 8):
                        eng_c = "gpsimd" if c >= 5 else "vector"
                        if half == 0:
                            p.op(eng_c, "tensor_copy", ybs_c(c, 1, 1023),
                                 xyc(c, 0, 1023), inc=[(SEM_OF[eng_c], 1)])
                            p.op("gpsimd", "memset", ybs_c(c, 0, 1), 0.0)
                        else:
                            p.op(eng_c, "tensor_copy", ybs_c(c, 0, 1024),
                                 xyc(c, h0 - 1, 1024), inc=[(SEM_OF[eng_c], 1)])
                    p.wait("tensor", p.tok("G"))
                    p.wait("tensor", p.tok("V"))
                    t_ybs = p.tok("G")
                    p.wait("tensor", t_wqs)
                    p.wait("tensor", t_tabs)
                    for it in range(8):
                        i = half * 8 + it
                        for og in range(3):
                            p.wait("tensor", bank_free[og])
                        for c in range(NCH):
                            lhs = ybs_c(c, it * 128, 128)
                            for og in range(3):
                                p.op("tensor", "matmul", psf[og], lhs,
                                     wqsv[:, c * 1536 + og * 512:
                                          c * 1536 + og * 512 + 512],
                                     start=(c == 0), stop=(c == NCH - 1))
                        t_qkv = p.inc_last("tensor", "T")
                        for og in range(3):
                            bank_ready[og] = t_qkv
                        p.wait("vector", t_qkv)
                        vt3 = vt_t(i).rearrange("p (h d) -> p h d", d=65)[:, :, 0:64]
                        ve_v = p.opt("vector", "tensor_copy", vt3,
                                     psf[2].rearrange("p (h d) -> p h d", d=64))
                        bank_free[2] = ve_v
                        p.wait("scalar", v_k)
                        p.wait("scalar", t_qkv)
                        s_q = p.opt("scalar", "activation", sqpv[:, 0:512],
                                    psf[0], AF.Square)
                        s_k = p.opt("scalar", "activation", sqpv[:, 512:1024],
                                    psf[1], AF.Square)
                        p.wait("vector", s_q)
                        p.op("vector", "tensor_reduce",
                             statv[:, 0:8].unsqueeze(2),
                             sqpv[:, 0:512].rearrange("p (h d) -> p h d", d=64),
                             AXX, ALU.add)
                        p.wait("vector", s_k)
                        p.op("vector", "tensor_reduce",
                             statv[:, 8:16].unsqueeze(2),
                             sqpv[:, 512:1024].rearrange("p (h d) -> p h d",
                                                         d=64), AXX, ALU.add)
                        v_n = p.opt("vector", "tensor_scalar", statv[:, 16:32],
                                    statv[:, 0:16], 1e-24, None, ALU.max)
                        p.wait("scalar", v_n)
                        p.op("scalar", "activation", statv[:, 16:32],
                             statv[:, 16:32], AF.Ln)
                        s_rs = p.opt("scalar", "activation", statv[:, 16:32],
                                     statv[:, 16:32], AF.Exp, scale=-0.5)
                        sl = i % 2
                        p.wait("vector", s_rs)
                        p.wait("vector", g_q)
                        uq = qtmpv[:, 1024 * sl: 1024 * sl + 512]
                        uk = qtmpv[:, 1024 * sl + 512: 1024 * sl + 1024]
                        for (psi, uu, q0_, st0) in ((0, uq, 0, 16),
                                                    (1, uk, 64, 24)):
                            p.op("vector", "tensor_tensor",
                                 uu.rearrange("p (h d) -> p h d", d=64),
                                 psf[psi].rearrange("p (h d) -> p h d", d=64),
                                 qsbv[:, q0_:q0_ + 64].unsqueeze(1)
                                 .broadcast_to([128, 8, 64]), ALU.mult)
                            v_u = p.opt("vector", "tensor_tensor",
                                        uu.rearrange("p (h d) -> p h d", d=64),
                                        uu.rearrange("p (h d) -> p h d", d=64),
                                        statv[:, st0:st0 + 8].unsqueeze(2)
                                        .broadcast_to([128, 8, 64]), ALU.mult)
                            bank_free[psi] = v_u
                        qbq = qbv[:, 1024 * sl: 1024 * sl + 512]
                        qbk = qbv[:, 1024 * sl + 512: 1024 * sl + 1024]
                        p.wait("gpsimd", p.tok("V"))
                        p.wait("gpsimd", tq_)
                        g_q = rope8("gpsimd", uq, qbq, tab(0, i),
                                    tab(1, i), tab(2, i), ropev)
                        p.wait("vector", tk_)
                        v_k = rope8("vector", uk, qbk, tab(3, i),
                                    tab(4, i), tab(5, i), rope2v)
                        if pend_tr is not None:
                            emit_tr(*pend_tr)
                        pend_tr = (i, sl, g_q, v_k)

                if pend_tr is not None:
                    emit_tr(*pend_tr)
                    pend_tr = None
                ar.free("ybs", "wqs", "qtmp", "ropesc", "ropes2", "qb16",
                        "qsb")

            if "p1b" in parts:
                phase_barrier()
                qfv, kfv, vtv = ar.view("qf"), ar.view("kf"), ar.view("vt")

                def vt_t(i):
                    return vtv[:, 520 * i: 520 * i + 520]

                of_ = ar.alloc("of", 8192, BF16)
                pb = ar.alloc("pb", 2048, BF16)
                pscr = ar.alloc("pscr", 512, BF16)
                wob = ar.alloc("wob", 4096, BF16)
                anrm = ar.alloc("anrm", 2048, F32)
                ofv, pbv, pscrv, wobv = (ar.view(n) for n in
                                         ("of", "pb", "pscr", "wob"))
                anv = ar.view("anrm")
                p.wait("sync", t_conv[f"wo{l}"])
                t_wo = p.op("sync", "dma_start",
                            out=wobv.rearrange("p (c o) -> p c o", o=1024),
                            in_=_chunked(dwb[f"wo{l}"]), inc=[("dW1", 16)])

                def headq(h, t0, n):
                    c, r = divmod(h, 2)
                    return qfv[64 * r:64 * r + 64, c * T + t0: c * T + t0 + n]

                def headk(h, t0, n):
                    c, r = divmod(h, 2)
                    return kfv[64 * r:64 * r + 64, c * T + t0: c * T + t0 + n]

                def heado(h, t0, n):
                    c, r = divmod(h, 2)
                    return ofv[64 * r:64 * r + 64, c * T + t0: c * T + t0 + n]

                prev_anrm = [None, None]

                def normalize_h(h, q0, ta):
                    hh = h % 2
                    abank = 4 + hh
                    r_rho = row(hh)
                    p.wait("scalar", ta)
                    p.op("scalar", "activation", r_rho,
                         psf[abank][64:65, 0:512], AF.Ln)
                    s_r = p.opt("scalar", "activation", r_rho, r_rho, AF.Exp,
                                scale=-1.0)
                    p.wait("vector", ta)
                    asl = anv[:, 512 * hh: 512 * hh + 512][0:64, :]
                    vr = p.opt("vector", "tensor_copy", asl,
                               psf[abank][0:64, 0:512])
                    bank_free[abank] = vr
                    bbank = 6
                    p.wait("tensor", s_r)
                    p.wait("tensor", bank_free[bbank])
                    tb = p.op("tensor", "matmul", psf[bbank][0:64, 0:512],
                              ones_row[:, 0:64], r_rho, start=True,
                              stop=True, inc=[("T", 1)])
                    bank_ready[bbank] = tb
                    p.wait("vector", tb)
                    vo = p.opt("vector", "tensor_mul", heado(h, q0, 512),
                               asl, psf[bbank][0:64, 0:512])
                    bank_free[bbank] = vo
                    prev_anrm[hh] = vo

                for w in range(4):
                    q0 = w * WIN

                    def emit_A(h, kc, span0, toff, ns, pchunk, scr, se, vm,
                               first_a, last, q0=q0):
                        hh = h % 2
                        abank = 4 + hh
                        kg = q0 - 512 + 128 * kc
                        va = vt_t(kg // 128).rearrange("p (g d) -> p g d",
                                                       d=65)[:, h, :]
                        p.wait("tensor", se)
                        if kc < 4:
                            clean0, cleann = 0, toff
                        else:
                            clean0, cleann = toff + 128, ns - toff - 128
                        if cleann > 0:
                            tmm(psf[abank][0:65, span0 + clean0:
                                           span0 + clean0 + cleann],
                                va, pchunk[:, clean0:clean0 + cleann],
                                abank, start=first_a, stop=False)
                            first_a = False
                        p.wait("tensor", vm)
                        tmm(psf[abank][0:65, tri_col(kc, q0):
                                       tri_col(kc, q0) + 128],
                            va, scr, abank, start=first_a, stop=last)
                        return False

                    pend = None
                    for h in range(8):
                        hh = h % 2
                        abank = 4 + hh
                        p.wait("tensor", bank_free[abank])
                        first_a = True
                        kcs = list(range(4, 8)) if w == 0 else list(range(8))
                        prev = None
                        for idx, kc in enumerate(kcs):
                            kg = q0 - 512 + 128 * kc
                            if kc < 4:
                                span0, span1, tri, mask = 0, 128 * (kc + 1), kc, mlo
                            else:
                                kr = kc - 4
                                span0, span1, tri, mask = 128 * kr, 512, kr, mup
                            ns = span1 - span0
                            sbank = 2 * hh + idx % 2
                            ts_ = tmm(psf[sbank][:, 0:ns], headk(h, kg, 128),
                                      headq(h, q0 + span0, ns), sbank,
                                      start=True, stop=True)
                            pchunk = pbv[:, 512 * (2 * hh + idx % 2):
                                         512 * (2 * hh + idx % 2) + ns]
                            p.wait("scalar", ts_)
                            se = p.opt("scalar", "activation", pchunk,
                                       psf[sbank][:, 0:ns], AF.Exp,
                                       bias=cloc_c(l), scale=8.0)
                            bank_free[sbank] = se
                            toff = tri * 128 - span0
                            scr = pscrv[:, 128 * (2 * hh + idx % 2):
                                        128 * (2 * hh + idx % 2) + 128]
                            eng_m = "gpsimd" if hh else "vector"
                            p.wait(eng_m, se)
                            vm = p.op(eng_m, "tensor_mul", scr,
                                      pchunk[:, toff:toff + 128], mask,
                                      inc=[(SEM_OF[eng_m], 1)])
                            if prev is not None:
                                first_a = emit_A(*prev, first_a, False)
                            prev = (h, kc, span0, toff, ns, pchunk, scr, se, vm)
                        first_a = emit_A(*prev, first_a, True)
                        ta = bank_ready[abank]
                        if pend is not None:
                            normalize_h(*pend)
                        pend = (h, q0, ta)
                    normalize_h(*pend)
                    p.wait("tensor", t_wo)
                    p.wait("tensor", p.tok("V"))
                    for ofc in range(8):
                        p.wait("tensor", bank_free[6])
                        for fc in range(4):
                            p.op("tensor", "matmul", psf[6],
                                 wobv[:, fc * 1024 + ofc * 128:
                                      fc * 1024 + ofc * 128 + 128],
                                 ofv[:, fc * T + q0: fc * T + q0 + 512],
                                 start=(fc == 0), stop=(fc == 3))
                        tw = tgroup_tok(6)
                        p.wait("vector", tw)
                        vv = p.opt("vector", "tensor_tensor", xyc(ofc, q0, 512),
                                   xyc(ofc, q0, 512), psf[6], ALU.add)
                        bank_free[6] = vv
                if not keep_bufs:
                    ar.free("of", "pb", "pscr", "wob", "anrm", "qf", "kf", "vt",
                            "sqp", "stat")

            if "p2" in parts:
                phase_barrier()
                w2b = ar.alloc("w2b", 22528, BF16)
                ffb = ar.alloc("ffb", 11264, BF16)
                lnh = ar.alloc("lnh", 8192, BF16)
                w1b = ar.alloc("w1b", 4096, BF16)
                ggs = ar.alloc("ggs", 2048, F32)
                t2s = ar.alloc("t2s", 2048, F32)
                lnhv, ffv, w1v, w2v = (ar.view(n) for n in
                                       ("lnh", "ffb", "w1b", "w2b"))
                ggv, t2v = ar.view("ggs"), ar.view("t2s")
                p.wait("sync", t_conv[f"w2_{l}"])
                p.op("sync", "dma_start",
                     out=w2v[:, 0:21 * 1024].rearrange("p (c o) -> p c o",
                                                       o=1024),
                     in_=_chunked(dwb[f"w2_{l}"][0:2688, :]),
                     inc=[("dW2", 16)])
                t_w2 = p.op("sync", "dma_start",
                            out=w2v[0:42, 21 * 1024:22 * 1024],
                            in_=dwb[f"w2_{l}"][2688:2730, :],
                            inc=[("dW2", 16)])
                p.wait("sync", t_conv[f"w1_{l}"])
                w1_slot_free = [None, None]
                shsv = sm[:, 280:288]

                def emit_save_ln(qq, t_guard):
                    h0 = qq * 512
                    sv_w = shsv[:, (qq % 2) * 4: (qq % 2) * 4 + 4]
                    sv_r = shsv[:, ((qq + 1) % 2) * 4: ((qq + 1) % 2) * 4 + 4]
                    p.wait("gpsimd", p.tok("V"))
                    for c in range(4, 8):
                        t_sv = p.op("gpsimd", "tensor_copy",
                                    sv_w[:, c - 4:c - 3], xyc(c, h0 + 511, 1))

                    def src(c, qi, h0=h0, sv_r=sv_r):
                        q0 = h0
                        if c < 4:
                            return [(xyc(c, q0, 512), 0, 512)]
                        if q0 == 0:
                            return [(xyc(c, 0, 511), 1, 511)]
                        return [(sv_r[:, c - 4:c - 3], 0, 1),
                                (xyc(c, q0, 511), 1, 511)]

                    def dst(c, qi, qq=qq):
                        return lnhv[:, 4096 * (qq % 2) + c * 512:
                                    4096 * (qq % 2) + c * 512 + 512]

                    p.wait("vector", t_guard)
                    p.wait("gpsimd", t_guard)
                    p.wait("vector", p.tok("dGX"))
                    p.wait("gpsimd", p.tok("dGX"))
                    ln_pass(src, dst, 1, lg=l)
                    return t_sv, p.tok("V"), p.tok("G")

                _, lnV, lnG = emit_save_ln(0, p.tok("T"))
                t_p2_entry = p.tok("T")
                t_pairs_prev = None
                for qq in range(4):
                    h0 = qq * 512
                    lnslot = 4096 * (qq % 2)
                    t_sv = None
                    for pair in range(NHP):
                        pslot = pair % 2
                        wblk = w1v[:, pslot * 2048: pslot * 2048 + 2048]
                        wblk3 = wblk.rearrange("p (c o) -> p c o", o=256)
                        pa0 = 128 * pair
                        pg0 = HID + 128 * pair
                        na = min(128, HID - pa0)
                        dsem = "dW0" if pslot == 0 else "dW1"
                        p.wait("sync", w1_slot_free[pslot])
                        p.op("sync", "dma_start", out=wblk3[:, :, 0:na],
                             in_=_chunked(dwb[f"w1_{l}"])[:, :, pa0:pa0 + na],
                             inc=[(dsem, 16)])
                        t_wp = p.op("sync", "dma_start",
                                    out=wblk3[:, :, 128:128 + na],
                                    in_=_chunked(dwb[f"w1_{l}"])
                                    [:, :, pg0:pg0 + na], inc=[(dsem, 16)])
                        if pair == 0:
                            p.wait("tensor", lnV)
                            p.wait("tensor", lnG)
                        if pair == 4 and qq < 3:
                            t_sv, lnV, lnG = emit_save_ln(
                                qq + 1,
                                t_pairs_prev if qq >= 1 else t_p2_entry)
                        p.wait("tensor", t_wp)
                        abk, gbk = pair % 2, 2 + pair % 2
                        p.wait("tensor", bank_free[abk])
                        p.wait("tensor", bank_free[gbk])
                        for c in range(NCH):
                            rhs = lnhv[:, lnslot + c * 512:
                                       lnslot + c * 512 + 512]
                            p.op("tensor", "matmul", psf[abk][0:na, 0:512],
                                 wblk3[:, c, 0:na], rhs, start=(c == 0),
                                 stop=(c == NCH - 1))
                            p.op("tensor", "matmul", psf[gbk][0:na, 0:512],
                                 wblk3[:, c, 128:128 + na], rhs, start=(c == 0),
                                 stop=(c == NCH - 1))
                        tg = p.inc_last("tensor", "T")
                        bank_ready[abk] = tg
                        bank_ready[gbk] = tg
                        w1_slot_free[pslot] = tg
                        p.wait("scalar", tg)
                        gsl = ggv[:, 512 * (pair % 2): 512 * (pair % 2) + 512]
                        sg = p.opt("scalar", "activation", gsl[0:na, :],
                                   psf[gbk][0:na, 0:512], AF.Gelu,
                                   bias=b1g_c(l, pair)[0:na, :])
                        bank_free[gbk] = sg
                        p.wait("vector", sg)
                        vf = p.opt("vector", "scalar_tensor_tensor",
                                   ffv[0:na, pair * 512: pair * 512 + 512],
                                   psf[abk][0:na, 0:512],
                                   b1a_c(l, pair)[0:na, :],
                                   gsl[0:na, :], ALU.add, ALU.mult)
                        bank_free[abk] = vf
                        if pair == NHP - 1:
                            t_pairs_prev = tg
                    t_ff = p.tok("V")
                    p.wait("tensor", t_ff)
                    p.wait("tensor", t_w2)
                    p.wait("vector", t_sv)
                    for ofc in range(NCH):
                        bank = 6 + ofc % 2
                        p.wait("tensor", bank_free[bank])
                        for pair in range(NHP):
                            npr = min(128, HID - 128 * pair)
                            p.op("tensor", "matmul", psf[bank][:, 0:512],
                                 w2v[0:npr, pair * 1024 + ofc * 128:
                                     pair * 1024 + ofc * 128 + 128],
                                 ffv[0:npr, pair * 512: pair * 512 + 512],
                                 start=(pair == 0), stop=(pair == NHP - 1))
                        tw2 = tgroup_tok(bank)
                        p.wait("vector", tw2)
                        yq = xyc(ofc, h0, 512)
                        tsl = t2v[:, 512 * (ofc % 2): 512 * (ofc % 2) + 512]
                        if not final_ffn:
                            p.op("vector", "tensor_scalar", tsl, psf[bank],
                                 b2_c(l, ofc), None, ALU.add)
                            vy = p.opt("vector", "tensor_tensor", yq, yq, tsl,
                                       ALU.add)
                        else:
                            p.op("vector", "tensor_scalar", tsl, psf[bank],
                                 b2_c(l, ofc), 0.5, ALU.add, ALU.mult)
                            vy = p.opt("vector", "scalar_tensor_tensor", yq,
                                       yq, 0.5, tsl, ALU.mult, ALU.add)
                            xr = dspill[:, ofc * T + h0: ofc * T + h0 + 512]
                            p.wait("gpsimd", vy)
                            p.op("gpsimd", "dma_start", out=yq, in_=xr,
                                 accum_op=ALU.add, inc=[("dGX", 16)])
                        bank_free[bank] = vy
                if not keep_bufs:
                    ar.free("lnh", "ffb", "w1b", "w2b", "ggs", "t2s")

        def attn_phase(keep_bufs=False):
            phase_barrier()
            p.wait("vector", p.tok("dX"))
            p.wait("vector", p.tok("dGX"))
            lnx = ar.alloc("lnx", 16384, BF16)
            lnxv = ar.view("lnx")

            def srcA(c, qi):
                return [(xyc(c, qi * 512, 512), 0, 512)]

            def dstA(c, qi):
                return lnxv[:, c * T + qi * 512: c * T + qi * 512 + 512]

            ln_pass(srcA, dstA, 4)
            p.wait("tensor", p.tok("G"))
            p.wait("tensor", p.tok("V"))
            t_lnx = p.tok("G")
            phase_barrier()

            wpb = ar.alloc("wpb", 8192, BF16)
            wpv = ar.view("wpb")
            p.wait("sync", t_conv["W_proj"])
            t_wpj = p.op("sync", "dma_start",
                         out=wpv.rearrange("p (c o) -> p c o", o=1024),
                         in_=_chunked(dwb["W_proj"]), inc=[("dW2", 16)])


            tmax = ar.alloc("tmax", 64, F32)
            tmaxv = ar.view("tmax")[0:16, 0:16]
            statv = ar.view("statq16")
            cst = ar.view("cstats")
            cbg = ar.view("cbig")

            for g in range(4):
                gqk = ar.alloc("gqk", 8192, BF16)
                gvt = ar.alloc("gvt", 4160, BF16)
                wat = ar.alloc("wat", 6144, BF16)
                qkst = ar.alloc("qkst", 1024, BF16)
                sqb = ar.alloc("sqb", 2048, F32)
                gqkv, gvtv, watv = ar.view("gqk"), ar.view("gvt"), ar.view("wat")
                qkstv, sqbv = ar.view("qkst"), ar.view("sqb")
                gv65 = gvtv.rearrange("p (i h d) -> p (i h) d", h=4, d=65)
                p.op("gpsimd", "memset", gv65[:, :, 64:65], 1.0)
                t_gv = p.op("gpsimd", "engine_nop", inc=[("G", 1)])
                p.wait("sync", t_conv["W_attn"])
                wa3 = watv.rearrange("p (c o) -> p c o", o=768)
                dwa = _chunked(dwb["W_attn"])
                for r in range(2):
                    p.op("sync", "dma_start",
                         out=wa3[:, :, 128 * r:128 * r + 128],
                         in_=dwa[:, :, 512 * r + 128 * g: 512 * r + 128 * g + 128],
                         inc=[("dW2", 16)])
                    p.op("sync", "dma_start",
                         out=wa3[:, :, 256 + 128 * r:256 + 128 * r + 128],
                         in_=dwa[:, :, 1024 + 512 * r + 128 * g:
                                 1024 + 512 * r + 128 * g + 128],
                         inc=[("dW2", 16)])
                t_wa = p.op("sync", "dma_start", out=wa3[:, :, 512:768],
                            in_=dwa[:, :, 2048 + 256 * g: 2048 + 256 * g + 256],
                            inc=[("dW2", 16)])
                p.wait("tensor", t_wa)
                p.wait("tensor", t_lnx)
                p.wait("tensor", t_btab)
                p.wait("tensor", t_gv)
                v_r = tq_ = None
                for i in range(NTILE):
                    p.wait("tensor", bank_free[0])
                    p.wait("tensor", bank_free[1])
                    for c in range(NCH):
                        lhs = lnxv[:, c * T + 128 * i: c * T + 128 * i + 128]
                        p.op("tensor", "matmul", psf[0], lhs, wa3[:, c, 0:512],
                             start=(c == 0), stop=(c == NCH - 1))
                        p.op("tensor", "matmul", psf[1][:, 0:256], lhs,
                             wa3[:, c, 512:768], start=(c == 0),
                             stop=(c == NCH - 1))
                    t_qkv = p.inc_last("tensor", "T")
                    bank_ready[0] = t_qkv
                    bank_ready[1] = t_qkv
                    p.wait("vector", t_qkv)
                    vt3 = gvtv[:, 260 * i:260 * i + 260].rearrange(
                        "p (h d) -> p h d", d=65)[:, :, 0:64]
                    ve_v = p.opt("vector", "tensor_copy", vt3,
                                 psf[1][:, 0:256].rearrange("p (h d) -> p h d",
                                                            d=64))
                    bank_free[1] = ve_v
                    p.wait("scalar", v_r)
                    p.wait("scalar", t_qkv)
                    s_q = p.opt("scalar", "activation", sqbv[:, 0:512], psf[0],
                                AF.Square)
                    p.wait("vector", s_q)
                    v_n = p.opt("vector", "tensor_reduce",
                                statv[:, 0:16].unsqueeze(2),
                                sqbv[:, 0:512].rearrange("p (h d) -> p h d",
                                                         d=32), AXX, ALU.add)
                    p.wait("tensor", v_n)
                    p.wait("tensor", bank_free[6])
                    tt_ = p.op("tensor", "transpose", psf[6][0:16, 0:128],
                               statv[:, 0:16], idnf, inc=[("T", 1)])
                    bank_ready[6] = tt_
                    p.wait("vector", tt_)
                    v_tm = p.opt("vector", "tensor_reduce",
                                 tmaxv[:, i:i + 1].unsqueeze(2),
                                 psf[6][0:16, 0:128].rearrange(
                                     "p (a b) -> p a b", a=1), AXX, ALU.max)
                    bank_free[6] = v_tm
                    qe = psf[0].rearrange("p (h j v) -> p h j v", j=16, v=2)
                    sa4 = sqbv[:, 0:512].rearrange("p (h j v) -> p h j v",
                                                   j=16, v=2)
                    sb4 = sqbv[:, 512:1024].rearrange("p (h j v) -> p h j v",
                                                      j=16, v=2)
                    ob = qkstv[:, 0:512].rearrange("p (h j v) -> p h j v",
                                                   j=16, v=2)
                    csb = btab_t(0, i).rearrange("p (j v) -> p j v", v=2) \
                        .unsqueeze(1).broadcast_to([128, 16, 16, 2])
                    ssb = btab_t(1, i).rearrange("p (j v) -> p j v", v=2) \
                        .unsqueeze(1).broadcast_to([128, 16, 16, 2])
                    p.wait("vector", tq_)
                    p.op("vector", "tensor_tensor", sa4, qe, csb, ALU.mult)
                    p.op("vector", "tensor_tensor", sb4[:, :, :, 0:1],
                         qe[:, :, :, 1:2], ssb[:, :, :, 0:1], ALU.mult)
                    p.op("vector", "tensor_tensor", sb4[:, :, :, 1:2],
                         qe[:, :, :, 0:1], ssb[:, :, :, 1:2], ALU.mult)
                    v_r = p.opt("vector", "tensor_add", ob, sa4, sb4)
                    bank_free[0] = v_r
                    p.wait("tensor", v_r)
                    p.wait("tensor", bank_free[3])
                    for j in range(4):
                        p.op("tensor", "transpose",
                             psb[3][:, 128 * j:128 * j + 128],
                             qkstv[:, 128 * j:128 * j + 128], idn)
                    tq_ = tgroup_tok(3)
                    p.wait("vector", tq_)
                    v_e = p.opt("vector", "tensor_copy",
                                gqkv.rearrange("p (c t) -> p c t", t=T)
                                [:, :, 128 * i:128 * i + 128],
                                psb[3][:, 0:512].rearrange(
                                    "p (j t) -> p j t", t=128))
                    bank_free[3] = v_e
                t_all = p.tok("V")
                p.wait("vector", t_all)
                p.op("vector", "tensor_reduce", cst[0:16, 0:4].unsqueeze(2),
                     tmaxv.rearrange("p (a b) -> p a b", b=4), AXX, ALU.max)
                v_km = p.opt("vector", "tensor_reduce",
                             cst[0:16, 4:5].unsqueeze(2),
                             tmaxv.rearrange("p (a b) -> p a b", a=1), AXX,
                             ALU.max)
                p.wait("scalar", v_km)
                b_h = p.op("scalar", "dma_start", out=drow[0:1, 0:8],
                           in_=cst[8:16, 4:5], inc=[("dX", 16)])
                p.wait("gpsimd", b_h)
                t_kb = p.op("gpsimd", "dma_start", out=cst[0:8, 5:6],
                            in_=drow[0:1, 0:8], inc=[("dGX", 16)])
                p.wait("vector", t_kb)
                p.op("vector", "tensor_scalar", cst[0:8, 0:4], cst[0:8, 0:4],
                     cst[0:8, 5:6], None, ALU.mult)
                v_c = p.opt("vector", "tensor_scalar", cst[0:8, 0:4],
                            cst[0:8, 0:4], 1e-30, None, ALU.max)
                p.wait("scalar", v_c)
                p.op("scalar", "activation", cst[0:8, 0:4], cst[0:8, 0:4],
                     AF.Ln)
                s_c = p.opt("scalar", "activation", cst[0:8, 0:4],
                            cst[0:8, 0:4], AF.Exp, scale=0.5)
                p.wait("vector", s_c)
                v_c2 = p.opt("vector", "tensor_scalar", cst[0:8, 0:4],
                             cst[0:8, 0:4], -1.0, None, ALU.mult)
                p.wait("scalar", v_c2)
                b1_ = p.op("scalar", "dma_start", out=drow[0:1, 0:32],
                           in_=cst[0:8, 0:4], inc=[("dX", 16)])
                p.wait("gpsimd", b1_)
                b2_ = p.op("gpsimd", "dma_start", out=row(2, 32),
                           in_=drow[0:1, 0:32], inc=[("dGX", 16)])
                p.wait("tensor", b2_)
                p.wait("tensor", bank_free[6])
                t_cb = p.op("tensor", "matmul", psf[6][:, 0:32], ones_row,
                            row(2, 32), start=True, stop=True, inc=[("T", 1)])
                bank_ready[6] = t_cb
                p.wait("vector", t_cb)
                v_cb = p.opt("vector", "tensor_copy", cbg[:, 0:32],
                             psf[6][:, 0:32])
                bank_free[6] = v_cb

                def ghead(is_k, hh, t0, n):
                    c = (2 if is_k else 0) + hh // 4
                    r = hh % 4
                    return gqkv[32 * r:32 * r + 32, c * T + t0: c * T + t0 + n]

                dfm = ar.alloc("dfm", 4096, BF16)
                pbuf2 = ar.alloc("pbuf2", 2048, BF16)
                pscr2 = ar.alloc("pscr2", 512, BF16)
                ast = ar.alloc("ast", 4096, F32)
                dstg = ar.alloc("dstg", 1088, F32)
                dnb = ar.alloc("dnb", 512, BF16)
                dfmv = ar.view("dfm")
                pb2v, ps2v = ar.view("pbuf2"), ar.view("pscr2")
                astv, dstgv, dnbv = ar.view("ast"), ar.view("dstg"), ar.view("dnb")
                def pair_tail(pr, qt, q0, slot, ve0, ve1):
                    p.wait("tensor", ve0)
                    p.wait("tensor", ve1)
                    for side in range(2):
                        bank = 4 + side
                        p.wait("tensor", bank_free[bank])
                        for j in range(4):
                            p.op("tensor", "transpose",
                                 psf[bank][:, 128 * j:128 * j + 65],
                                 astv[0:65, 1024 * slot + 512 * side + 128 * j:
                                      1024 * slot + 512 * side + 128 * j + 128],
                                 idnf[0:65, 0:65])
                        tgroup_tok(bank)
                    t_t2 = bank_ready[5]
                    p.wait("vector", bank_ready[4])
                    p.wait("vector", t_t2)
                    a1 = psf[4].rearrange("p (j u) -> p j u", u=128)
                    a2 = psf[5].rearrange("p (j u) -> p j u", u=128)
                    p.op("vector", "reciprocal",
                         dstgv[:, 512:516].unsqueeze(2), a1[:, :, 64:65])
                    p.op("vector", "reciprocal",
                         dstgv[:, 516:520].unsqueeze(2), a2[:, :, 64:65])
                    p.op("vector", "tensor_tensor",
                         dstgv[:, 0:256].rearrange("p (j d) -> p j d", d=64),
                         a1[:, :, 0:64],
                         dstgv[:, 512:516].unsqueeze(2)
                         .broadcast_to([128, 4, 64]), ALU.mult)
                    v_n2 = p.opt("vector", "tensor_tensor",
                                 dstgv[:, 256:512].rearrange(
                                     "p (j d) -> p j d", d=64),
                                 a2[:, :, 0:64],
                                 dstgv[:, 516:520].unsqueeze(2)
                                 .broadcast_to([128, 4, 64]), ALU.mult)
                    bank_free[4] = v_n2
                    bank_free[5] = v_n2
                    v_d = p.opt("vector", "scalar_tensor_tensor",
                                dstgv[:, 256:512], dstgv[:, 256:512],
                                lamcol, dstgv[:, 0:256], ALU.mult,
                                ALU.subtract)
                    p.wait("scalar", v_d)
                    s_sq = p.opt("scalar", "activation", dstgv[:, 0:256],
                                 dstgv[:, 256:512], AF.Square)
                    p.wait("vector", s_sq)
                    v_ss = p.opt("vector", "tensor_reduce",
                                 dstgv[:, 512:516].unsqueeze(2),
                                 dstgv[:, 0:256].rearrange(
                                     "p (j d) -> p j d", d=64), AXX, ALU.add)
                    p.wait("scalar", v_ss)
                    p.op("scalar", "activation", dstgv[:, 512:516],
                         dstgv[:, 512:516], AF.Ln, bias=eps_col,
                         scale=1.0 / 64.0)
                    s_rm = p.opt("scalar", "activation", dstgv[:, 512:516],
                                 dstgv[:, 512:516], AF.Exp, scale=-0.5)
                    p.wait("vector", s_rm)
                    p.op("vector", "tensor_scalar", dstgv[:, 512:516],
                         dstgv[:, 512:516], -(1.0 - LAMBDA_INIT), None,
                         ALU.mult)
                    v_dn = p.opt("vector", "tensor_tensor",
                                 dnbv.rearrange("p (j d) -> p j d", d=64)
                                 [:, :, :] if False else
                                 dnbv[:, 0:256].rearrange(
                                     "p (j d) -> p j d", d=64),
                                 dstgv[:, 256:512].rearrange(
                                     "p (j d) -> p j d", d=64),
                                 dstgv[:, 512:516].unsqueeze(2)
                                 .broadcast_to([128, 4, 64]), ALU.mult)
                    p.wait("tensor", v_dn)
                    p.wait("tensor", bank_free[6])
                    for j in range(4):
                        p.op("tensor", "transpose",
                             psb[6][0:64, 128 * j:128 * j + 128],
                             dnbv[:, 64 * j:64 * j + 64], idn)
                    t_tb = tgroup_tok(6)
                    p.wait("vector", t_tb)
                    cc, rr = divmod(pr, 2)
                    v_df = p.opt("vector", "tensor_copy",
                                 dfmv[64 * rr:64 * rr + 64,
                                      cc * T + q0: cc * T + q0 + 512]
                                 .rearrange("p (j t) -> p j t", t=128),
                                 psb[6][:, 0:512].rearrange(
                                     "p (j t) -> p j t", t=128)[0:64, :, :])
                    bank_free[6] = v_df

                p.wait("vector", v_cb)
                p.wait("scalar", v_cb)
                tails = []
                ve_pair = [None, None]
                for qt in range(4):
                    q0 = qt * 512
                    for pr in range(4):
                        for side in range(2):
                            hh = pr + 4 * side
                            abank = 2 + side
                            p.wait("tensor", bank_free[abank])
                            first_a = True
                            nkc = 4 * qt + 4
                            prev = None

                            def emit_bA(kc, pchunk, scr, se, vm, first_a, last,
                                        abank=abank, q0=q0, qt=qt, pr=pr):
                                diag = kc - 4 * qt
                                span0 = 128 * diag if diag >= 0 else 0
                                ns = 512 - span0
                                va = gvtv[:, 260 * kc + 65 * pr:
                                          260 * kc + 65 * pr + 65]
                                p.wait("tensor", se)
                                if diag < 0:
                                    tmm(psf[abank][0:65, 0:512], va, pchunk,
                                        abank, start=first_a, stop=last)
                                    return False
                                if ns > 128:
                                    tmm(psf[abank][0:65, span0 + 128:512],
                                        va, pchunk[:, 128:ns], abank,
                                        start=first_a, stop=False)
                                    first_a = False
                                p.wait("tensor", vm)
                                tmm(psf[abank][0:65, span0:span0 + 128],
                                    va, scr, abank, start=first_a, stop=last)
                                return False

                            for kc in range(nkc):
                                kg = 128 * kc
                                diag = kc - 4 * qt
                                span0 = 128 * diag if diag >= 0 else 0
                                ns = 512 - span0
                                sbank = kc % 2
                                ts_ = tmm(psf[sbank][:, 0:ns],
                                          ghead(True, hh, kg, 128),
                                          ghead(False, hh, q0 + span0, ns),
                                          sbank, start=True, stop=True,
                                          tile_position=(32 * (hh % 4), 0))
                                pchunk = pb2v[:, 512 * (kc % 2):
                                              512 * (kc % 2) + ns]
                                p.wait("scalar", ts_)
                                se = p.opt("scalar", "activation", pchunk,
                                           psf[sbank][:, 0:ns], AF.Exp,
                                           bias=cbg[:, 4 * hh + qt:
                                                    4 * hh + qt + 1],
                                           scale=1.0 / math.sqrt(32.0))
                                bank_free[sbank] = se
                                vm = None
                                scr = None
                                if diag >= 0:
                                    scr = ps2v[:, 128 * (kc % 2):
                                               128 * (kc % 2) + 128]
                                    eng_m = "gpsimd" if side else "vector"
                                    p.wait(eng_m, se)
                                    vm = p.op(eng_m, "tensor_mul", scr,
                                              pchunk[:, 0:128], mup,
                                              inc=[(SEM_OF[eng_m], 1)])
                                if prev is not None:
                                    first_a = emit_bA(*prev, first_a, False)
                                prev = (kc, pchunk, scr, se, vm)
                            first_a = emit_bA(*prev, first_a, True)
                            ta = bank_ready[abank]
                            p.wait("vector", ta)
                            ve = p.opt("vector", "tensor_copy",
                                       astv[0:65, 1024 * (pr % 2) + 512 * side:
                                            1024 * (pr % 2) + 512 * side + 512],
                                       psf[abank][0:65, 0:512])
                            bank_free[abank] = ve
                            ve_pair[side] = ve
                        tails.append((pr, qt, q0, pr % 2,
                                      ve_pair[0], ve_pair[1]))
                        if len(tails) > 1:
                            pair_tail(*tails.pop(0))
                for t_ in tails:
                    pair_tail(*t_)
                tails = []
                t_dfm = p.tok("V")
                p.wait("tensor", t_dfm)
                p.wait("tensor", t_wpj)
                for ofc in range(8):
                    for tq in range(4):
                        bank = tq % 2
                        p.wait("tensor", bank_free[bank])
                        for cc in range(2):
                            p.op("tensor", "matmul", psf[bank],
                                 wpv[:, (2 * g + cc) * 1024 + 128 * ofc:
                                     (2 * g + cc) * 1024 + 128 * ofc + 128],
                                 dfmv[:, cc * T + 512 * tq:
                                      cc * T + 512 * tq + 512],
                                 start=(cc == 0), stop=(cc == 1))
                        twp = tgroup_tok(bank)
                        p.wait("vector", twp)
                        v_x = p.opt("vector", "tensor_tensor",
                                    xyc(ofc, 512 * tq, 512),
                                    xyc(ofc, 512 * tq, 512), psf[bank], ALU.add)
                        bank_free[bank] = v_x
                ar.free("gqk", "gvt", "wat", "qkst", "sqb", "dfm", "pbuf2",
                        "pscr2", "ast", "dstg", "dnb")
                phase_barrier()
            if not keep_bufs:
                ar.free("lnx", "wpb", "tmax")

        def final_store():
            phase_barrier()
            p.wait("tensor", p.tok("dX"))
            p.wait("tensor", p.tok("dGX"))
            stg = ar.alloc("outst", 2048, F32)
            stv = ar.view("outst")
            t_store = None
            for i in range(NTILE):
                for half in range(2):
                    b = half
                    p.wait("tensor", bank_free[b])
                    for j in range(4):
                        c = half * 4 + j
                        p.op("tensor", "transpose",
                             psf[b][:, 128 * j:128 * j + 128],
                             xyc(c, 128 * i, 128), idnf)
                    tf = tgroup_tok(b)
                    p.wait("vector", tf)
                    p.wait("vector", t_store)
                    vv = p.opt("vector", "tensor_copy",
                               stv[:, half * 512: half * 512 + 512], psf[b])
                    bank_free[b] = vv
                p.wait("sync", p.tok("V"))
                t_store = p.op("sync", "dma_start",
                               out=dout[128 * i:128 * i + 128, :], in_=stv,
                               inc=[("dO", 16)])
            p.wait("sync", t_store)

        p.wait("vector", t_xin)
        if upto == "tiny":
            pass
        elif upto == "ln0":
            ffn_entry_ln()
        elif upto == "p1":
            ffn_entry_ln()
            ffn_layer(0, parts=("p1",), keep_bufs=True)
        elif upto == "p1b":
            ffn_entry_ln()
            ffn_layer(0, parts=("p1", "p1b"), keep_bufs=True)
        elif upto == "l0":
            ffn_entry_ln()
            ffn_layer(0)
        elif upto == "ffn1":
            ffn_entry_ln()
            ffn_layer(0)
            ffn_layer(1, final_ffn=True)
        elif upto == "attn":
            ffn_entry_ln()
            ffn_layer(0)
            ffn_layer(1, final_ffn=True)
            attn_phase()
        else:
            ffn_entry_ln()
            ffn_layer(0)
            ffn_layer(1, final_ffn=True)
            attn_phase()
            ffn_entry_ln()
            ffn_layer(0)
            ffn_layer(1, final_ffn=True)
        final_store()

        if dumps_req:
            p.barrier_all()
            p.wait("sync", p.tok("dX"))
            p.wait("sync", p.tok("dGX"))
            tok = None
            for nm in dumps_req:
                if nm == "XY":
                    ap_, shp, dt = xy[:, :], [128, NCH * T], F32
                elif nm == "SM":
                    ap_, shp, dt = sm[:, :], [128, 1800], F32
                else:
                    ap_ = ar.view(nm)
                    dt = ar.regions[nm][2]
                    shp = list(ap_.shape)
                dd = nc.dram_tensor("d_" + nm, shp, dt, kind="ExternalOutput").ap()
                tok = p.op("sync", "dma_start", out=dd, in_=ap_, inc=[("dO", 16)])
            p.wait("sync", tok)

        block = es.enter_context(nc.Block())
        p.emit(block)
    return nc


def _prep_weights(inputs):
    f = np.float32
    wq = np.asarray(inputs["L0_ffn_wqkv"], f)
    wo = np.asarray(inputs["L0_ffn_wo"], f)
    w1 = np.asarray(inputs["L0_ffn_w1"], f)
    w2 = np.asarray(inputs["L0_ffn_w2"], f)
    lam = np.zeros((1, 130), f)
    lam[0, 0:32] = np.asarray(inputs["L0_attn_lambda_q1"], f)
    lam[0, 32:64] = np.asarray(inputs["L0_attn_lambda_k1"], f)
    lam[0, 64:96] = np.asarray(inputs["L0_attn_lambda_q2"], f)
    lam[0, 96:128] = np.asarray(inputs["L0_attn_lambda_k2"], f)
    qs = np.zeros((1, 128), f)
    qs[0, 0:64] = np.asarray(inputs["L0_ffn_q_scale"], f)[0]
    qs[0, 64:128] = np.asarray(inputs["L0_ffn_q_scale"], f)[1]
    ks = np.zeros((1, 128), f)
    ks[0, 0:64] = np.asarray(inputs["L0_ffn_k_scale"], f)[0]
    ks[0, 64:128] = np.asarray(inputs["L0_ffn_k_scale"], f)[1]

    def fmcols(vec, n):
        return np.asarray(vec, f).reshape(n, 128).T.copy()

    lng = np.concatenate([fmcols(inputs["L0_ffn_ln_g"][l], 8) for l in range(2)],
                         axis=1)
    lnb = np.concatenate([fmcols(inputs["L0_ffn_ln_b"][l], 8) for l in range(2)],
                         axis=1)
    b2 = np.concatenate([fmcols(inputs["L0_ffn_b2"][l], 8) for l in range(2)],
                        axis=1)
    b1p = np.zeros((128, 88), f)
    for l in range(2):
        b1l = np.asarray(inputs["L0_ffn_b1"][l], f)
        va = np.zeros(22 * 128, f)
        va[:HID] = b1l[:HID]
        vg = np.zeros(22 * 128, f)
        vg[:HID] = b1l[HID:]
        b1p[:, 44 * l: 44 * l + 22] = va.reshape(22, 128).T
        b1p[:, 44 * l + 22: 44 * l + 44] = vg.reshape(22, 128).T
    return {
        "W_attn": np.ascontiguousarray(inputs["L0_attn_W_attn"], f),
        "W_proj": np.ascontiguousarray(inputs["L0_attn_W_proj"], f),
        "lam": lam, "qs": qs, "ks": ks,
        "wqkv0": np.ascontiguousarray(wq[0]), "wqkv1": np.ascontiguousarray(wq[1]),
        "wo0": np.ascontiguousarray(wo[0]), "wo1": np.ascontiguousarray(wo[1]),
        "w1_0": np.ascontiguousarray(w1[0]), "w1_1": np.ascontiguousarray(w1[1]),
        "w2_0": np.ascontiguousarray(w2[0]), "w2_1": np.ascontiguousarray(w2[1]),
        "lng": lng, "lnb": lnb, "b1": b1p, "b2": b2,
    }


def _prep_x(x_b):
    return np.ascontiguousarray(np.asarray(x_b, np.float32).T.reshape(NCH, 128, T))


def kernel(**inputs):
    x = np.asarray(inputs["L0_attn_x"], np.float32)
    w = _prep_weights(inputs)
    nc = build_program("all")
    in_maps = []
    for b in range(8):
        m = dict(w)
        m["x"] = _prep_x(x[b])
        in_maps.append(m)
    res = run_bass_kernel_spmd(nc, in_maps, core_ids=list(range(8)))
    out = np.stack([res.results[b]["out"] for b in range(8)], axis=0)
    return out.astype(np.float32)


# version 10
# speedup vs baseline: 1.4694x; 1.0036x over previous
import math
import contextlib
import numpy as np
import ml_dtypes
import concourse.bass as bass
import concourse.mybir as mybir
from concourse.bass_utils import run_bass_kernel_spmd

F32 = mybir.dt.float32
BF16 = mybir.dt.bfloat16
I32 = mybir.dt.int32
AF = mybir.ActivationFunctionType
ALU = mybir.AluOpType
AXX = mybir.AxisListType.X

T = 2048
D = 1024
NCH = 8
NTILE = 16
WIN = 512
HID = 2730
NHP = 22
LAMBDA_INIT = 0.2
C_LOC = 8.0 * math.exp((127.0 / 256.0) * (-math.log(25.6 / 89.6)))

ENGINES = ("sync", "gpsimd", "tensor", "vector", "scalar")
SEM_OF = {"vector": "V", "scalar": "S", "gpsimd": "G", "tensor": "T"}


class Prog:
    def __init__(self, nc, es):
        self.nc = nc
        self.stream = {e: [] for e in ENGINES}
        self.sems = {}
        self.count = {}
        self.last_wait = {e: {} for e in ENGINES}
        self._es = es

    def sem(self, name):
        if name not in self.sems:
            self.sems[name] = self._es.enter_context(self.nc.semaphore(name))
            self.count[name] = 0
        return name

    def wait(self, eng, tok):
        if tok is None:
            return
        s, v = tok
        if v <= 0 or self.last_wait[eng].get(s, 0) >= v:
            return
        self.last_wait[eng][s] = v
        self.stream[eng].append(("__wait__", (self.sems[s], v), {}, []))

    ASYNC_OPS = {"memset", "tensor_reduce", "reduce_max", "reduce_sum",
                 "reciprocal", "iota", "affine_select", "memzero"}
    ASYNC_OPS = ASYNC_OPS | {"tensor_tensor", "tensor_scalar", "tensor_copy",
                             "scalar_tensor_tensor", "tensor_add",
                             "tensor_sub", "tensor_mul", "tensor_max",
                             "activation"}

    def op(self, eng, fname, *args, inc=(), **kwargs):
        if eng == "gpsimd" and fname not in ("dma_start", "__wait__"):
            self.wait("gpsimd", self.tok("G"))
            if not any(s == "G" for s, _ in inc):
                inc = list(inc) + [("G", 1)]
        selfwait = (eng in ("vector", "scalar") and fname in self.ASYNC_OPS)
        if selfwait and not any(s == SEM_OF[eng] for s, _ in inc):
            inc = list(inc) + [(SEM_OF[eng], 1)]
        toks = []
        incs = []
        for s, amt in inc:
            self.count[s] += amt
            toks.append((s, self.count[s]))
            incs.append((self.sems[s], amt))
        self.stream[eng].append((fname, args, kwargs, incs))
        if selfwait:
            self.wait(eng, (SEM_OF[eng], self.count[SEM_OF[eng]]))
        return toks[0] if toks else None

    def opt(self, eng, fname, *args, **kwargs):
        return self.op(eng, fname, *args, inc=[(SEM_OF[eng], 1)], **kwargs)

    def tok(self, semname):
        return (semname, self.count[semname])

    def inc_last(self, eng, semname, amt=1):
        fname, args, kwargs, incs = self.stream[eng][-1]
        assert fname != "__wait__", "inc_last on a wait"
        self.count[semname] += amt
        incs.append((self.sems[semname], amt))
        return (semname, self.count[semname])

    def barrier_all(self):
        for s in ("T", "V", "S", "G"):
            self.wait("sync", self.tok(s))

    def emit(self, block):
        for eng_name in ENGINES:
            ops = self.stream[eng_name]

            def body(eng, ops=ops):
                for fname, args, kwargs, incs in ops:
                    if fname == "__wait__":
                        eng.wait_ge(*args)
                        continue
                    inst = getattr(eng, fname)(*args, **kwargs)
                    for sh, amt in incs:
                        inst.then_inc(sh, amt)

            getattr(block, eng_name)(body)


class Arena:

    def __init__(self, tensor, cols):
        self.t = tensor
        self.cols = cols
        self.regions = {}
        self.cursor = 0
        self.high = 0

    def alloc(self, name, cols, dtype=BF16, align=32):
        assert name not in self.regions, name
        start = (self.cursor + align - 1) // align * align
        assert start + cols <= self.cols, \
            f"arena overflow: {name} {start}+{cols}>{self.cols}"
        self.cursor = start + cols
        self.high = max(self.high, self.cursor)
        self.regions[name] = (start, cols, dtype)
        return self.view(name)

    def view(self, name):
        start, nb, dtype = self.regions[name]
        ap = self.t.ap()
        if dtype == F32:
            return ap.bitcast(F32)[:, start // 2: (start + nb) // 2]
        return ap[:, start: start + nb]

    def free(self, *names):
        for name in names:
            self.regions.pop(name, None)
        self.cursor = max((s + n for s, n, _ in self.regions.values()), default=0)


def _chunked(dram_ap, p=128):
    return dram_ap.rearrange("(c q) o -> c q o", q=p).transpose([1, 0, 2])


def build_program(upto="all", dumps_req=()):
    nc = bass.Bass("TRN2", target_bir_lowering=False)
    es = contextlib.ExitStack()

    dx = nc.dram_tensor("x", [NCH, 128, T], F32, kind="ExternalInput").ap()
    wshapes = {
        "W_attn": [D, 3 * D], "W_proj": [D, D], "lam": [1, 130],
        "wqkv0": [D, 1536], "wqkv1": [D, 1536],
        "wo0": [512, D], "wo1": [512, D],
        "w1_0": [D, 2 * HID], "w1_1": [D, 2 * HID],
        "w2_0": [HID, D], "w2_1": [HID, D],
        "qs": [1, 128], "ks": [1, 128],
        "lng": [128, 16], "lnb": [128, 16], "b1": [128, 88], "b2": [128, 16],
    }
    dw = {nm: nc.dram_tensor(nm, shp, F32, kind="ExternalInput").ap()
          for nm, shp in wshapes.items()}
    dout = nc.dram_tensor("out", [T, D], F32, kind="ExternalOutput").ap()
    dwb = {nm: nc.dram_tensor("b_" + nm, wshapes[nm], BF16, kind="Internal").ap()
           for nm in ("wqkv0", "wqkv1", "W_attn", "W_proj", "wo0", "wo1",
                      "w1_0", "w1_1", "w2_0", "w2_1")}
    dspill = nc.dram_tensor("xspill", [128, NCH * T], F32, kind="Internal").ap()
    drow = nc.dram_tensor("rowsc", [1, 64], F32, kind="Internal").ap()

    with es:
        p = Prog(nc, es)
        XY = es.enter_context(nc.sbuf_tensor("XY", [128, NCH * T], F32))
        AR = es.enter_context(nc.sbuf_tensor("AR", [128, 67000], BF16))
        SM = es.enter_context(nc.sbuf_tensor("SM", [128, 2900], F32))
        ar = Arena(AR, 67000)
        psum = [es.enter_context(nc.psum_tensor(f"ps{i}", [128, 512], F32))
                for i in range(8)]
        psf = [x.ap() for x in psum]
        psb = [x.ap().bitcast(BF16) for x in psum]

        for s in ("dS", "dG", "dC", "dX", "dGX", "T", "V", "S", "G", "dO",
                  "dW0", "dW1", "dW2"):
            p.sem(s)
        for i in range(10):
            p.sem(f"cv{i}")

        xy = XY.ap()
        sm = SM.ap()

        def xyc(c, t0=0, n=T):
            return xy[:, c * T + t0: c * T + t0 + n]

        ones_col = sm[:, 0:1]
        lng_a = sm[:, 4:20]
        lnb_a = sm[:, 20:36]
        b2_a = sm[:, 36:52]
        b1_a = sm[:, 52:140]
        cloc_a = sm[:, 140:142]
        lamcol = sm[:, 142:143]
        eps_col = sm[:, 143:144]
        eps_p0 = sm[0:1, 144:145]

        def lng_c(l, c):
            return lng_a[:, l * 8 + c: l * 8 + c + 1]

        def lnb_c(l, c):
            return lnb_a[:, l * 8 + c: l * 8 + c + 1]

        def b2_c(l, c):
            return b2_a[:, l * 8 + c: l * 8 + c + 1]

        def b1a_c(l, j):
            return b1_a[:, l * 44 + j: l * 44 + j + 1]

        def b1g_c(l, j):
            return b1_a[:, l * 44 + 22 + j: l * 44 + 22 + j + 1]

        def cloc_c(l):
            return cloc_a[:, l:l + 1]

        def row(i, n=512):
            assert i <= 2 and 1300 + i * 512 + n <= 2900
            return sm[0:1, 1300 + i * 512: 1300 + i * 512 + n]

        def r1(i, n=1):
            return sm[0:1, 990 + i: 990 + i + n]

        ones_row = sm[0:1, 150:278]
        qsrow = sm[0:1, 600:728]
        ksrow = sm[0:1, 728:856]
        lamrow = sm[0:1, 856:986]

        t_xin = p.op("sync", "dma_start",
                     out=xy.rearrange("p (c t) -> p c t", t=T),
                     in_=dx.transpose([1, 0, 2]), inc=[("dS", 16)])

        p.op("gpsimd", "memset", ones_col, 1.0)
        p.op("gpsimd", "memset", sm[0:1, 150:278], 1.0)
        p.op("gpsimd", "memset", eps_col, 1e-5)
        p.op("gpsimd", "memset", eps_p0, 1e-5)
        idn = ar.alloc("ident", 128, BF16)
        p.op("gpsimd", "memset", idn, 0.0)
        p.op("gpsimd", "affine_select", out=idn, in_=idn, compare_op=ALU.not_equal,
             fill=1.0, base=0, pattern=[[-1, 128]], channel_multiplier=1)
        idnf = ar.alloc("identf", 256, F32)
        p.op("gpsimd", "memset", idnf, 0.0)
        p.op("gpsimd", "affine_select", out=idnf, in_=idnf, compare_op=ALU.not_equal,
             fill=1.0, base=0, pattern=[[-1, 128]], channel_multiplier=1)
        onesln_bf = ar.alloc("onesln", 32, BF16)[:, 0:1]
        p.op("gpsimd", "memset", onesln_bf, 1.0 / 1024.0)
        mlo = ar.alloc("mask_lo", 128, BF16)
        p.op("gpsimd", "memset", mlo, 1.0)
        p.op("gpsimd", "affine_select", out=mlo, in_=mlo, compare_op=ALU.is_ge,
             fill=0.0, base=0, pattern=[[-1, 128]], channel_multiplier=1)
        mup = ar.alloc("mask_up", 128, BF16)
        p.op("gpsimd", "memset", mup, 1.0)
        p.op("gpsimd", "affine_select", out=mup, in_=mup, compare_op=ALU.is_ge,
             fill=0.0, base=0, pattern=[[1, 128]], channel_multiplier=-1)
        statq16 = ar.alloc("statq16", 64, F32)
        cstats = ar.alloc("cstats", 16, F32)
        cbig = ar.alloc("cbig", 64, F32)
        tabsP = ar.alloc("tabsP", 3072, BF16)
        btabP = ar.alloc("btabP", 1024, BF16)
        t_const = p.op("gpsimd", "engine_nop", inc=[("G", 1)])

        t_small = None
        for nm, dst in (("lng", lng_a), ("lnb", lnb_a), ("b2", b2_a),
                        ("b1", b1_a)):
            t_small = p.op("sync", "dma_start", out=dst, in_=dw[nm],
                           inc=[("dS", 16)])
        t_small = p.op("sync", "dma_start", out=qsrow, in_=dw["qs"], inc=[("dS", 16)])
        t_small = p.op("sync", "dma_start", out=ksrow, in_=dw["ks"], inc=[("dS", 16)])
        t_small = p.op("sync", "dma_start", out=lamrow, in_=dw["lam"], inc=[("dS", 16)])

        import os as _os
        conv_order = ["wqkv0", "wo0", "w1_0", "w2_0", "wqkv1", "wo1", "w1_1",
                      "w2_1", "W_attn", "W_proj"]
        t_conv = {}
        for ci, nm in enumerate(conv_order):
            t_conv[nm] = p.op("gpsimd", "dma_start", out=dwb[nm], in_=dw[nm],
                              inc=[(f"cv{ci}", 16)])

        p.wait("vector", t_small)
        for l in range(2):
            p.op("vector", "tensor_reduce", r1(0 + 4 * l).unsqueeze(2),
                 qsrow[:, 64 * l:64 * l + 64].rearrange("p (a b) -> p a b", a=1),
                 AXX, ALU.max, apply_absolute_value=True)
            p.op("vector", "tensor_reduce", r1(1 + 4 * l).unsqueeze(2),
                 ksrow[:, 64 * l:64 * l + 64].rearrange("p (a b) -> p a b", a=1),
                 AXX, ALU.max, apply_absolute_value=True)
            p.op("vector", "tensor_tensor", r1(2 + 4 * l), r1(0 + 4 * l),
                 r1(1 + 4 * l), ALU.mult)
            p.opt("vector", "tensor_scalar", r1(3 + 4 * l), r1(2 + 4 * l),
                  -C_LOC, None, ALU.mult)
        p.op("vector", "tensor_tensor", r1(8, 32), lamrow[:, 0:32],
             lamrow[:, 32:64], ALU.mult)
        p.op("vector", "tensor_tensor", r1(40, 32), lamrow[:, 64:96],
             lamrow[:, 96:128], ALU.mult)
        p.op("vector", "tensor_reduce", r1(72).unsqueeze(2),
             r1(8, 32).rearrange("p (a b) -> p a b", a=1), AXX, ALU.add)
        tok = p.opt("vector", "tensor_reduce", r1(73).unsqueeze(2),
                    r1(40, 32).rearrange("p (a b) -> p a b", a=1), AXX, ALU.add)
        p.wait("scalar", tok)
        p.op("scalar", "activation", r1(74), r1(72), AF.Exp)
        tok = p.opt("scalar", "activation", r1(75), r1(73), AF.Exp)
        p.wait("vector", tok)
        p.op("vector", "tensor_sub", r1(76), r1(74), r1(75))
        t_lamv = p.opt("vector", "tensor_scalar", r1(77), r1(76), LAMBDA_INIT,
                       None, ALU.add)
        p.wait("tensor", t_lamv)
        p.wait("tensor", t_const)
        p.op("tensor", "matmul", psf[7][:, 0:1], ones_row, r1(77), start=True,
             stop=True)
        p.op("tensor", "matmul", psf[7][:, 1:2], ones_row, r1(3), start=True,
             stop=True)
        tk = p.op("tensor", "matmul", psf[7][:, 2:3], ones_row, r1(7),
                  start=True, stop=True, inc=[("T", 1)])
        p.wait("vector", tk)
        p.op("vector", "tensor_copy", lamcol, psf[7][:, 0:1])
        p.op("vector", "tensor_copy", cloc_c(0), psf[7][:, 1:2])
        t_cloc = p.opt("vector", "tensor_copy", cloc_c(1), psf[7][:, 2:3])

        def tab(tb, i):
            return tabsP[:, tb * 512 + 32 * i: tb * 512 + 32 * i + 32]

        def btab_t(tb, i):
            return btabP[:, tb * 512 + 32 * i: tb * 512 + 32 * i + 32]

        tbsc = ar.alloc("tbsc", 10240, F32)
        tv = tbsc
        i32v = tv.bitcast(I32)
        jint = i32v[:, 0:32]
        jint16 = i32v[:, 112:128]
        Ti = i32v[:, 96:112]
        framp = tv[:, 0:32]
        theta2p = tv[:, 32:64]
        lnbase = tv[:, 64:96]
        Tf = tv[:, 160:176]
        framp16 = tv[:, 112:128]
        theta16 = tv[:, 128:144]
        pw = tv[:, 176:192]
        wA = tv[:, 512:1024]
        wB = tv[:, 1024:1536]
        wC = tv[:, 1536:2048]
        wD = tv[:, 2048:2560]
        wS = tv[:, 2560:3072]
        wI = tv[:, 3072:3584]
        wint = i32v[:, 3584:4096]
        wG = tv[:, 4096:4608]

        def bc3(row, nj):
            return row.unsqueeze(1).broadcast_to([128, 16, nj])

        def tile3(w, nj=32):
            return w[:, 0:16 * nj].rearrange("p (i j) -> p i j", j=nj)

        p.wait("gpsimd", t_const)
        p.op("gpsimd", "iota", jint, [[1, 32]], base=0, channel_multiplier=0)
        p.op("gpsimd", "iota", jint16, [[1, 16]], base=0, channel_multiplier=0)
        t_io = p.op("gpsimd", "iota", Ti, [[128, 16]], base=0,
                    channel_multiplier=1)
        p.wait("vector", t_io)
        p.op("vector", "tensor_copy", framp, jint)
        p.op("vector", "tensor_copy", framp16, jint16)
        tok = p.opt("vector", "tensor_copy", Tf, Ti)
        p.wait("scalar", tok)
        p.op("scalar", "activation", theta2p, framp, AF.Exp,
             scale=-math.log(1e4) / 32.0)
        p.op("scalar", "activation", theta2p, theta2p, AF.Identity,
             scale=1.0 / (2 * math.pi))
        t_th16 = p.opt("scalar", "activation", theta16, framp16, AF.Exp,
                       scale=-math.log(1e4) / 16.0)
        v = p.opt("vector", "tensor_scalar", lnbase, framp, 2.0 / 89.6,
                  25.6 / 89.6, ALU.mult, ALU.add)
        p.wait("vector", t_th16)
        p.op("vector", "tensor_scalar", theta16, theta16,
             1.0 / (2 * math.pi), None, ALU.mult)
        p.wait("scalar", v)
        t_lb = p.opt("scalar", "activation", lnbase, lnbase, AF.Ln)
        p.wait("vector", t_lb)

        last_sin = [None]

        def sincos(theta_row, nj, sin_out, cos_out):
            n = 16 * nj
            p.op("vector", "tensor_tensor", tile3(wA, nj), bc3(theta_row, nj),
                 Tf.unsqueeze(2).broadcast_to([128, 16, nj]), ALU.mult)
            for (dst, shift) in ((sin_out, None), (cos_out, 0.25)):
                src_ = wA[:, 0:n]
                if shift is not None:
                    p.op("vector", "tensor_scalar", wB[:, 0:n], src_, shift,
                         None, ALU.add)
                    src_ = wB[:, 0:n]
                p.wait("vector", last_sin[0])
                p.op("vector", "tensor_copy", wint[:, 0:n], src_)
                p.op("vector", "tensor_copy", wC[:, 0:n], wint[:, 0:n])
                p.op("vector", "tensor_sub", wC[:, 0:n], src_, wC[:, 0:n])
                p.op("vector", "tensor_scalar", wG[:, 0:n], wC[:, 0:n], 0.5,
                     None, ALU.is_gt)
                vv = p.opt("vector", "tensor_sub", wC[:, 0:n], wC[:, 0:n],
                           wG[:, 0:n])
                p.wait("scalar", vv)
                sc = p.opt("scalar", "activation", dst, wC[:, 0:n], AF.Sin,
                           scale=2 * math.pi)
                last_sin[0] = sc
            return sc

        sc1 = sincos(theta2p, 32, wD[:, 0:512], wS[:, 0:512])
        p.op("vector", "tensor_scalar", pw, Tf, 1.0 / 256.0, -4.0, ALU.mult,
             ALU.add)
        v = p.opt("vector", "tensor_tensor", tile3(wI), bc3(lnbase, 32),
                  pw.unsqueeze(2).broadcast_to([128, 16, 32]), ALU.mult)
        p.wait("scalar", v)
        p.wait("scalar", sc1)
        s_S = p.opt("scalar", "activation", wA[:, 0:512], wI[:, 0:512], AF.Exp)
        s_I = p.opt("scalar", "activation", wB[:, 0:512], wI[:, 0:512], AF.Exp,
                    scale=-1.0)
        p.wait("vector", s_S)
        p.wait("vector", s_I)
        p.op("vector", "tensor_mul", tabsP[:, 0:512], wS[:, 0:512],
             wA[:, 0:512])
        p.op("vector", "tensor_mul", tabsP[:, 1024:1536], wD[:, 0:512],
             wA[:, 0:512])
        p.op("vector", "scalar_tensor_tensor", tabsP[:, 512:1024],
             wD[:, 0:512], -1.0, wA[:, 0:512], ALU.mult, ALU.mult)
        p.op("vector", "tensor_mul", tabsP[:, 1536:2048], wS[:, 0:512],
             wB[:, 0:512])
        p.op("vector", "tensor_mul", tabsP[:, 2560:3072], wD[:, 0:512],
             wB[:, 0:512])
        t_tabs_done = p.opt("vector", "scalar_tensor_tensor",
                            tabsP[:, 2048:2560], wD[:, 0:512], -1.0,
                            wB[:, 0:512], ALU.mult, ALU.mult)
        sc2 = sincos(theta16, 16, wD[:, 0:256], wS[:, 0:256])
        p.wait("vector", sc2)
        cse = tabsP.bitcast(BF16) if False else None
        bt0 = btabP[:, 0:512].rearrange("p (i j v) -> p i j v", j=16, v=2)
        bt1 = btabP[:, 512:1024].rearrange("p (i j v) -> p i j v", j=16, v=2)
        w3c = wS[:, 0:256].rearrange("p (i j) -> p i j", j=16).unsqueeze(3)
        w3s = wD[:, 0:256].rearrange("p (i j) -> p i j", j=16).unsqueeze(3)
        p.op("vector", "tensor_copy", bt0[:, :, :, 0:1], w3c)
        p.op("vector", "tensor_copy", bt0[:, :, :, 1:2], w3c)
        p.op("vector", "scalar_tensor_tensor", bt1[:, :, :, 0:1], w3s, -1.0,
             w3s, ALU.mult, ALU.bypass)
        t_tabs_done = p.opt("vector", "tensor_copy", bt1[:, :, :, 1:2], w3s)
        t_tabs = t_tabs_done
        t_btab = t_tabs_done
        ar.free("tbsc")

        bank_free = [None] * 8
        bank_free[7] = t_cloc
        bank_ready = [None] * 8

        def phase_barrier():
            for e in ("vector", "scalar", "gpsimd", "sync"):
                for sname in ("T", "V", "S", "G"):
                    if SEM_OF.get(e) != sname:
                        p.wait(e, p.tok(sname))

        def tgroup_tok(bank):
            tok = p.inc_last("tensor", "T")
            bank_ready[bank] = tok
            return tok

        def tri_col(kc, q0):
            return (kc if kc < 4 else kc - 4) * 128

        def tmm(out_ap, lhsT, rhs, bank, start, stop, **kw):
            if start:
                p.wait("tensor", bank_free[bank])
            if stop:
                tok = p.op("tensor", "matmul", out_ap, lhsT, rhs, start=start,
                           stop=stop, inc=[("T", 1)], **kw)
                bank_ready[bank] = tok
                return tok
            p.op("tensor", "matmul", out_ap, lhsT, rhs, start=start, stop=stop,
                 **kw)
            return None

        LNB_M, LNB_Q, LNB_R = 4, 5, 4

        def ln_pass(srcfn, dstfn, quarters, lg=None, pre=None, post=None):
            castbuf = ar.alloc("ln_cast", 2048, BF16)
            sqbuf = ar.alloc("ln_sq", 2048, BF16)
            tmpq = ar.alloc("ln_tmp", 2048, F32)
            lnbc = ar.alloc("ln_bc", 2048, F32)
            prev_ssq = None
            for qi in range(quarters):
                if pre is not None:
                    pre(qi)
                t_mean = t_ssq = None
                prev_guard = prev_ssq
                for bat in range(2):
                    cs = range(4 * bat, 4 * bat + 4)
                    p.wait("gpsimd", prev_guard)
                    p.wait("vector", prev_guard)
                    vc = None
                    for c in cs:
                        pieces = srcfn(c, qi)
                        cc = c - 4 * bat
                        if not any(off == 0 for (_, off, _) in pieces):
                            p.op("gpsimd", "memset",
                                 castbuf[:, cc * 512: cc * 512 + 1], 0.0)
                        for (a, off, n) in pieces:
                            eng_c = "gpsimd" if c >= 5 else "vector"
                            vc2 = p.op(eng_c, "tensor_copy",
                                       castbuf[:, cc * 512 + off:
                                               cc * 512 + off + n],
                                       a, inc=[(SEM_OF[eng_c], 1)])
                            if c % 2:
                                vc = vc2
                    p.wait("tensor", vc)
                    p.wait("tensor", p.tok("V"))
                    if bat == 0:
                        p.wait("tensor", bank_free[LNB_M])
                    pieces_all = [(c, a, off, n) for c in cs
                                  for (a, off, n) in srcfn(c, qi)]
                    for pi, (c, a, off, n) in enumerate(pieces_all):
                        cc = c - 4 * bat
                        p.op("tensor", "matmul", psf[LNB_M][0:1, off:off + n],
                             onesln_bf,
                             castbuf[:, cc * 512 + off: cc * 512 + off + n],
                             start=(bat == 0 and pi == 0),
                             stop=(bat == 1 and pi == len(pieces_all) - 1))
                    t_mean = tgroup_tok(LNB_M)
                    p.wait("scalar", prev_guard)
                    p.wait("scalar", vc)
                    p.wait("scalar", p.tok("V"))
                    sq_tok = p.opt("scalar", "activation", sqbuf, castbuf,
                                   AF.Square)
                    p.wait("tensor", sq_tok)
                    if bat == 0:
                        p.wait("tensor", bank_free[LNB_Q])
                    for pi, (c, a, off, n) in enumerate(pieces_all):
                        cc = c - 4 * bat
                        p.op("tensor", "matmul", psf[LNB_Q][0:1, off:off + n],
                             onesln_bf,
                             sqbuf[:, cc * 512 + off: cc * 512 + off + n],
                             start=(bat == 0 and pi == 0),
                             stop=(bat == 1 and pi == len(pieces_all) - 1))
                    t_ssq = tgroup_tok(LNB_Q)
                    prev_guard = t_ssq
                prev_ssq = t_ssq
                r_mean, r_rs, r_mrs = row(0), row(1), row(2)
                p.wait("vector", t_mean)
                p.op("vector", "tensor_copy", r_mean, psf[LNB_M][0:1, 0:512])
                v1 = p.opt("vector", "tensor_tensor", r_rs, r_mean, r_mean,
                           ALU.mult)
                bank_free[LNB_M] = v1
                p.wait("vector", t_ssq)
                v2 = p.opt("vector", "tensor_tensor", r_rs,
                           psf[LNB_Q][0:1, 0:512], r_rs, ALU.subtract)
                bank_free[LNB_Q] = v2
                p.wait("scalar", v2)
                p.op("scalar", "activation", r_rs, r_rs, AF.Ln, bias=eps_p0)
                s1 = p.opt("scalar", "activation", r_rs, r_rs, AF.Exp, scale=-0.5)
                p.wait("vector", s1)
                v3 = p.opt("vector", "scalar_tensor_tensor", r_mrs, r_mean,
                           -1.0, r_rs, ALU.mult, ALU.mult)
                p.wait("tensor", v3)
                p.wait("tensor", bank_free[LNB_R])
                tr = p.op("tensor", "matmul", psf[LNB_R][:, 0:512], ones_row,
                          r_rs, start=True, stop=True, inc=[("T", 1)])
                bank_ready[LNB_R] = tr
                p.wait("tensor", bank_free[LNB_Q])
                tm_ = p.op("tensor", "matmul", psf[LNB_Q][:, 0:512], ones_row,
                           r_mrs, start=True, stop=True, inc=[("T", 1)])
                bank_ready[LNB_Q] = tm_
                p.wait("vector", tr)
                p.wait("vector", tm_)
                rs_sb = lnbc[:, 0:512]
                mrs_sb = lnbc[:, 512:1024]
                p.op("vector", "tensor_copy", rs_sb, psf[LNB_R][:, 0:512])
                vev = p.opt("vector", "tensor_copy", mrs_sb,
                            psf[LNB_Q][:, 0:512])
                bank_free[LNB_R] = vev
                bank_free[LNB_Q] = vev
                p.wait("gpsimd", vev)
                for c in range(NCH):
                    pieces = srcfn(c, qi)
                    dst = dstfn(c, qi)
                    covered0 = any(off == 0 for (_, off, _) in pieces)
                    eng_c = "gpsimd" if c >= 5 else "vector"
                    sl_ = 0 if eng_c == "vector" else 1
                    tq2 = tmpq[:, 512 * sl_: 512 * sl_ + 512]
                    for (a, off, n) in pieces:
                        p.op(eng_c, "tensor_tensor", tq2[:, off:off + n],
                             a, rs_sb[:, off:off + n], ALU.mult)
                        p.op(eng_c, "tensor_add",
                             dst[:, off:off + n], tq2[:, off:off + n],
                             mrs_sb[:, off:off + n], inc=[(SEM_OF[eng_c], 1)])
                        if lg is not None:
                            p.op(eng_c, "tensor_scalar",
                                 dst[:, off:off + n], dst[:, off:off + n],
                                 lng_c(lg, c), lnb_c(lg, c),
                                 ALU.mult, ALU.add, inc=[(SEM_OF[eng_c], 1)])
                    if not covered0:
                        if lg is None:
                            p.op(eng_c, "tensor_copy", dst[:, 0:1],
                                 mrs_sb[:, 0:1], inc=[(SEM_OF[eng_c], 1)])
                        else:
                            p.op(eng_c, "tensor_scalar", dst[:, 0:1],
                                 mrs_sb[:, 0:1], lng_c(lg, c),
                                 lnb_c(lg, c), ALU.mult, ALU.add,
                                 inc=[(SEM_OF[eng_c], 1)])
                vlast = p.tok("G")
                if post is not None:
                    post(qi, vlast)
            ar.free("ln_cast", "ln_sq", "ln_tmp", "ln_bc")

        prefetched = {}

        def ffn_entry_ln():
            phase_barrier()
            ar.alloc("wqs", 12288, BF16)
            p.wait("sync", t_conv["wqkv0"])
            prefetched["wqs"] = p.op(
                "sync", "dma_start",
                out=ar.view("wqs").rearrange("p (c o) -> p c o", o=1536),
                in_=_chunked(dwb["wqkv0"]), inc=[("dW0", 16)])
            spill_toks = [None] * 4

            def pre(qi):
                q0 = qi * 512
                p.wait("scalar", p.tok("V"))
                p.wait("scalar", p.tok("dX"))
                p.wait("scalar", p.tok("dGX"))
                p.wait("scalar", p.tok("dS"))
                src3 = xy.rearrange("p (c t) -> p c t", t=T)[:, :, q0:q0 + 512]
                dst3 = dspill.rearrange("p (c t) -> p c t", t=T)[:, :, q0:q0 + 512]
                spill_toks[qi] = p.op("scalar", "dma_start", out=dst3, in_=src3,
                                      inc=[("dX", 16)])

            def src(c, qi):
                return [(xyc(c, qi * 512, 512), 0, 512)]

            def dst(c, qi):
                p.wait("gpsimd", spill_toks[qi])
                p.wait("vector", spill_toks[qi])
                return xyc(c, qi * 512, 512)

            ln_pass(src, dst, 4, pre=pre)

        def rope8(eng, u, out_bf, cs, ssn, ssp, scr):
            u4 = u.rearrange("p (h v d) -> p h v d", v=2, d=32)
            sa = scr[:, 0:512].rearrange("p (h d) -> p h d", d=64)
            sa4 = scr[:, 0:512].rearrange("p (h v d) -> p h v d", v=2, d=32)
            sb = scr[:, 512:1024].rearrange("p (h d) -> p h d", d=64)
            sb3 = scr[:, 512:1024].rearrange("p (h d) -> p h d", d=64)
            o3 = out_bf.rearrange("p (h d) -> p h d", d=64)
            cs2 = cs.unsqueeze(1).unsqueeze(1).broadcast_to([128, 8, 2, 32])
            ssnb = ssn.unsqueeze(1).broadcast_to([128, 8, 32])
            sspb = ssp.unsqueeze(1).broadcast_to([128, 8, 32])
            u3 = u.rearrange("p (h d) -> p h d", d=64)
            p.op(eng, "tensor_tensor", sa4, u4, cs2, ALU.mult)
            p.op(eng, "tensor_tensor", sb3[:, :, 0:32], u3[:, :, 32:64], ssnb,
                 ALU.mult)
            p.op(eng, "tensor_tensor", sb3[:, :, 32:64], u3[:, :, 0:32], sspb,
                 ALU.mult)
            return p.opt(eng, "tensor_tensor", o3, sa, sb, ALU.add)

        def ffn_layer(l, final_ffn=False, parts=("p1", "p1b", "p2"),
                      keep_bufs=False):
            if "p1" in parts:
                phase_barrier()
                qf = ar.alloc("qf", 8192, BF16)
                kf = ar.alloc("kf", 8192, BF16)
                vt = ar.alloc("vt", 8320, BF16)
                sqp = ar.alloc("sqp", 2048, F32)
                stat = ar.alloc("stat", 64, F32)
                ybs = ar.alloc("ybs", 8192, BF16)
                if "wqs" not in ar.regions:
                    ar.alloc("wqs", 12288, BF16)
                qtmp = ar.alloc("qtmp", 4096, F32)
                ropesc = ar.alloc("ropesc", 2048, F32)
                ropes2 = ar.alloc("ropes2", 2048, F32)
                qb16 = ar.alloc("qb16", 2048, BF16)
                qsb = ar.alloc("qsb", 256, F32)
                ybsv, wqsv, qfv, kfv = (ar.view(n) for n in
                                        ("ybs", "wqs", "qf", "kf"))
                vtv, qtmpv, ropev = ar.view("vt"), ar.view("qtmp"), ar.view("ropesc")
                rope2v = ar.view("ropes2")
                qbv, qsbv = ar.view("qb16"), ar.view("qsb")
                sqpv, statv = ar.view("sqp"), ar.view("stat")

                def vt_t(i):
                    return vtv[:, 520 * i: 520 * i + 520]

                v65 = vtv.rearrange("p (i h d) -> p (i h) d", h=8, d=65)
                p.op("gpsimd", "memset", v65[:, :, 64:65], 1.0)

                if l == 0 and "wqs" in prefetched:
                    t_wqs = prefetched.pop("wqs")
                else:
                    p.wait("sync", t_conv[f"wqkv{l}"])
                    t_wqs = p.op("sync", "dma_start",
                                 out=wqsv.rearrange("p (c o) -> p c o", o=1536),
                                 in_=_chunked(dwb[f"wqkv{l}"]),
                                 inc=[("dW0", 16)])


                p.wait("tensor", t_small)
                p.wait("tensor", bank_free[7])
                p.op("tensor", "matmul", psf[7][:, 0:64], ones_row,
                     qsrow[:, 64 * l:64 * l + 64], start=True, stop=True)
                tq = p.op("tensor", "matmul", psf[7][:, 64:128], ones_row,
                          ksrow[:, 64 * l:64 * l + 64], start=True, stop=True,
                          inc=[("T", 1)])
                bank_ready[7] = tq
                p.wait("vector", tq)
                vq = p.opt("vector", "tensor_copy", qsbv, psf[7][:, 0:128])
                bank_free[7] = vq

                def ybs_c(c, t0=0, n=1024):
                    return ybsv[:, c * 1024 + t0: c * 1024 + t0 + n]

                g_q = tq_ = tk_ = v_k = None
                pend_tr = None

                def emit_tr(i, sl, g_q2, v_k2):
                    nonlocal tq_, tk_
                    qbq = qbv[:, 1024 * sl: 1024 * sl + 512]
                    qbk = qbv[:, 1024 * sl + 512: 1024 * sl + 1024]
                    p.wait("tensor", g_q2)
                    p.wait("tensor", v_k2)
                    p.wait("tensor", bank_free[3])
                    for j in range(4):
                        p.op("tensor", "transpose",
                             psb[3][:, 128 * j:128 * j + 128],
                             qbq[:, 128 * j:128 * j + 128], idn)
                    tq_ = tgroup_tok(3)
                    p.wait("tensor", bank_free[6])
                    for j in range(4):
                        p.op("tensor", "transpose",
                             psb[6][:, 128 * j:128 * j + 128],
                             qbk[:, 128 * j:128 * j + 128], idn)
                    tk_ = tgroup_tok(6)
                    p.wait("vector", tq_)
                    v_e = p.opt("vector", "tensor_copy",
                                qfv.rearrange("p (c t) -> p c t", t=T)
                                [:, :, 128 * i:128 * i + 128],
                                psb[3][:, 0:512].rearrange(
                                    "p (j t) -> p j t", t=128))
                    bank_free[3] = v_e
                    p.wait("vector", tk_)
                    v_e2 = p.opt("vector", "tensor_copy",
                                 kfv.rearrange("p (c t) -> p c t", t=T)
                                 [:, :, 128 * i:128 * i + 128],
                                 psb[6][:, 0:512].rearrange(
                                     "p (j t) -> p j t", t=128))
                    bank_free[6] = v_e2
                for half in range(2):
                    h0 = half * 1024
                    p.wait("gpsimd", p.tok("T") if half else None)
                    p.wait("vector", p.tok("T") if half else None)
                    for c in range(4):
                        eng_c = "gpsimd" if c >= 5 else "vector"
                        p.op(eng_c, "tensor_copy", ybs_c(c), xyc(c, h0, 1024),
                             inc=[(SEM_OF[eng_c], 1)])
                    for c in range(4, 8):
                        eng_c = "gpsimd" if c >= 5 else "vector"
                        if half == 0:
                            p.op(eng_c, "tensor_copy", ybs_c(c, 1, 1023),
                                 xyc(c, 0, 1023), inc=[(SEM_OF[eng_c], 1)])
                            p.op("gpsimd", "memset", ybs_c(c, 0, 1), 0.0)
                        else:
                            p.op(eng_c, "tensor_copy", ybs_c(c, 0, 1024),
                                 xyc(c, h0 - 1, 1024), inc=[(SEM_OF[eng_c], 1)])
                    p.wait("tensor", p.tok("G"))
                    p.wait("tensor", p.tok("V"))
                    t_ybs = p.tok("G")
                    p.wait("tensor", t_wqs)
                    p.wait("tensor", t_tabs)
                    for it in range(8):
                        i = half * 8 + it
                        for og in range(3):
                            p.wait("tensor", bank_free[og])
                        for c in range(NCH):
                            lhs = ybs_c(c, it * 128, 128)
                            for og in range(3):
                                p.op("tensor", "matmul", psf[og], lhs,
                                     wqsv[:, c * 1536 + og * 512:
                                          c * 1536 + og * 512 + 512],
                                     start=(c == 0), stop=(c == NCH - 1))
                        t_qkv = p.inc_last("tensor", "T")
                        for og in range(3):
                            bank_ready[og] = t_qkv
                        p.wait("vector", t_qkv)
                        vt3 = vt_t(i).rearrange("p (h d) -> p h d", d=65)[:, :, 0:64]
                        ve_v = p.opt("vector", "tensor_copy", vt3,
                                     psf[2].rearrange("p (h d) -> p h d", d=64))
                        bank_free[2] = ve_v
                        p.wait("scalar", v_k)
                        p.wait("scalar", t_qkv)
                        s_q = p.opt("scalar", "activation", sqpv[:, 0:512],
                                    psf[0], AF.Square)
                        s_k = p.opt("scalar", "activation", sqpv[:, 512:1024],
                                    psf[1], AF.Square)
                        p.wait("vector", s_q)
                        p.op("vector", "tensor_reduce",
                             statv[:, 0:8].unsqueeze(2),
                             sqpv[:, 0:512].rearrange("p (h d) -> p h d", d=64),
                             AXX, ALU.add)
                        p.wait("vector", s_k)
                        p.op("vector", "tensor_reduce",
                             statv[:, 8:16].unsqueeze(2),
                             sqpv[:, 512:1024].rearrange("p (h d) -> p h d",
                                                         d=64), AXX, ALU.add)
                        v_n = p.opt("vector", "tensor_scalar", statv[:, 16:32],
                                    statv[:, 0:16], 1e-24, None, ALU.max)
                        p.wait("scalar", v_n)
                        p.op("scalar", "activation", statv[:, 16:32],
                             statv[:, 16:32], AF.Ln)
                        s_rs = p.opt("scalar", "activation", statv[:, 16:32],
                                     statv[:, 16:32], AF.Exp, scale=-0.5)
                        sl = i % 2
                        p.wait("vector", s_rs)
                        p.wait("vector", g_q)
                        uq = qtmpv[:, 1024 * sl: 1024 * sl + 512]
                        uk = qtmpv[:, 1024 * sl + 512: 1024 * sl + 1024]
                        for (psi, uu, q0_, st0) in ((0, uq, 0, 16),
                                                    (1, uk, 64, 24)):
                            p.op("vector", "tensor_tensor",
                                 uu.rearrange("p (h d) -> p h d", d=64),
                                 psf[psi].rearrange("p (h d) -> p h d", d=64),
                                 qsbv[:, q0_:q0_ + 64].unsqueeze(1)
                                 .broadcast_to([128, 8, 64]), ALU.mult)
                            v_u = p.opt("vector", "tensor_tensor",
                                        uu.rearrange("p (h d) -> p h d", d=64),
                                        uu.rearrange("p (h d) -> p h d", d=64),
                                        statv[:, st0:st0 + 8].unsqueeze(2)
                                        .broadcast_to([128, 8, 64]), ALU.mult)
                            bank_free[psi] = v_u
                        qbq = qbv[:, 1024 * sl: 1024 * sl + 512]
                        qbk = qbv[:, 1024 * sl + 512: 1024 * sl + 1024]
                        p.wait("gpsimd", p.tok("V"))
                        p.wait("gpsimd", tq_)
                        g_q = rope8("gpsimd", uq, qbq, tab(0, i),
                                    tab(1, i), tab(2, i), ropev)
                        p.wait("vector", tk_)
                        v_k = rope8("vector", uk, qbk, tab(3, i),
                                    tab(4, i), tab(5, i), rope2v)
                        if pend_tr is not None:
                            emit_tr(*pend_tr)
                        pend_tr = (i, sl, g_q, v_k)

                if pend_tr is not None:
                    emit_tr(*pend_tr)
                    pend_tr = None
                ar.free("ybs", "wqs", "qtmp", "ropesc", "ropes2", "qb16",
                        "qsb")

            if "p1b" in parts:
                phase_barrier()
                qfv, kfv, vtv = ar.view("qf"), ar.view("kf"), ar.view("vt")

                def vt_t(i):
                    return vtv[:, 520 * i: 520 * i + 520]

                of_ = ar.alloc("of", 8192, BF16)
                pb = ar.alloc("pb", 2048, BF16)
                pscr = ar.alloc("pscr", 512, BF16)
                wob = ar.alloc("wob", 4096, BF16)
                anrm = ar.alloc("anrm", 2048, F32)
                ofv, pbv, pscrv, wobv = (ar.view(n) for n in
                                         ("of", "pb", "pscr", "wob"))
                anv = ar.view("anrm")
                p.wait("sync", t_conv[f"wo{l}"])
                t_wo = p.op("sync", "dma_start",
                            out=wobv.rearrange("p (c o) -> p c o", o=1024),
                            in_=_chunked(dwb[f"wo{l}"]), inc=[("dW1", 16)])

                def headq(h, t0, n):
                    c, r = divmod(h, 2)
                    return qfv[64 * r:64 * r + 64, c * T + t0: c * T + t0 + n]

                def headk(h, t0, n):
                    c, r = divmod(h, 2)
                    return kfv[64 * r:64 * r + 64, c * T + t0: c * T + t0 + n]

                def heado(h, t0, n):
                    c, r = divmod(h, 2)
                    return ofv[64 * r:64 * r + 64, c * T + t0: c * T + t0 + n]

                prev_anrm = [None, None]

                def normalize_h(h, q0, ta):
                    hh = h % 2
                    abank = 4 + hh
                    r_rho = row(hh)
                    p.wait("scalar", ta)
                    p.op("scalar", "activation", r_rho,
                         psf[abank][64:65, 0:512], AF.Ln)
                    s_r = p.opt("scalar", "activation", r_rho, r_rho, AF.Exp,
                                scale=-1.0)
                    p.wait("vector", ta)
                    asl = anv[:, 512 * hh: 512 * hh + 512][0:64, :]
                    vr = p.opt("vector", "tensor_copy", asl,
                               psf[abank][0:64, 0:512])
                    bank_free[abank] = vr
                    bbank = 6
                    p.wait("tensor", s_r)
                    p.wait("tensor", bank_free[bbank])
                    tb = p.op("tensor", "matmul", psf[bbank][0:64, 0:512],
                              ones_row[:, 0:64], r_rho, start=True,
                              stop=True, inc=[("T", 1)])
                    bank_ready[bbank] = tb
                    p.wait("vector", tb)
                    vo = p.opt("vector", "tensor_mul", heado(h, q0, 512),
                               asl, psf[bbank][0:64, 0:512])
                    bank_free[bbank] = vo
                    prev_anrm[hh] = vo

                for w in range(4):
                    q0 = w * WIN

                    def emit_A(h, kc, span0, toff, ns, pchunk, scr, se, vm,
                               first_a, last, q0=q0):
                        hh = h % 2
                        abank = 4 + hh
                        kg = q0 - 512 + 128 * kc
                        va = vt_t(kg // 128).rearrange("p (g d) -> p g d",
                                                       d=65)[:, h, :]
                        p.wait("tensor", se)
                        if kc < 4:
                            clean0, cleann = 0, toff
                        else:
                            clean0, cleann = toff + 128, ns - toff - 128
                        if cleann > 0:
                            tmm(psf[abank][0:65, span0 + clean0:
                                           span0 + clean0 + cleann],
                                va, pchunk[:, clean0:clean0 + cleann],
                                abank, start=first_a, stop=False)
                            first_a = False
                        p.wait("tensor", vm)
                        tmm(psf[abank][0:65, tri_col(kc, q0):
                                       tri_col(kc, q0) + 128],
                            va, scr, abank, start=first_a, stop=last)
                        return False

                    pend = None
                    for h in range(8):
                        hh = h % 2
                        abank = 4 + hh
                        p.wait("tensor", bank_free[abank])
                        first_a = True
                        kcs = list(range(4, 8)) if w == 0 else list(range(8))
                        prev = None
                        for idx, kc in enumerate(kcs):
                            kg = q0 - 512 + 128 * kc
                            if kc < 4:
                                span0, span1, tri, mask = 0, 128 * (kc + 1), kc, mlo
                            else:
                                kr = kc - 4
                                span0, span1, tri, mask = 128 * kr, 512, kr, mup
                            ns = span1 - span0
                            sbank = 2 * hh + idx % 2
                            ts_ = tmm(psf[sbank][:, 0:ns], headk(h, kg, 128),
                                      headq(h, q0 + span0, ns), sbank,
                                      start=True, stop=True)
                            pchunk = pbv[:, 512 * (2 * hh + idx % 2):
                                         512 * (2 * hh + idx % 2) + ns]
                            p.wait("scalar", ts_)
                            se = p.opt("scalar", "activation", pchunk,
                                       psf[sbank][:, 0:ns], AF.Exp,
                                       bias=cloc_c(l), scale=8.0)
                            bank_free[sbank] = se
                            toff = tri * 128 - span0
                            scr = pscrv[:, 128 * (2 * hh + idx % 2):
                                        128 * (2 * hh + idx % 2) + 128]
                            eng_m = "gpsimd" if hh else "vector"
                            p.wait(eng_m, se)
                            vm = p.op(eng_m, "tensor_mul", scr,
                                      pchunk[:, toff:toff + 128], mask,
                                      inc=[(SEM_OF[eng_m], 1)])
                            if prev is not None:
                                first_a = emit_A(*prev, first_a, False)
                            prev = (h, kc, span0, toff, ns, pchunk, scr, se, vm)
                        first_a = emit_A(*prev, first_a, True)
                        ta = bank_ready[abank]
                        if pend is not None:
                            normalize_h(*pend)
                        pend = (h, q0, ta)
                    normalize_h(*pend)
                    p.wait("tensor", t_wo)
                    p.wait("tensor", p.tok("V"))
                    for ofc in range(8):
                        p.wait("tensor", bank_free[6])
                        for fc in range(4):
                            p.op("tensor", "matmul", psf[6],
                                 wobv[:, fc * 1024 + ofc * 128:
                                      fc * 1024 + ofc * 128 + 128],
                                 ofv[:, fc * T + q0: fc * T + q0 + 512],
                                 start=(fc == 0), stop=(fc == 3))
                        tw = tgroup_tok(6)
                        p.wait("vector", tw)
                        vv = p.opt("vector", "tensor_tensor", xyc(ofc, q0, 512),
                                   xyc(ofc, q0, 512), psf[6], ALU.add)
                        bank_free[6] = vv
                if not keep_bufs:
                    ar.free("of", "pb", "pscr", "wob", "anrm", "qf", "kf", "vt",
                            "sqp", "stat")

            if "p2" in parts:
                phase_barrier()
                w2b = ar.alloc("w2b", 22528, BF16)
                ffb = ar.alloc("ffb", 11264, BF16)
                lnh = ar.alloc("lnh", 8192, BF16)
                w1b = ar.alloc("w1b", 4096, BF16)
                ggs = ar.alloc("ggs", 2048, F32)
                t2s = ar.alloc("t2s", 2048, F32)
                lnhv, ffv, w1v, w2v = (ar.view(n) for n in
                                       ("lnh", "ffb", "w1b", "w2b"))
                ggv, t2v = ar.view("ggs"), ar.view("t2s")
                p.wait("scalar", t_conv[f"w2_{l}"])
                p.op("scalar", "dma_start",
                     out=w2v[:, 0:21 * 1024].rearrange("p (c o) -> p c o",
                                                       o=1024),
                     in_=_chunked(dwb[f"w2_{l}"][0:2688, :]),
                     inc=[("dW2", 16)])
                t_w2 = p.op("scalar", "dma_start",
                            out=w2v[0:42, 21 * 1024:22 * 1024],
                            in_=dwb[f"w2_{l}"][2688:2730, :],
                            inc=[("dW2", 16)])
                p.wait("sync", t_conv[f"w1_{l}"])
                w1_slot_free = [None, None]
                shsv = sm[:, 280:288]

                def emit_save_ln(qq, t_guard):
                    h0 = qq * 512
                    sv_w = shsv[:, (qq % 2) * 4: (qq % 2) * 4 + 4]
                    sv_r = shsv[:, ((qq + 1) % 2) * 4: ((qq + 1) % 2) * 4 + 4]
                    p.wait("gpsimd", p.tok("V"))
                    for c in range(4, 8):
                        t_sv = p.op("gpsimd", "tensor_copy",
                                    sv_w[:, c - 4:c - 3], xyc(c, h0 + 511, 1))

                    def src(c, qi, h0=h0, sv_r=sv_r):
                        q0 = h0
                        if c < 4:
                            return [(xyc(c, q0, 512), 0, 512)]
                        if q0 == 0:
                            return [(xyc(c, 0, 511), 1, 511)]
                        return [(sv_r[:, c - 4:c - 3], 0, 1),
                                (xyc(c, q0, 511), 1, 511)]

                    def dst(c, qi, qq=qq):
                        return lnhv[:, 4096 * (qq % 2) + c * 512:
                                    4096 * (qq % 2) + c * 512 + 512]

                    p.wait("vector", t_guard)
                    p.wait("gpsimd", t_guard)
                    p.wait("vector", p.tok("dGX"))
                    p.wait("gpsimd", p.tok("dGX"))
                    ln_pass(src, dst, 1, lg=l)
                    return t_sv, p.tok("V"), p.tok("G")

                _, lnV, lnG = emit_save_ln(0, p.tok("T"))
                t_p2_entry = p.tok("T")
                t_pairs_prev = None
                for qq in range(4):
                    h0 = qq * 512
                    lnslot = 4096 * (qq % 2)
                    t_sv = None
                    for pair in range(NHP):
                        pslot = pair % 2
                        wblk = w1v[:, pslot * 2048: pslot * 2048 + 2048]
                        wblk3 = wblk.rearrange("p (c o) -> p c o", o=256)
                        pa0 = 128 * pair
                        pg0 = HID + 128 * pair
                        na = min(128, HID - pa0)
                        dsem = "dW0" if pslot == 0 else "dW1"
                        p.wait("sync", w1_slot_free[pslot])
                        p.op("sync", "dma_start", out=wblk3[:, :, 0:na],
                             in_=_chunked(dwb[f"w1_{l}"])[:, :, pa0:pa0 + na],
                             inc=[(dsem, 16)])
                        t_wp = p.op("sync", "dma_start",
                                    out=wblk3[:, :, 128:128 + na],
                                    in_=_chunked(dwb[f"w1_{l}"])
                                    [:, :, pg0:pg0 + na], inc=[(dsem, 16)])
                        if pair == 0:
                            p.wait("tensor", lnV)
                            p.wait("tensor", lnG)
                        if pair == 4 and qq < 3:
                            t_sv, lnV, lnG = emit_save_ln(
                                qq + 1,
                                t_pairs_prev if qq >= 1 else t_p2_entry)
                        p.wait("tensor", t_wp)
                        abk, gbk = pair % 2, 2 + pair % 2
                        p.wait("tensor", bank_free[abk])
                        p.wait("tensor", bank_free[gbk])
                        for c in range(NCH):
                            rhs = lnhv[:, lnslot + c * 512:
                                       lnslot + c * 512 + 512]
                            p.op("tensor", "matmul", psf[abk][0:na, 0:512],
                                 wblk3[:, c, 0:na], rhs, start=(c == 0),
                                 stop=(c == NCH - 1))
                            p.op("tensor", "matmul", psf[gbk][0:na, 0:512],
                                 wblk3[:, c, 128:128 + na], rhs, start=(c == 0),
                                 stop=(c == NCH - 1))
                        tg = p.inc_last("tensor", "T")
                        bank_ready[abk] = tg
                        bank_ready[gbk] = tg
                        w1_slot_free[pslot] = tg
                        p.wait("scalar", tg)
                        gsl = ggv[:, 512 * (pair % 2): 512 * (pair % 2) + 512]
                        sg = p.opt("scalar", "activation", gsl[0:na, :],
                                   psf[gbk][0:na, 0:512], AF.Gelu,
                                   bias=b1g_c(l, pair)[0:na, :])
                        bank_free[gbk] = sg
                        p.wait("vector", sg)
                        vf = p.opt("vector", "scalar_tensor_tensor",
                                   ffv[0:na, pair * 512: pair * 512 + 512],
                                   psf[abk][0:na, 0:512],
                                   b1a_c(l, pair)[0:na, :],
                                   gsl[0:na, :], ALU.add, ALU.mult)
                        bank_free[abk] = vf
                        if pair == NHP - 1:
                            t_pairs_prev = tg
                    t_ff = p.tok("V")
                    p.wait("tensor", t_ff)
                    p.wait("tensor", t_w2)
                    p.wait("vector", t_sv)
                    for ofc in range(NCH):
                        bank = 6 + ofc % 2
                        p.wait("tensor", bank_free[bank])
                        for pair in range(NHP):
                            npr = min(128, HID - 128 * pair)
                            p.op("tensor", "matmul", psf[bank][:, 0:512],
                                 w2v[0:npr, pair * 1024 + ofc * 128:
                                     pair * 1024 + ofc * 128 + 128],
                                 ffv[0:npr, pair * 512: pair * 512 + 512],
                                 start=(pair == 0), stop=(pair == NHP - 1))
                        tw2 = tgroup_tok(bank)
                        p.wait("vector", tw2)
                        yq = xyc(ofc, h0, 512)
                        tsl = t2v[:, 512 * (ofc % 2): 512 * (ofc % 2) + 512]
                        if not final_ffn:
                            p.op("vector", "tensor_scalar", tsl, psf[bank],
                                 b2_c(l, ofc), None, ALU.add)
                            vy = p.opt("vector", "tensor_tensor", yq, yq, tsl,
                                       ALU.add)
                        else:
                            p.op("vector", "tensor_scalar", tsl, psf[bank],
                                 b2_c(l, ofc), 0.5, ALU.add, ALU.mult)
                            vy = p.opt("vector", "scalar_tensor_tensor", yq,
                                       yq, 0.5, tsl, ALU.mult, ALU.add)
                            xr = dspill[:, ofc * T + h0: ofc * T + h0 + 512]
                            p.wait("gpsimd", vy)
                            p.op("gpsimd", "dma_start", out=yq, in_=xr,
                                 accum_op=ALU.add, inc=[("dGX", 16)])
                        bank_free[bank] = vy
                if not keep_bufs:
                    ar.free("lnh", "ffb", "w1b", "w2b", "ggs", "t2s")

        def attn_phase(keep_bufs=False):
            phase_barrier()
            p.wait("vector", p.tok("dX"))
            p.wait("vector", p.tok("dGX"))
            lnx = ar.alloc("lnx", 16384, BF16)
            lnxv = ar.view("lnx")

            def emit_slab(g, watv):
                wa3 = watv.rearrange("p (c o) -> p c o", o=768)
                dwa = _chunked(dwb["W_attn"])
                for r in range(2):
                    p.op("sync", "dma_start",
                         out=wa3[:, :, 128 * r:128 * r + 128],
                         in_=dwa[:, :, 512 * r + 128 * g:
                                 512 * r + 128 * g + 128],
                         inc=[("dW2", 16)])
                    p.op("sync", "dma_start",
                         out=wa3[:, :, 256 + 128 * r:256 + 128 * r + 128],
                         in_=dwa[:, :, 1024 + 512 * r + 128 * g:
                                 1024 + 512 * r + 128 * g + 128],
                         inc=[("dW2", 16)])
                return p.op("sync", "dma_start", out=wa3[:, :, 512:768],
                            in_=dwa[:, :, 2048 + 256 * g: 2048 + 256 * g + 256],
                            inc=[("dW2", 16)])

            wpb = ar.alloc("wpb", 8192, BF16)
            wpv = ar.view("wpb")
            p.wait("sync", t_conv["W_proj"])
            t_wpj = p.op("sync", "dma_start",
                         out=wpv.rearrange("p (c o) -> p c o", o=1024),
                         in_=_chunked(dwb["W_proj"]), inc=[("dW2", 16)])
            ar.alloc("wat0", 6144, BF16)
            p.wait("sync", t_conv["W_attn"])
            pf_wa0 = emit_slab(0, ar.view("wat0"))

            def srcA(c, qi):
                return [(xyc(c, qi * 512, 512), 0, 512)]

            def dstA(c, qi):
                return lnxv[:, c * T + qi * 512: c * T + qi * 512 + 512]

            ln_pass(srcA, dstA, 4)
            p.wait("tensor", p.tok("G"))
            p.wait("tensor", p.tok("V"))
            t_lnx = p.tok("G")
            phase_barrier()

            tmax = ar.alloc("tmax", 64, F32)
            tmaxv = ar.view("tmax")[0:16, 0:16]
            statv = ar.view("statq16")
            cst = ar.view("cstats")
            cbg = ar.view("cbig")

            for g in range(4):
                gqk = ar.alloc("gqk", 8192, BF16)
                gvt = ar.alloc("gvt", 4160, BF16)
                wat_nm = "wat0" if g == 0 else "wat"
                if g > 0:
                    ar.alloc("wat", 6144, BF16)
                qkst = ar.alloc("qkst", 1024, BF16)
                sqb = ar.alloc("sqb", 2048, F32)
                gqkv, gvtv, watv = (ar.view("gqk"), ar.view("gvt"),
                                    ar.view(wat_nm))
                qkstv, sqbv = ar.view("qkst"), ar.view("sqb")
                gv65 = gvtv.rearrange("p (i h d) -> p (i h) d", h=4, d=65)
                p.op("gpsimd", "memset", gv65[:, :, 64:65], 1.0)
                t_gv = p.op("gpsimd", "engine_nop", inc=[("G", 1)])
                wa3 = watv.rearrange("p (c o) -> p c o", o=768)
                if g == 0:
                    t_wa = pf_wa0
                else:
                    p.wait("sync", t_conv["W_attn"])
                    t_wa = emit_slab(g, watv)
                p.wait("tensor", t_wa)
                p.wait("tensor", t_lnx)
                p.wait("tensor", t_btab)
                p.wait("tensor", t_gv)
                v_r = tq_ = None
                for i in range(NTILE):
                    p.wait("tensor", bank_free[0])
                    p.wait("tensor", bank_free[1])
                    for c in range(NCH):
                        lhs = lnxv[:, c * T + 128 * i: c * T + 128 * i + 128]
                        p.op("tensor", "matmul", psf[0], lhs, wa3[:, c, 0:512],
                             start=(c == 0), stop=(c == NCH - 1))
                        p.op("tensor", "matmul", psf[1][:, 0:256], lhs,
                             wa3[:, c, 512:768], start=(c == 0),
                             stop=(c == NCH - 1))
                    t_qkv = p.inc_last("tensor", "T")
                    bank_ready[0] = t_qkv
                    bank_ready[1] = t_qkv
                    p.wait("vector", t_qkv)
                    vt3 = gvtv[:, 260 * i:260 * i + 260].rearrange(
                        "p (h d) -> p h d", d=65)[:, :, 0:64]
                    ve_v = p.opt("vector", "tensor_copy", vt3,
                                 psf[1][:, 0:256].rearrange("p (h d) -> p h d",
                                                            d=64))
                    bank_free[1] = ve_v
                    p.wait("scalar", v_r)
                    p.wait("scalar", t_qkv)
                    s_q = p.opt("scalar", "activation", sqbv[:, 0:512], psf[0],
                                AF.Square)
                    p.wait("vector", s_q)
                    v_n = p.opt("vector", "tensor_reduce",
                                statv[:, 0:16].unsqueeze(2),
                                sqbv[:, 0:512].rearrange("p (h d) -> p h d",
                                                         d=32), AXX, ALU.add)
                    p.wait("tensor", v_n)
                    p.wait("tensor", bank_free[6])
                    tt_ = p.op("tensor", "transpose", psf[6][0:16, 0:128],
                               statv[:, 0:16], idnf, inc=[("T", 1)])
                    bank_ready[6] = tt_
                    p.wait("vector", tt_)
                    v_tm = p.opt("vector", "tensor_reduce",
                                 tmaxv[:, i:i + 1].unsqueeze(2),
                                 psf[6][0:16, 0:128].rearrange(
                                     "p (a b) -> p a b", a=1), AXX, ALU.max)
                    bank_free[6] = v_tm
                    qe = psf[0].rearrange("p (h j v) -> p h j v", j=16, v=2)
                    sa4 = sqbv[:, 0:512].rearrange("p (h j v) -> p h j v",
                                                   j=16, v=2)
                    sb4 = sqbv[:, 512:1024].rearrange("p (h j v) -> p h j v",
                                                      j=16, v=2)
                    ob = qkstv[:, 0:512].rearrange("p (h j v) -> p h j v",
                                                   j=16, v=2)
                    csb = btab_t(0, i).rearrange("p (j v) -> p j v", v=2) \
                        .unsqueeze(1).broadcast_to([128, 16, 16, 2])
                    ssb = btab_t(1, i).rearrange("p (j v) -> p j v", v=2) \
                        .unsqueeze(1).broadcast_to([128, 16, 16, 2])
                    p.wait("vector", tq_)
                    p.op("vector", "tensor_tensor", sa4, qe, csb, ALU.mult)
                    p.op("vector", "tensor_tensor", sb4[:, :, :, 0:1],
                         qe[:, :, :, 1:2], ssb[:, :, :, 0:1], ALU.mult)
                    p.op("vector", "tensor_tensor", sb4[:, :, :, 1:2],
                         qe[:, :, :, 0:1], ssb[:, :, :, 1:2], ALU.mult)
                    v_r = p.opt("vector", "tensor_add", ob, sa4, sb4)
                    bank_free[0] = v_r
                    p.wait("tensor", v_r)
                    p.wait("tensor", bank_free[3])
                    for j in range(4):
                        p.op("tensor", "transpose",
                             psb[3][:, 128 * j:128 * j + 128],
                             qkstv[:, 128 * j:128 * j + 128], idn)
                    tq_ = tgroup_tok(3)
                    p.wait("vector", tq_)
                    v_e = p.opt("vector", "tensor_copy",
                                gqkv.rearrange("p (c t) -> p c t", t=T)
                                [:, :, 128 * i:128 * i + 128],
                                psb[3][:, 0:512].rearrange(
                                    "p (j t) -> p j t", t=128))
                    bank_free[3] = v_e
                t_all = p.tok("V")
                p.wait("vector", t_all)
                p.op("vector", "tensor_reduce", cst[0:16, 0:4].unsqueeze(2),
                     tmaxv.rearrange("p (a b) -> p a b", b=4), AXX, ALU.max)
                v_km = p.opt("vector", "tensor_reduce",
                             cst[0:16, 4:5].unsqueeze(2),
                             tmaxv.rearrange("p (a b) -> p a b", a=1), AXX,
                             ALU.max)
                p.wait("scalar", v_km)
                b_h = p.op("scalar", "dma_start", out=drow[0:1, 0:8],
                           in_=cst[8:16, 4:5], inc=[("dX", 16)])
                p.wait("gpsimd", b_h)
                t_kb = p.op("gpsimd", "dma_start", out=cst[0:8, 5:6],
                            in_=drow[0:1, 0:8], inc=[("dGX", 16)])
                p.wait("vector", t_kb)
                p.op("vector", "tensor_scalar", cst[0:8, 0:4], cst[0:8, 0:4],
                     cst[0:8, 5:6], None, ALU.mult)
                v_c = p.opt("vector", "tensor_scalar", cst[0:8, 0:4],
                            cst[0:8, 0:4], 1e-30, None, ALU.max)
                p.wait("scalar", v_c)
                p.op("scalar", "activation", cst[0:8, 0:4], cst[0:8, 0:4],
                     AF.Ln)
                s_c = p.opt("scalar", "activation", cst[0:8, 0:4],
                            cst[0:8, 0:4], AF.Exp, scale=0.5)
                p.wait("vector", s_c)
                v_c2 = p.opt("vector", "tensor_scalar", cst[0:8, 0:4],
                             cst[0:8, 0:4], -1.0, None, ALU.mult)
                p.wait("scalar", v_c2)
                b1_ = p.op("scalar", "dma_start", out=drow[0:1, 0:32],
                           in_=cst[0:8, 0:4], inc=[("dX", 16)])
                p.wait("gpsimd", b1_)
                b2_ = p.op("gpsimd", "dma_start", out=row(2, 32),
                           in_=drow[0:1, 0:32], inc=[("dGX", 16)])
                p.wait("tensor", b2_)
                p.wait("tensor", bank_free[6])
                t_cb = p.op("tensor", "matmul", psf[6][:, 0:32], ones_row,
                            row(2, 32), start=True, stop=True, inc=[("T", 1)])
                bank_ready[6] = t_cb
                p.wait("vector", t_cb)
                v_cb = p.opt("vector", "tensor_copy", cbg[:, 0:32],
                             psf[6][:, 0:32])
                bank_free[6] = v_cb

                def ghead(is_k, hh, t0, n):
                    c = (2 if is_k else 0) + hh // 4
                    r = hh % 4
                    return gqkv[32 * r:32 * r + 32, c * T + t0: c * T + t0 + n]

                ar.free("qkst", "sqb")
                dfm = ar.alloc("dfm", 4096, BF16)
                pbuf2 = ar.alloc("pbuf2", 2048, BF16)
                pscr2 = ar.alloc("pscr2", 512, BF16)
                ast = ar.alloc("ast", 4096, F32)
                dstg = ar.alloc("dstg", 1088, F32)
                dnb = ar.alloc("dnb", 512, BF16)
                dfmv = ar.view("dfm")
                pb2v, ps2v = ar.view("pbuf2"), ar.view("pscr2")
                astv, dstgv, dnbv = ar.view("ast"), ar.view("dstg"), ar.view("dnb")
                def pair_tail(pr, qt, q0, slot, ve0, ve1):
                    p.wait("tensor", ve0)
                    p.wait("tensor", ve1)
                    for side in range(2):
                        bank = 4 + side
                        p.wait("tensor", bank_free[bank])
                        for j in range(4):
                            p.op("tensor", "transpose",
                                 psf[bank][:, 128 * j:128 * j + 65],
                                 astv[0:65, 1024 * slot + 512 * side + 128 * j:
                                      1024 * slot + 512 * side + 128 * j + 128],
                                 idnf[0:65, 0:65])
                        tgroup_tok(bank)
                    t_t2 = bank_ready[5]
                    p.wait("vector", bank_ready[4])
                    p.wait("vector", t_t2)
                    a1 = psf[4].rearrange("p (j u) -> p j u", u=128)
                    a2 = psf[5].rearrange("p (j u) -> p j u", u=128)
                    p.op("vector", "reciprocal",
                         dstgv[:, 512:516].unsqueeze(2), a1[:, :, 64:65])
                    p.op("vector", "reciprocal",
                         dstgv[:, 516:520].unsqueeze(2), a2[:, :, 64:65])
                    p.op("vector", "tensor_tensor",
                         dstgv[:, 0:256].rearrange("p (j d) -> p j d", d=64),
                         a1[:, :, 0:64],
                         dstgv[:, 512:516].unsqueeze(2)
                         .broadcast_to([128, 4, 64]), ALU.mult)
                    v_n2 = p.opt("vector", "tensor_tensor",
                                 dstgv[:, 256:512].rearrange(
                                     "p (j d) -> p j d", d=64),
                                 a2[:, :, 0:64],
                                 dstgv[:, 516:520].unsqueeze(2)
                                 .broadcast_to([128, 4, 64]), ALU.mult)
                    bank_free[4] = v_n2
                    bank_free[5] = v_n2
                    v_d = p.opt("vector", "scalar_tensor_tensor",
                                dstgv[:, 256:512], dstgv[:, 256:512],
                                lamcol, dstgv[:, 0:256], ALU.mult,
                                ALU.subtract)
                    p.wait("scalar", v_d)
                    s_sq = p.opt("scalar", "activation", dstgv[:, 0:256],
                                 dstgv[:, 256:512], AF.Square)
                    p.wait("vector", s_sq)
                    v_ss = p.opt("vector", "tensor_reduce",
                                 dstgv[:, 512:516].unsqueeze(2),
                                 dstgv[:, 0:256].rearrange(
                                     "p (j d) -> p j d", d=64), AXX, ALU.add)
                    p.wait("scalar", v_ss)
                    p.op("scalar", "activation", dstgv[:, 512:516],
                         dstgv[:, 512:516], AF.Ln, bias=eps_col,
                         scale=1.0 / 64.0)
                    s_rm = p.opt("scalar", "activation", dstgv[:, 512:516],
                                 dstgv[:, 512:516], AF.Exp, scale=-0.5)
                    p.wait("vector", s_rm)
                    p.op("vector", "tensor_scalar", dstgv[:, 512:516],
                         dstgv[:, 512:516], -(1.0 - LAMBDA_INIT), None,
                         ALU.mult)
                    v_dn = p.opt("vector", "tensor_tensor",
                                 dnbv.rearrange("p (j d) -> p j d", d=64)
                                 [:, :, :] if False else
                                 dnbv[:, 0:256].rearrange(
                                     "p (j d) -> p j d", d=64),
                                 dstgv[:, 256:512].rearrange(
                                     "p (j d) -> p j d", d=64),
                                 dstgv[:, 512:516].unsqueeze(2)
                                 .broadcast_to([128, 4, 64]), ALU.mult)
                    p.wait("tensor", v_dn)
                    p.wait("tensor", bank_free[6])
                    for j in range(4):
                        p.op("tensor", "transpose",
                             psb[6][0:64, 128 * j:128 * j + 128],
                             dnbv[:, 64 * j:64 * j + 64], idn)
                    t_tb = tgroup_tok(6)
                    p.wait("vector", t_tb)
                    cc, rr = divmod(pr, 2)
                    v_df = p.opt("vector", "tensor_copy",
                                 dfmv[64 * rr:64 * rr + 64,
                                      cc * T + q0: cc * T + q0 + 512]
                                 .rearrange("p (j t) -> p j t", t=128),
                                 psb[6][:, 0:512].rearrange(
                                     "p (j t) -> p j t", t=128)[0:64, :, :])
                    bank_free[6] = v_df

                p.wait("vector", v_cb)
                p.wait("scalar", v_cb)
                tails = []
                ve_pair = [None, None]
                for qt in range(4):
                    q0 = qt * 512
                    for pr in range(4):
                        for side in range(2):
                            hh = pr + 4 * side
                            abank = 2 + side
                            p.wait("tensor", bank_free[abank])
                            first_a = True
                            nkc = 4 * qt + 4
                            prev = None

                            def emit_bA(kc, pchunk, scr, se, vm, first_a, last,
                                        abank=abank, q0=q0, qt=qt, pr=pr):
                                diag = kc - 4 * qt
                                span0 = 128 * diag if diag >= 0 else 0
                                ns = 512 - span0
                                va = gvtv[:, 260 * kc + 65 * pr:
                                          260 * kc + 65 * pr + 65]
                                p.wait("tensor", se)
                                if diag < 0:
                                    tmm(psf[abank][0:65, 0:512], va, pchunk,
                                        abank, start=first_a, stop=last)
                                    return False
                                if ns > 128:
                                    tmm(psf[abank][0:65, span0 + 128:512],
                                        va, pchunk[:, 128:ns], abank,
                                        start=first_a, stop=False)
                                    first_a = False
                                p.wait("tensor", vm)
                                tmm(psf[abank][0:65, span0:span0 + 128],
                                    va, scr, abank, start=first_a, stop=last)
                                return False

                            for kc in range(nkc):
                                kg = 128 * kc
                                diag = kc - 4 * qt
                                span0 = 128 * diag if diag >= 0 else 0
                                ns = 512 - span0
                                sbank = kc % 2
                                ts_ = tmm(psf[sbank][:, 0:ns],
                                          ghead(True, hh, kg, 128),
                                          ghead(False, hh, q0 + span0, ns),
                                          sbank, start=True, stop=True,
                                          tile_position=(32 * (hh % 4), 0))
                                pchunk = pb2v[:, 512 * (kc % 2):
                                              512 * (kc % 2) + ns]
                                p.wait("scalar", ts_)
                                se = p.opt("scalar", "activation", pchunk,
                                           psf[sbank][:, 0:ns], AF.Exp,
                                           bias=cbg[:, 4 * hh + qt:
                                                    4 * hh + qt + 1],
                                           scale=1.0 / math.sqrt(32.0))
                                bank_free[sbank] = se
                                vm = None
                                scr = None
                                if diag >= 0:
                                    scr = ps2v[:, 128 * (kc % 2):
                                               128 * (kc % 2) + 128]
                                    eng_m = "gpsimd" if side else "vector"
                                    p.wait(eng_m, se)
                                    vm = p.op(eng_m, "tensor_mul", scr,
                                              pchunk[:, 0:128], mup,
                                              inc=[(SEM_OF[eng_m], 1)])
                                if prev is not None:
                                    first_a = emit_bA(*prev, first_a, False)
                                prev = (kc, pchunk, scr, se, vm)
                            first_a = emit_bA(*prev, first_a, True)
                            ta = bank_ready[abank]
                            p.wait("vector", ta)
                            ve = p.opt("vector", "tensor_copy",
                                       astv[0:65, 1024 * (pr % 2) + 512 * side:
                                            1024 * (pr % 2) + 512 * side + 512],
                                       psf[abank][0:65, 0:512])
                            bank_free[abank] = ve
                            ve_pair[side] = ve
                        tails.append((pr, qt, q0, pr % 2,
                                      ve_pair[0], ve_pair[1]))
                        if len(tails) > 1:
                            pair_tail(*tails.pop(0))
                for t_ in tails:
                    pair_tail(*t_)
                tails = []
                t_dfm = p.tok("V")
                p.wait("tensor", t_dfm)
                p.wait("tensor", t_wpj)
                for ofc in range(8):
                    for tq in range(4):
                        bank = tq % 2
                        p.wait("tensor", bank_free[bank])
                        for cc in range(2):
                            p.op("tensor", "matmul", psf[bank],
                                 wpv[:, (2 * g + cc) * 1024 + 128 * ofc:
                                     (2 * g + cc) * 1024 + 128 * ofc + 128],
                                 dfmv[:, cc * T + 512 * tq:
                                      cc * T + 512 * tq + 512],
                                 start=(cc == 0), stop=(cc == 1))
                        twp = tgroup_tok(bank)
                        p.wait("vector", twp)
                        v_x = p.opt("vector", "tensor_tensor",
                                    xyc(ofc, 512 * tq, 512),
                                    xyc(ofc, 512 * tq, 512), psf[bank], ALU.add)
                        bank_free[bank] = v_x
                ar.free("gqk", "gvt", wat_nm, "dfm", "pbuf2",
                        "pscr2", "ast", "dstg", "dnb")
                phase_barrier()
            if not keep_bufs:
                ar.free("lnx", "wpb", "tmax")

        def final_store():
            phase_barrier()
            p.wait("tensor", p.tok("dX"))
            p.wait("tensor", p.tok("dGX"))
            stg = ar.alloc("outst", 2048, F32)
            stv = ar.view("outst")
            t_store = None
            for i in range(NTILE):
                for half in range(2):
                    b = half
                    p.wait("tensor", bank_free[b])
                    for j in range(4):
                        c = half * 4 + j
                        p.op("tensor", "transpose",
                             psf[b][:, 128 * j:128 * j + 128],
                             xyc(c, 128 * i, 128), idnf)
                    tf = tgroup_tok(b)
                    p.wait("vector", tf)
                    p.wait("vector", t_store)
                    vv = p.opt("vector", "tensor_copy",
                               stv[:, half * 512: half * 512 + 512], psf[b])
                    bank_free[b] = vv
                p.wait("sync", p.tok("V"))
                t_store = p.op("sync", "dma_start",
                               out=dout[128 * i:128 * i + 128, :], in_=stv,
                               inc=[("dO", 16)])
            p.wait("sync", t_store)

        p.wait("vector", t_xin)
        if upto == "tiny":
            pass
        elif upto == "ln0":
            ffn_entry_ln()
        elif upto == "p1":
            ffn_entry_ln()
            ffn_layer(0, parts=("p1",), keep_bufs=True)
        elif upto == "p1b":
            ffn_entry_ln()
            ffn_layer(0, parts=("p1", "p1b"), keep_bufs=True)
        elif upto == "l0":
            ffn_entry_ln()
            ffn_layer(0)
        elif upto == "ffn1":
            ffn_entry_ln()
            ffn_layer(0)
            ffn_layer(1, final_ffn=True)
        elif upto == "attn":
            ffn_entry_ln()
            ffn_layer(0)
            ffn_layer(1, final_ffn=True)
            attn_phase()
        else:
            ffn_entry_ln()
            ffn_layer(0)
            ffn_layer(1, final_ffn=True)
            attn_phase()
            ffn_entry_ln()
            ffn_layer(0)
            ffn_layer(1, final_ffn=True)
        final_store()

        if dumps_req:
            p.barrier_all()
            p.wait("sync", p.tok("dX"))
            p.wait("sync", p.tok("dGX"))
            tok = None
            for nm in dumps_req:
                if nm == "XY":
                    ap_, shp, dt = xy[:, :], [128, NCH * T], F32
                elif nm == "SM":
                    ap_, shp, dt = sm[:, :], [128, 1800], F32
                else:
                    ap_ = ar.view(nm)
                    dt = ar.regions[nm][2]
                    shp = list(ap_.shape)
                dd = nc.dram_tensor("d_" + nm, shp, dt, kind="ExternalOutput").ap()
                tok = p.op("sync", "dma_start", out=dd, in_=ap_, inc=[("dO", 16)])
            p.wait("sync", tok)

        block = es.enter_context(nc.Block())
        p.emit(block)
    return nc


def _prep_weights(inputs):
    f = np.float32
    wq = np.asarray(inputs["L0_ffn_wqkv"], f)
    wo = np.asarray(inputs["L0_ffn_wo"], f)
    w1 = np.asarray(inputs["L0_ffn_w1"], f)
    w2 = np.asarray(inputs["L0_ffn_w2"], f)
    lam = np.zeros((1, 130), f)
    lam[0, 0:32] = np.asarray(inputs["L0_attn_lambda_q1"], f)
    lam[0, 32:64] = np.asarray(inputs["L0_attn_lambda_k1"], f)
    lam[0, 64:96] = np.asarray(inputs["L0_attn_lambda_q2"], f)
    lam[0, 96:128] = np.asarray(inputs["L0_attn_lambda_k2"], f)
    qs = np.zeros((1, 128), f)
    qs[0, 0:64] = np.asarray(inputs["L0_ffn_q_scale"], f)[0]
    qs[0, 64:128] = np.asarray(inputs["L0_ffn_q_scale"], f)[1]
    ks = np.zeros((1, 128), f)
    ks[0, 0:64] = np.asarray(inputs["L0_ffn_k_scale"], f)[0]
    ks[0, 64:128] = np.asarray(inputs["L0_ffn_k_scale"], f)[1]

    def fmcols(vec, n):
        return np.asarray(vec, f).reshape(n, 128).T.copy()

    lng = np.concatenate([fmcols(inputs["L0_ffn_ln_g"][l], 8) for l in range(2)],
                         axis=1)
    lnb = np.concatenate([fmcols(inputs["L0_ffn_ln_b"][l], 8) for l in range(2)],
                         axis=1)
    b2 = np.concatenate([fmcols(inputs["L0_ffn_b2"][l], 8) for l in range(2)],
                        axis=1)
    b1p = np.zeros((128, 88), f)
    for l in range(2):
        b1l = np.asarray(inputs["L0_ffn_b1"][l], f)
        va = np.zeros(22 * 128, f)
        va[:HID] = b1l[:HID]
        vg = np.zeros(22 * 128, f)
        vg[:HID] = b1l[HID:]
        b1p[:, 44 * l: 44 * l + 22] = va.reshape(22, 128).T
        b1p[:, 44 * l + 22: 44 * l + 44] = vg.reshape(22, 128).T
    return {
        "W_attn": np.ascontiguousarray(inputs["L0_attn_W_attn"], f),
        "W_proj": np.ascontiguousarray(inputs["L0_attn_W_proj"], f),
        "lam": lam, "qs": qs, "ks": ks,
        "wqkv0": np.ascontiguousarray(wq[0]), "wqkv1": np.ascontiguousarray(wq[1]),
        "wo0": np.ascontiguousarray(wo[0]), "wo1": np.ascontiguousarray(wo[1]),
        "w1_0": np.ascontiguousarray(w1[0]), "w1_1": np.ascontiguousarray(w1[1]),
        "w2_0": np.ascontiguousarray(w2[0]), "w2_1": np.ascontiguousarray(w2[1]),
        "lng": lng, "lnb": lnb, "b1": b1p, "b2": b2,
    }


def _prep_x(x_b):
    return np.ascontiguousarray(np.asarray(x_b, np.float32).T.reshape(NCH, 128, T))


def kernel(**inputs):
    x = np.asarray(inputs["L0_attn_x"], np.float32)
    w = _prep_weights(inputs)
    nc = build_program("all")
    in_maps = []
    for b in range(8):
        m = dict(w)
        m["x"] = _prep_x(x[b])
        in_maps.append(m)
    res = run_bass_kernel_spmd(nc, in_maps, core_ids=list(range(8)))
    out = np.stack([res.results[b]["out"] for b in range(8)], axis=0)
    return out.astype(np.float32)
